# version 04
# baseline (speedup 1.0000x reference)
import contextlib
import os
import numpy as np
import ml_dtypes
import concourse.bass as bass
import concourse.mybir as mybir
from concourse.bass_utils import run_bass_kernel_spmd

F32 = mybir.dt.float32
BF16 = mybir.dt.bfloat16
I32 = mybir.dt.int32
AF = mybir.ActivationFunctionType
ALU = mybir.AluOpType
AX = mybir.AxisListType
bf16 = ml_dtypes.bfloat16

B, L, E = 4, 8192, 1024
NLOC = 4096
H, DH = 16, 64
F = 4608
NE, S = 8, 512
HID = 4096
NQ, DB = 8, 128
NT = NLOC // 128
NBLK = NLOC // 512
NPAIR = H // 2
STW = 65
STP = NPAIR // 2 * STW * 2
SCOLS = 4 * STW

N_PHASES = int(os.environ.get("MK_PHASES", "99"))
CCMASK = int(os.environ.get("MK_CCMASK", "7"))
ABN = int(os.environ.get("MK_ABN", "0"))
MK_ONLY = set(int(s) for s in os.environ.get("MK_ONLY", "").split(",") if s)
NOCC = os.environ.get("MK_NOCC", "") == "1"
DEBUG = [s for s in os.environ.get("MK_DEBUG", "").split(",") if s]


class Prog:
    def __init__(self, nc):
        self.nc = nc
        self.es = contextlib.ExitStack()
        self.sems = {}
        self.dram = {}
        self.consts = {}
        self.dbg = {}
        self.bar_count = 0
        self.clear_count = 0
        self.phase_sems = []
        self.protected = set()

    def sem(self, name, protect=False):
        if name not in self.sems:
            self.sems[name] = self.es.enter_context(self.nc.semaphore(name))
            if protect:
                self.protected.add(name)
            else:
                self.phase_sems.append((name, self.sems[name]))
        return self.sems[name]

    def drains(self, eng, streams):
        for ds in streams:
            for s in range(ds.n):
                if ds.issued[s]:
                    eng.wait_ge(ds.sems[s], 16 * ds.per * ds.issued[s])


def barrier(p, block):
    sem = p.sem("s_bar", protect=True)
    bsem = p.sem("s_barc", protect=True)
    p.bar_count += 1
    target = 5 * p.bar_count
    for name in ("sync", "gpsimd", "vector", "scalar", "tensor"):
        def f(eng):
            eng.nop().then_inc(sem, 1)
            eng.wait_ge(sem, target)
        getattr(block, name)(f)
    to_clear = [h for (n, h) in p.phase_sems if n not in p.protected]
    names = [n for (n, h) in p.phase_sems if n not in p.protected]
    if to_clear:
        p.clear_count += 1

        @block.gpsimd
        def f2(gpsimd):
            p.nc.clear_and_free_semaphores(to_clear)
            gpsimd.nop().then_inc(bsem, 1)

        for name in ("sync", "vector", "scalar", "tensor"):
            def f3(eng):
                eng.wait_ge(bsem, p.clear_count)
            getattr(block, name)(f3)
        for n in names:
            del p.sems[n]
    p.phase_sems = []



def emit_wconv(p, block):
    nc = p.nc
    d = p.dram
    s_wcv = p.sem("s_wcv", protect=True)
    dcv = Dstr(p, "wcv", 2)

    @block.gpsimd
    def _(gpsimd):
        n = 0
        for ch in range(32):
            dcv.issue(gpsimd, n, [(d["w1b"][ch * 128:(ch + 1) * 128, :],
                                   d["w1f"][ch * 128:(ch + 1) * 128, :], {})])
            gpsimd.nop().then_inc(s_wcv, 1)
            n += 1
        for ch in range(128):
            dcv.issue(gpsimd, n, [(d["w2b"][ch * 128:(ch + 1) * 128, :],
                                   d["w2f"][ch * 128:(ch + 1) * 128, :], {})])
            gpsimd.nop().then_inc(s_wcv, 1)
            n += 1
        p.drains(gpsimd, [dcv])
        gpsimd.nop().then_inc(s_wcv, 100)


class Dstr:

    def __init__(self, p, name, nslots, per=1):
        self.sems = [p.sem(f"{name}{i}") for i in range(nslots)]
        self.n = nslots
        self.per = per
        self.issued = [0] * nslots

    def issue(self, eng, idx, pairs):
        s = idx % self.n
        r = self.issued[s]
        if r >= 1:
            eng.wait_ge(self.sems[s], 16 * self.per * r)
        for out, in_, kw in pairs:
            eng.dma_start(out=out, in_=in_, **kw).then_inc(self.sems[s], 16)
        self.issued[s] += 1

    def wait(self, eng, idx):
        s = idx % self.n
        eng.wait_ge(self.sems[s], 16 * self.per * (idx // self.n + 1))



def selfsync(p, eng, name):
    sem = p.sem(name)
    base = getattr(p, "_ss_" + name, 0)

    def step(ins):
        nonlocal base
        base += 1
        setattr(p, "_ss_" + name, base)
        ins.then_inc(sem, 1)
        eng.wait_ge(sem, base)
        return ins
    return step



def build_program():
    nc = bass.Bass("TRN2", target_bir_lowering=False)
    p = Prog(nc)
    d = p.dram

    def inp(name, shape, dtype=F32):
        d[name] = nc.dram_tensor(name, shape, dtype, kind="ExternalInput")

    inp("x_in", [NLOC, E])
    inp("w_qkv", [E, 3 * E])
    inp("w_out", [E, E])
    inp("g1c", [128, 8])
    inp("g2c", [128, 8])
    inp("w_dp1", [E, 256])
    inp("w_dp2", [256, 1])
    inp("w_gate", [E, F])
    inp("w_up", [E, F])
    inp("w_down", [F, E])
    inp("ghidc", [128, 36])
    inp("wk", [E, DB])
    inp("wv", [E, DB])
    inp("wqf", [NQ * E, DB])
    inp("wp", [E, E])
    inp("bp_row", [1, E])
    inp("gn_row", [1, E])
    inp("gs_row", [1, E])
    inp("slot_emb", [4 * S, E])
    inp("w1f", [4 * E, HID])
    inp("b1c", [128, 4 * 32])
    inp("w2f", [4 * HID, E])
    inp("b2", [4, E])
    inp("cos_rep", [NLOC, E])
    inp("sin_rep", [NLOC, E])
    inp("tril", [128, 128], BF16)
    inp("diagmask", [512, 512], BF16)
    inp("ones", [128, 128], BF16)
    inp("onesf", [128, 8], F32)
    inp("onesrowf", [1, 128], F32)
    inp("g2_row", [1, E], F32)
    inp("ghid_row", [1, F], BF16)
    inp("iota_row", [1, F])
    inp("wstate", [128, 8])
    inp("offs", [1, 32], I32)
    inp("identb", [128, 128], BF16)
    inp("identf", [128, 128], F32)

    d["x_out"] = nc.dram_tensor("x_out", [NLOC, E], F32, kind="ExternalOutput")

    def scr(name, shape, dtype):
        d[name] = nc.dram_tensor(name, shape, dtype)

    scr("xcur", [NLOC, E], F32)
    scr("w1b", [4 * E, HID], BF16)
    scr("w2b", [4 * HID, E], BF16)
    scr("ybuf", [NLOC, E], F32)
    scr("hbuf", [NLOC, E], BF16)
    scr("kbuf", [NLOC, E], BF16)
    scr("qbuf", [NLOC, E], BF16)
    scr("vext", [NLOC, H * STW], BF16)
    scr("aobuf", [NLOC, E], BF16)
    scr("hidbuf", [NLOC, F], BF16)
    scr("cc_st_in", [128, 2 * SCOLS], F32)
    scr("cc_st_out", [8 * 128, 2 * SCOLS], F32)
    scr("cc_xm_in", [1, E], F32)
    scr("cc_xm_out", [8, E], F32)

    for name in DEBUG:
        shp = {"y_ln": [NLOC, E], "h1": [NLOC, E], "kv": [NLOC, E + H * STW],
               "h2f": [NLOC, E],
               "states": [128, 2 * SCOLS], "stin": [128, 2 * SCOLS],
               "y_lin": [NLOC, E], "h2": [NLOC, E], "xmean": [8, E],
               "sizes": [1, 8], "y_ad": [NLOC, E], "xn": [NLOC, E],
               "edisp": [128, 8192], "dm": [16, 128],
               "slots": [4 * S, E], "cn": [128, 128], "eo": [128, 16 * (E + 2)],
               "outp": [L, E + 2], "ffn1": [NLOC, E],
               "xlnb": [NLOC, E], "kvb": [NLOC, 2 * DB], "qh": [NLOC, DB],
               "ob": [4 * L, DB], "attnb": [NLOC, E],
               "mbufd": [NLOC, E + 2], "pbufd": [NLOC, E + 2]}[name]
        dt = {"kv": BF16, "h1": BF16, "h2": BF16, "edisp": BF16, "xn": BF16,
              "eo": BF16, "outp": BF16, "dm": BF16, "cn": BF16, "slots": BF16,
              "mbufd": BF16, "pbufd": BF16,
              "xlnb": BF16, "kvb": BF16, "qh": BF16, "ob": BF16}.get(name, F32)
        p.dbg[name] = nc.dram_tensor("dbg_" + name, shp, dt, kind="ExternalOutput")

    with nc.Block() as block:
        ph_setup(p, block)
        if MK_ONLY:
            ph_stub(p, block)
            emit_wconv(p, block)
            barrier(p, block)
            for phn in sorted(MK_ONLY):
                if phn == 4:
                    ph_ffn(p, block, 0)
                elif phn == 5:
                    ph_attnb(p, block)
                elif phn == 6:
                    ph_ffn(p, block, 1)
                barrier(p, block)
        else:
            if N_PHASES >= 1:
                ph_a(p, block)
            barrier(p, block)
            if N_PHASES >= 2:
                ph_b(p, block)
                emit_wconv(p, block)
                barrier(p, block)
            if N_PHASES >= 3:
                ph_c(p, block)
                barrier(p, block)
            if N_PHASES >= 4:
                ph_ffn(p, block, 0)
                barrier(p, block)
            if N_PHASES >= 5:
                ph_attnb(p, block)
                barrier(p, block)
            if N_PHASES >= 6:
                ph_ffn(p, block, 1)
                barrier(p, block)
        ph_finish(p, block)
    p.es.close()
    return p



def ph_setup(p, block):
    nc = p.nc
    d = p.dram
    c = p.consts
    names = ["g1c", "g2c", "ghidc", "tril", "diagmask", "ones", "onesf", "onesrowf",
             "iota_row", "wstate", "offs", "identb", "identf", "bp_row"]
    shapes = {"g1c": ([128, 8], F32), "g2c": ([128, 8], F32),
              "ghidc": ([128, 36], F32), "tril": ([128, 128], BF16),
              "diagmask": ([128, 4 * 512], BF16), "ones": ([128, 128], BF16),
              "onesf": ([128, 8], F32), "onesrowf": ([1, 128], F32),
              "iota_row": ([1, F], F32),
              "wstate": ([128, 8], F32), "offs": ([1, 32], I32),
              "identb": ([128, 128], BF16), "identf": ([128, 128], F32),
              "bp_row": ([1, E], F32)}
    for n in names:
        c[n] = nc.alloc_sbuf_tensor("c_" + n, *shapes[n])
    s_ld = p.sem("s_setup")
    nl = len(names) + 3

    @block.sync
    def _(sync):
        for n in names:
            if n == "diagmask":
                for o in range(4):
                    sync.dma_start(out=c[n][:, o * 512:(o + 1) * 512],
                                   in_=d[n][o * 128:(o + 1) * 128, :]).then_inc(s_ld, 16)
            else:
                sync.dma_start(out=c[n][:], in_=d[n][:]).then_inc(s_ld, 16)

    for name in ("sync", "gpsimd", "vector", "scalar", "tensor"):
        def f(eng):
            eng.wait_ge(s_ld, 16 * nl)
        getattr(block, name)(f)



def ph_a(p, block):
    nc = p.nc
    d = p.dram
    c = p.consts
    aes = contextlib.ExitStack()
    p.es_ab = contextlib.ExitStack()
    sstates = p.es_ab.enter_context(
        nc.sbuf_tensor("a_sstates", [128, NT * 2 * SCOLS], BF16))

    def sb(name, shape, dtype):
        return aes.enter_context(nc.sbuf_tensor(name, shape, dtype))

    def psb(name):
        return aes.enter_context(nc.psum_tensor(name, [128, 512], F32))

    swkv = [sb(f"a_swkv{cc}", [128, 2 * E], BF16) for cc in range(8)]
    sx = sb("a_sx", [128, 2 * E], F32)
    sx2 = sb("a_sx2", [128, 2 * E], F32)
    sstat = sb("a_sstat", [128, 13 * 128], F32)
    sjunk = sb("a_sjunk", [128, E], F32)
    sy = sb("a_sy", [128, 2 * E], F32)
    shp = sb("a_shp", [128, 2 * E], BF16)
    sht = [sb(f"a_sht{i}", [128, 8 * 512], BF16) for i in range(2)]
    skv = sb("a_skv", [128, 3 * E], F32)
    skb = sb("a_skb", [128, 2 * E], BF16)
    svb = sb("a_svb", [128, 2 * H * STW], BF16)
    srop = sb("a_srop", [128, 4 * E], F32)
    stmp = sb("a_stmp", [128, 4 * E], F32)
    scur = sb("a_scur", [128, 2 * SCOLS], F32)
    ppk = [psb(f"a_ppk{i}") for i in range(4)]
    pst = [psb(f"a_pst{i}") for i in range(4)]
    p.a_handles = dict(sstates=sstates)
    p.a_es = aes

    s_w = p.sem("a_w")
    s_fw = p.sem("a_fw")
    s_acc = p.sem("a_acc")
    s_sa = p.sem("a_sa")
    s_sb = p.sem("a_sb")
    s_sc = p.sem("a_sc")
    s_sd = p.sem("a_sd")
    s_fin = p.sem("a_fin")
    s_pk = p.sem("a_pk")
    s_ev = p.sem("a_ev")
    s_r1 = p.sem("a_r1")
    s_r2 = p.sem("a_r2")
    s_rp = p.sem("a_rp")
    s_sm = p.sem("a_sm")
    s_sn = p.sem("a_sn")
    s_cc = p.sem("a_cc")
    p.sem("a_cd", protect=True)

    dx = Dstr(p, "a_dx", 2)
    dx2 = Dstr(p, "a_dx2", 2)
    dht = Dstr(p, "a_dht", 2, per=8)
    dhs = Dstr(p, "a_dhs", 4)
    dys = Dstr(p, "a_dys", 2)
    drl = Dstr(p, "a_drl", 2, per=2)
    dsp = Dstr(p, "a_dsp", 2, per=2)

    def R(j):
        return sstat[:, j * 128:(j + 1) * 128]

    def Rc(j, i):
        return sstat[:, j * 128 + i:j * 128 + i + 1]

    def vext_cols(o, h0, nh):
        v = svb[:, o * H * STW + h0 * STW:]
        return v.rearrange("p (h w) -> p h w", w=STW)[:, 0:nh, 0:DH]

    def vext_ones(o):
        v = svb[:, o * H * STW:]
        return v.rearrange("p (h w) -> p h w", w=STW)[:, 0:H, DH:STW]

    @block.sync
    def _(sync):
        for i in range(NT):
            if i >= 2:
                sync.wait_ge(s_acc, i - 1)
            dx.issue(sync, i, [(sx[:, (i % 2) * E:(i % 2) * E + E],
                               d["x_in"][i * 128:(i + 1) * 128, :], {})])
        for i in range(NT):
            if i >= 2:
                sync.wait_ge(s_fin, i - 1)
            dx2.issue(sync, i, [(sx2[:, (i % 2) * E:(i % 2) * E + E],
                                d["x_in"][i * 128:(i + 1) * 128, :], {})])
        for b in range(NBLK):
            for t in range(4):
                dhs.wait(sync, b * 4 + t)
            if b >= 2:
                sync.wait_ge(s_pk, 8 * (b - 1))
            dht.issue(sync, b, [
                (sht[b % 2][:, cc * 512:(cc + 1) * 512],
                 d["hbuf"][b * 512:(b + 1) * 512, cc * 128:(cc + 1) * 128],
                 {"transpose": True}) for cc in range(8)])

    @block.gpsimd
    def _(gpsimd):
        for cc in range(8):
            gpsimd.dma_start(out=swkv[cc][:],
                             in_=d["w_qkv"][cc * 128:(cc + 1) * 128, E:3 * E]).then_inc(s_w, 16)
        for i in range(NT):
            gpsimd.wait_ge(s_fin, i + 1)
            o = (i % 2) * E
            dys.issue(gpsimd, i, [(d["ybuf"][i * 128:(i + 1) * 128, :], sy[:, o:o + E], {})])
            dhs.issue(gpsimd, i, [(d["hbuf"][i * 128:(i + 1) * 128, :], shp[:, o:o + E], {})])
        for step in range(NT + 2):
            if step < NT:
                i = step
                if i >= 2:
                    gpsimd.wait_ge(s_r1, i - 1)
                o = (i % 2) * 2 * E
                drl.issue(gpsimd, i, [
                    (srop[:, o:o + E], d["cos_rep"][i * 128:(i + 1) * 128, :], {}),
                    (srop[:, o + E:o + 2 * E], d["sin_rep"][i * 128:(i + 1) * 128, :], {})])
            if 0 <= step - 2 < NT:
                i = step - 2
                gpsimd.wait_ge(s_rp, i + 1)
                o2 = (i % 2)
                dsp.issue(gpsimd, i, [
                    (d["kbuf"][i * 128:(i + 1) * 128, :], skb[:, o2 * E:o2 * E + E], {}),
                    (d["vext"][i * 128:(i + 1) * 128, :],
                     svb[:, o2 * H * STW:(o2 + 1) * H * STW], {})])
        gpsimd.wait_ge(s_sn, NT)
        p.drains(gpsimd, [dys, dhs, dsp])
        gpsimd.dma_start(out=d["cc_st_in"][:], in_=scur[:]).then_inc(s_cc, 16)
        gpsimd.wait_ge(s_cc, 16)
        if not NOCC:
            gpsimd.collective_compute(
                "AllGather", ALU.bypass, replica_groups=[[0, 1, 2, 3, 4, 5, 6, 7]],
                ins=[d["cc_st_in"].ap().opt()], outs=[d["cc_st_out"].ap().opt()],
            ).then_inc(p.sem("a_cd"), 1)
        else:
            gpsimd.nop().then_inc(p.sem("a_cd"), 1)

    @block.scalar
    def _(scalar):
        scalar.wait_ge(p.sem("a_ms"), 1)
        for i in range(NT):
            dx.wait(scalar, i)
            xt = sx[:, (i % 2) * E:(i % 2) * E + E]
            scalar.activation(sjunk[:], xt, AF.Copy, accum_out=Rc(0, i))
            scalar.activation(sjunk[:], xt, AF.Square,
                              accum_out=Rc(1, i)).then_inc(s_acc, 1)
        scalar.wait_ge(s_sb, 1)
        scalar.activation(R(6), R(5), AF.Sqrt)
        scalar.activation(R(9), R(8), AF.Sqrt).then_inc(s_sc, 1)
        for step in range(NT + 1):
            if step < NT:
                i = step
                scalar.wait_ge(s_pk, 2 * i + 2)
                if i >= 1:
                    scalar.wait_ge(s_r1, i)
                if i >= 2:
                    dsp.wait(scalar, i - 2)
                    scalar.wait_ge(s_sm, i - 1)
                scalar.activation(skv[:, 0:512], ppk[0][:], AF.Copy)
                scalar.activation(skv[:, 512:1024], ppk[1][:], AF.Copy)
                o2 = (i % 2)
                scalar.activation(vext_cols(o2, 0, 8),
                                  ppk[2][:].rearrange("p (h e) -> p h e", e=DH), AF.Copy)
                scalar.activation(vext_cols(o2, 8, 8),
                                  ppk[3][:].rearrange("p (h e) -> p h e", e=DH),
                                  AF.Copy).then_inc(s_ev, 1)
            if 0 <= step - 1 < NT:
                i = step - 1
                scalar.wait_ge(s_r1, i + 1)
                if i >= 2:
                    scalar.wait_ge(s_rp, i - 1)
                eo = E + (i % 2) * E
                scalar.activation(skv[:, eo:eo + E],
                                  stmp[:, (i % 2) * 2 * E + E:(i % 2) * 2 * E + 2 * E],
                                  AF.Exp).then_inc(s_r2, 1)

    @block.vector
    def _(vector):
        vector.memset(sstat[:], 0.0).then_inc(p.sem("a_ms"), 1)
        vector.wait_ge(s_w, 16 * 8)
        for cc in range(8):
            ins = vector.tensor_scalar_mul(swkv[cc][:], swkv[cc][:], c["g1c"][:, cc:cc + 1])
        ins.then_inc(s_fw, 1)
        for o in range(2):
            vector.memset(vext_ones(o), 1.0)
        vector.wait_ge(s_acc, NT)
        st = selfsync(p, vector, "a_ssv")
        st(vector.tensor_scalar_mul(R(2), R(0), 1.0 / E))
        st(vector.tensor_tensor(R(3), R(2), R(2), ALU.mult))
        st(vector.tensor_scalar_mul(R(4), R(1), 1.0 / E))
        st(vector.tensor_tensor(R(4), R(4), R(3), ALU.subtract))
        st(vector.tensor_scalar_add(R(5), R(4), 1e-5))
        st(vector.reciprocal(R(5), R(5)))
        st(vector.tensor_tensor(R(8), R(4), R(5), ALU.mult))
        st(vector.tensor_scalar_add(R(8), R(8), 1e-6))
        vector.nop().then_inc(s_sb, 1)
        vector.wait_ge(s_sc, 1)
        st(vector.scalar_tensor_tensor(R(7), R(2), -1.0, R(6), ALU.mult, ALU.mult))
        st(vector.tensor_scalar(R(10), R(9), 1e-6, 1e6, ALU.max, ALU.min))
        st(vector.reciprocal(R(10), R(10)))
        st(vector.tensor_tensor(R(11), R(6), R(10), ALU.mult))
        st(vector.tensor_tensor(R(12), R(7), R(10), ALU.mult))
        vector.nop().then_inc(s_sd, 1)
        for i in range(NT):
            dx2.wait(vector, i)
            if i >= 2:
                dys.wait(vector, i - 2)
                dhs.wait(vector, i - 2)
            xt = sx2[:, (i % 2) * E:(i % 2) * E + E]
            o = (i % 2) * E
            vector.tensor_scalar(sy[:, o:o + E], xt, Rc(6, i), Rc(7, i),
                                 ALU.mult, ALU.add)
            vector.tensor_scalar(shp[:, o:o + E], xt, Rc(11, i), Rc(12, i),
                                 ALU.mult, ALU.add).then_inc(s_fin, 1)
        for step in range(NT + 2):
            if step < NT:
                i = step
                vector.wait_ge(s_ev, i + 1)
                drl.wait(vector, i)
                ot = (i % 2) * 2 * E
                orp = (i % 2) * 2 * E
                kt = skv[:, 0:E]
                cosb = srop[:, orp:orp + E]
                sinb = srop[:, orp + E:orp + 2 * E]
                t0 = stmp[:, ot:ot + E]
                t1 = stmp[:, ot + E:ot + 2 * E]
                vector.tensor_copy(t0[:, 0:E - 1:2], kt[:, 1:E:2])
                vector.tensor_copy(t0[:, 1:E:2], kt[:, 0:E - 1:2])
                vector.tensor_tensor(t0, t0, sinb, ALU.mult)
                vector.tensor_tensor(t1, kt, cosb, ALU.mult)
                vector.tensor_tensor(t0, t0, t1, ALU.add)
                vector.tensor_scalar_min(t1, t0, 0.0)
                vector.tensor_scalar_max(t0, t0, 0.0).then_inc(s_r1, 1)
            if 0 <= step - 1 < NT:
                i = step - 1
                vector.wait_ge(s_r2, i + 1)
                if i >= 2:
                    vector.wait_ge(s_sm, i - 1)
                    dsp.wait(vector, i - 2)
                eo = E + (i % 2) * E
                ins = vector.tensor_tensor(skb[:, (i % 2) * E:(i % 2) * E + E],
                                           stmp[:, (i % 2) * 2 * E:(i % 2) * 2 * E + E],
                                           skv[:, eo:eo + E], ALU.add)
                ins.then_inc(s_rp, 1)
            if 0 <= step - 2 < NT:
                j = step - 2
                vector.wait_ge(s_sm, j + 1)
                b0, b1 = pst[2 * (j % 2)], pst[2 * (j % 2) + 1]
                if j == 0:
                    vector.tensor_copy(scur[:, 0:SCOLS], b0[:, 0:SCOLS])
                    vector.tensor_copy(scur[:, SCOLS:2 * SCOLS], b1[:, 0:SCOLS])
                else:
                    vector.tensor_tensor(scur[:, 0:SCOLS], b0[:, 0:SCOLS],
                                         scur[:, 0:SCOLS], ALU.add)
                    vector.tensor_tensor(scur[:, SCOLS:2 * SCOLS], b1[:, 0:SCOLS],
                                         scur[:, SCOLS:2 * SCOLS], ALU.add)
                ins = vector.tensor_copy(
                    sstates[:, (2 * j) * SCOLS:(2 * j + 2) * SCOLS], scur[:])
                ins.then_inc(s_sn, 1)

    @block.tensor
    def _(tensor):
        tensor.wait_ge(s_fw, 1)

        def state_mms(j):
            tensor.wait_ge(s_rp, j + 1)
            if j >= 2:
                tensor.wait_ge(s_sn, j - 1)
            o2 = (j % 2)
            mm = None
            for h in range(H):
                pair = h // 2
                bank = pst[2 * (j % 2) + pair // 4]
                rowbase = (h % 2) * 64
                colbase = (pair % 4) * STW
                mm = tensor.matmul(
                    bank[rowbase:rowbase + 64, colbase:colbase + STW],
                    skb[:, o2 * E + h * DH:o2 * E + (h + 1) * DH],
                    svb[:, o2 * H * STW + h * STW:o2 * H * STW + (h + 1) * STW],
                    start=True, stop=True,
                    tile_position=(0, rowbase))
            mm.then_inc(s_sm, 1)

        for i in range(NT):
            b, sub = i // 4, i % 4
            if sub == 0:
                dht.wait(tensor, b)
            if i >= 1:
                tensor.wait_ge(s_ev, i)
            mm = None
            for half in range(2):
                for cc in range(8):
                    mm = tensor.matmul(
                        ppk[half][:],
                        sht[b % 2][:, cc * 512 + sub * 128:cc * 512 + sub * 128 + 128],
                        swkv[cc][:, half * 512:(half + 1) * 512],
                        start=(cc == 0), stop=(cc == 7))
            mm.then_inc(s_pk, 1)
            for half in range(2):
                for cc in range(8):
                    mm = tensor.matmul(
                        ppk[2 + half][:],
                        sht[b % 2][:, cc * 512 + sub * 128:cc * 512 + sub * 128 + 128],
                        swkv[cc][:, E + half * 512:E + (half + 1) * 512],
                        start=(cc == 0), stop=(cc == 7))
            mm.then_inc(s_pk, 1)
            if i >= 1:
                state_mms(i - 1)
        state_mms(NT - 1)

    emit_ph_a_debug(p, block)
    aes.close()


def emit_ph_a_debug(p, block):
    d, dbg = p.dram, p.dbg
    if not dbg:
        return
    s_dbg = p.sem("s_dbg")
    n = [0]

    @block.sync
    def _(sync):
        def cp(dst, src):
            sync.dma_start(out=dst, in_=src).then_inc(s_dbg, 16)
            n[0] += 1
        if "y_ln" in dbg:
            for s in range(2):
                sync.wait_ge(p.sem(f"a_dys{s}"), 16 * (NT // 2))
            cp(dbg["y_ln"][:], d["ybuf"][:])
        if "h1" in dbg:
            for s in range(4):
                sync.wait_ge(p.sem(f"a_dhs{s}"), 16 * (NT // 4))
            cp(dbg["h1"][:], d["hbuf"][:])
        if "kv" in dbg:
            for s in range(2):
                sync.wait_ge(p.sem(f"a_dsp{s}"), 16 * 2 * (NT // 2))
            cp(dbg["kv"][:, 0:E], d["kbuf"][:])
            cp(dbg["kv"][:, E:E + H * STW], d["vext"][:])
        if "states" in dbg:
            sync.wait_ge(p.sem("a_cc"), 16)
            sync.wait_ge(p.sem("a_cd"), 1)
            cp(dbg["states"][:], d["cc_st_in"][:])
        sync.wait_ge(s_dbg, 16 * n[0])


def ph_b(p, block):
    nc = p.nc
    d = p.dram
    c = p.consts
    bes = contextlib.ExitStack()
    scanes = contextlib.ExitStack()

    def sb(name, shape, dtype, es=None):
        return (es or bes).enter_context(nc.sbuf_tensor(name, shape, dtype))

    def psb(name, es=None):
        return (es or bes).enter_context(nc.psum_tensor(name, [128, 512], F32))

    sstates = p.a_handles["sstates"]
    swq = [sb(f"b_swq{cc}", [128, E], BF16) for cc in range(8)]
    sht = [sb(f"b_sht{i}", [128, 8 * 512], BF16) for i in range(2)]
    sgath = sb("b_sgath", [128, 2 * 2 * SCOLS], F32)
    sin_ = sb("b_sin", [128, 2 * SCOLS], F32)
    sprev = sb("b_sprev", [128, 2 * 2 * SCOLS], BF16)
    sq = sb("b_sq", [128, 3 * E], F32)
    sqb = sb("b_sqb", [128, 2 * E], BF16)
    srop = sb("b_srop", [128, 4 * E], F32)
    stmp = sb("b_stmp", [128, 4 * E], F32)
    sqt = sb("b_sqt", [128, 2 * E], BF16)
    skt = sb("b_skt", [128, 2 * E], BF16)
    svx = sb("b_svx", [128, 2 * H * STW], BF16)
    sA = sb("b_sA", [128, 4 * 128], BF16)
    sden = sb("b_sden", [128, 64], F32)
    sao = sb("b_sao", [128, 2 * E], BF16)
    saot = sb("b_saot", [128, 2 * 8 * 512], BF16)
    syt = sb("b_syt", [128, 2 * E], F32)
    pq = [psb(f"b_pq{i}", scanes) for i in range(2)]
    pA = [psb(f"b_pA{i}", scanes) for i in range(2)]
    pn = [psb(f"b_pn{i}", scanes) for i in range(4)]

    s_w = p.sem("b_w")
    s_fw = p.sem("b_fw")
    s_si = p.sem("b_si")
    s_ga = p.sem("b_gacc")
    s_pq = p.sem("b_pq")
    s_qe = p.sem("b_qe")
    s_r1 = p.sem("b_r1")
    s_r2 = p.sem("b_r2")
    s_qp = p.sem("b_qp")
    s_sv = p.sem("b_sv")
    s_am = p.sem("b_am")
    s_ni = p.sem("b_ni")
    s_ae = p.sem("b_ae")
    s_nm = p.sem("b_nm")
    s_no = p.sem("b_no")
    s_po = p.sem("b_po")
    s_yo = p.sem("b_yo")
    s_wo = p.sem("b_wo")

    dht = Dstr(p, "b_dht", 2, per=8)
    dqs = Dstr(p, "b_dqs", 2)
    dqt = Dstr(p, "b_dqt", 2, per=8)
    dkt = Dstr(p, "b_dkt", 2, per=8)
    dvx = Dstr(p, "b_dvx", 2)
    drl = Dstr(p, "b_drl", 2, per=2)
    dgt = Dstr(p, "b_dgt", 2)
    das = Dstr(p, "b_das", 2)
    dat = Dstr(p, "b_dat", 2, per=8)
    dyl = Dstr(p, "b_dyl", 2)
    dyo = Dstr(p, "b_dyo", 2)

    def st_cols(pair):
        return (pair % 4) * STW + (pair // 4) * SCOLS

    @block.sync
    def _(sync):
        sync.wait_ge(p.sem("a_cd"), 1)
        for r in range(8):
            if r >= 2:
                sync.wait_ge(s_ga, r - 1)
            gsrc = d["cc_st_in"][:] if NOCC else d["cc_st_out"][r * 128:(r + 1) * 128, :]
            dgt.issue(sync, r, [(sgath[:, (r % 2) * 2 * SCOLS:(r % 2 + 1) * 2 * SCOLS],
                                 gsrc, {})])
        def issue_dht(b):
            dht.issue(sync, b, [
                (sht[b % 2][:, cc * 512:(cc + 1) * 512],
                 d["hbuf"][b * 512:(b + 1) * 512, cc * 128:(cc + 1) * 128],
                 {"transpose": True}) for cc in range(8)])

        issue_dht(0)
        for j in range(NT):
            if j % 4 == 0 and j // 4 + 1 < NBLK:
                b = j // 4
                if b >= 1:
                    sync.wait_ge(s_pq, 8 * b)
                issue_dht(b + 1)
            if j >= 2:
                sync.wait_ge(s_nm, j - 1)
            dkt.issue(sync, j, [
                (skt[:, (j % 2) * E + hh * 128:(j % 2) * E + (hh + 1) * 128],
                 d["kbuf"][j * 128:(j + 1) * 128, hh * 128:(hh + 1) * 128],
                 {"transpose": True}) for hh in range(8)])
            dvx.issue(sync, j, [(svx[:, (j % 2) * H * STW:(j % 2 + 1) * H * STW],
                                 d["vext"][j * 128:(j + 1) * 128, :], {})])
            dqs.wait(sync, j)
            dqt.issue(sync, j, [
                (sqt[:, (j % 2) * E + hh * 128:(j % 2) * E + (hh + 1) * 128],
                 d["qbuf"][j * 128:(j + 1) * 128, hh * 128:(hh + 1) * 128],
                 {"transpose": True}) for hh in range(8)])
        for i in range(NT):
            if i % 4 == 0:
                b = i // 4
                for t in range(4):
                    das.wait(sync, b * 4 + t)
                if b >= 2:
                    sync.wait_ge(s_po, 4 * (b - 1))
                dat.issue(sync, b, [
                    (saot[:, (b % 2) * 8 * 512 + cc * 512:(b % 2) * 8 * 512 + (cc + 1) * 512],
                     d["aobuf"][b * 512:(b + 1) * 512, cc * 128:(cc + 1) * 128],
                     {"transpose": True}) for cc in range(8)])
            if i >= 2:
                sync.wait_ge(s_yo, i - 1)
                dyo.wait(sync, i - 2)
            dyl.issue(sync, i, [(syt[:, (i % 2) * E:(i % 2) * E + E],
                                 d["ybuf"][i * 128:(i + 1) * 128, :], {})])

    @block.gpsimd
    def _(gpsimd):
        for cc in range(8):
            gpsimd.dma_start(out=swq[cc][:],
                             in_=d["w_qkv"][cc * 128:(cc + 1) * 128, 0:E]).then_inc(s_w, 16)
        for step in range(NT + 5):
            if step < NT:
                i = step
                if i >= 2:
                    gpsimd.wait_ge(s_r1, i - 1)
                o = (i % 2) * 2 * E
                drl.issue(gpsimd, i, [
                    (srop[:, o:o + E], d["cos_rep"][i * 128:(i + 1) * 128, :], {}),
                    (srop[:, o + E:o + 2 * E], d["sin_rep"][i * 128:(i + 1) * 128, :], {})])
            if 0 <= step - 2 < NT:
                i = step - 2
                gpsimd.wait_ge(s_qp, i + 1)
                o2 = (i % 2)
                dqs.issue(gpsimd, i, [(d["qbuf"][i * 128:(i + 1) * 128, :],
                                       sqb[:, o2 * E:o2 * E + E], {})])
            if 0 <= step - 5 < NT:
                j = step - 5
                gpsimd.wait_ge(s_no, j + 1)
                das.issue(gpsimd, j, [(d["aobuf"][j * 128:(j + 1) * 128, :],
                                       sao[:, (j % 2) * E:(j % 2) * E + E], {})])
        gpsimd.wait_ge(s_pq, 2 * NT)
        for cc in range(8):
            gpsimd.dma_start(out=sht[cc // 4][:, (cc % 4) * 1024:(cc % 4) * 1024 + 1024],
                             in_=d["w_out"][cc * 128:(cc + 1) * 128, :]).then_inc(s_wo, 16)
        for i in range(NT):
            gpsimd.wait_ge(s_yo, i + 1)
            dyo.issue(gpsimd, i, [(d["ybuf"][i * 128:(i + 1) * 128, :],
                                   syt[:, (i % 2) * E:(i % 2) * E + E], {})])
        p.drains(gpsimd, [dqs, das, dyo])

    @block.scalar
    def _(scalar):
        for step in range(NT + 1):
            if 0 <= step - 1 < NT:
                i = step - 1
                scalar.wait_ge(s_r1, i + 1)
                if i >= 2:
                    scalar.wait_ge(s_qp, i - 1)
                eo = E + (i % 2) * E
                scalar.activation(sq[:, eo:eo + E],
                                  stmp[:, (i % 2) * 2 * E + E:(i % 2) * 2 * E + 2 * E],
                                  AF.Exp).then_inc(s_r2, 1)
            if step < NT:
                i = step
                scalar.wait_ge(s_pq, 2 * i + 2)
                if i >= 1:
                    scalar.wait_ge(s_r1, i)
                scalar.activation(sq[:, 0:512], pq[0][:], AF.Copy)
                scalar.activation(sq[:, 512:1024], pq[1][:], AF.Copy).then_inc(s_qe, 1)

    @block.vector
    def _(vector):
        vector.wait_ge(s_w, 16 * 8)
        for cc in range(8):
            ins = vector.tensor_scalar_mul(swq[cc][:], swq[cc][:], c["g1c"][:, cc:cc + 1])
        ins.then_inc(s_fw, 1)
        stp = selfsync(p, vector, "b_ssv")
        for r in range(8):
            dgt.wait(vector, r)
            g = sgath[:, (r % 2) * 2 * SCOLS:(r % 2 + 1) * 2 * SCOLS]
            if r == 0:
                stp(vector.tensor_scalar_mul(sin_[:], g, c["wstate"][:, 0:1]))
            else:
                stp(vector.scalar_tensor_tensor(sin_[:], g, c["wstate"][:, r:r + 1],
                                                sin_[:], ALU.mult, ALU.add))
            vector.nop().then_inc(s_ga, 1)
        vector.nop().then_inc(s_si, 1)
        for step in range(NT + 4):
            if 0 <= step - 4 < NT:
                j = step - 4
                vector.wait_ge(s_nm, j + 1)
                if j >= 2:
                    das.wait(vector, j - 2)
                sts = selfsync(p, vector, "b_ssn")
                for bk in range(4):
                    sts(vector.tensor_copy(
                        sden[:, bk * 4:bk * 4 + 4],
                        pn[bk][:, 0:4 * STW].rearrange("p (h w) -> p h w", w=STW)
                        [:, :, DH:STW].rearrange("p h w -> p (h w)")))
                sts(vector.tensor_scalar_max(sden[:, 16:32], sden[:, 0:16], 1e-6))
                sts(vector.reciprocal(sden[:, 32:48], sden[:, 16:32]))
                for h in range(H):
                    vector.tensor_scalar_mul(
                        sao[:, (j % 2) * E + h * DH:(j % 2) * E + (h + 1) * DH],
                        pn[h // 4][:, (h % 4) * STW:(h % 4) * STW + DH],
                        sden[:, 32 + h:33 + h])
                vector.nop().then_inc(s_no, 1)
            if 0 <= step - 3 < NT:
                j = step - 3
                for h in range(H):
                    g = j * H + h
                    vector.wait_ge(s_am, g + 1)
                    if g >= 4:
                        vector.wait_ge(s_ni, g - 3)
                    vector.tensor_tensor(
                        sA[:, (g % 4) * 128:(g % 4) * 128 + 128],
                        pA[h % 2][:, 0:128], c["tril"][:], ALU.mult).then_inc(s_ae, 1)
            if 0 <= step - 2 < NT:
                j = step - 2
                vector.wait_ge(s_si, 1)
                if j >= 2:
                    vector.wait_ge(s_nm, j - 1)
                dstp = sprev[:, (j % 2) * 2 * SCOLS:(j % 2 + 1) * 2 * SCOLS]
                if j == 0:
                    vector.tensor_copy(dstp, sin_[:]).then_inc(s_sv, 1)
                else:
                    vector.tensor_tensor(
                        dstp, sstates[:, (2 * j - 2) * SCOLS:(2 * j) * SCOLS],
                        sin_[:], ALU.add).then_inc(s_sv, 1)
            if 0 <= step - 1 < NT:
                i = step - 1
                vector.wait_ge(s_r2, i + 1)
                if i >= 2:
                    dqs.wait(vector, i - 2)
                eo = E + (i % 2) * E
                vector.tensor_tensor(sqb[:, (i % 2) * E:(i % 2) * E + E],
                                     stmp[:, (i % 2) * 2 * E:(i % 2) * 2 * E + E],
                                     sq[:, eo:eo + E], ALU.add).then_inc(s_qp, 1)
            if step < NT:
                i = step
                vector.wait_ge(s_qe, i + 1)
                drl.wait(vector, i)
                ot = (i % 2) * 2 * E
                qt_ = sq[:, 0:E]
                cosb = srop[:, ot:ot + E]
                sinb = srop[:, ot + E:ot + 2 * E]
                t0 = stmp[:, ot:ot + E]
                t1 = stmp[:, ot + E:ot + 2 * E]
                vector.tensor_copy(t0[:, 0:E - 1:2], qt_[:, 1:E:2])
                vector.tensor_copy(t0[:, 1:E:2], qt_[:, 0:E - 1:2])
                vector.tensor_tensor(t0, t0, sinb, ALU.mult)
                vector.tensor_tensor(t1, qt_, cosb, ALU.mult)
                vector.tensor_tensor(t0, t0, t1, ALU.add)
                vector.tensor_scalar_min(t1, t0, 0.0)
                vector.tensor_scalar_max(t0, t0, 0.0).then_inc(s_r1, 1)

    @block.tensor
    def _(tensor):
        tensor.wait_ge(s_fw, 1)
        for step in range(NT + 3):
            if step < NT:
                i = step
                b, sub = i // 4, i % 4
                if sub == 0:
                    dht.wait(tensor, b)
                if i >= 1:
                    tensor.wait_ge(s_qe, i)
                for half in range(2):
                    mm = None
                    for cc in range(8):
                        mm = tensor.matmul(
                            pq[half][:],
                            sht[b % 2][:, cc * 512 + sub * 128:cc * 512 + sub * 128 + 128],
                            swq[cc][:, half * 512:(half + 1) * 512],
                            start=(cc == 0), stop=(cc == 7))
                    mm.then_inc(s_pq, 1)
            if 0 <= step - 3 < NT:
                j = step - 3
                dkt.wait(tensor, j)
                dqt.wait(tensor, j)
                dvx.wait(tensor, j)
                tensor.wait_ge(s_sv, j + 1)
                o2 = (j % 2)
                for hs in range(H + 2):
                    if hs < H:
                        h = hs
                        rb = (h % 2) * 64
                        hh = h // 2
                        if j * H + h >= 2:
                            tensor.wait_ge(s_ae, j * H + h - 1)
                        tensor.matmul(
                            pA[h % 2][:, 0:128],
                            skt[rb:rb + 64, o2 * E + hh * 128:o2 * E + hh * 128 + 128],
                            sqt[rb:rb + 64, o2 * E + hh * 128:o2 * E + hh * 128 + 128],
                            start=True, stop=True,
                            tile_position=(rb, 0)).then_inc(s_am, 1)
                    if hs - 2 >= 0:
                        h = hs - 2
                        rb = (h % 2) * 64
                        hh = h // 2
                        bank = pn[h // 4]
                        col = (h % 4) * STW
                        vslice = svx[:, o2 * H * STW + h * STW:o2 * H * STW + (h + 1) * STW]
                        tensor.wait_ge(s_ae, j * H + h + 1)
                        if j >= 1 and h == 0:
                            tensor.wait_ge(s_no, j)
                        g = j * H + h
                        tensor.matmul(bank[:, col:col + STW],
                                      sA[:, (g % 4) * 128:(g % 4) * 128 + 128],
                                      vslice, start=True, stop=False).then_inc(s_ni, 1)
                        mm = tensor.matmul(
                            bank[:, col:col + STW],
                            sqt[rb:rb + 64, o2 * E + hh * 128:o2 * E + hh * 128 + 128],
                            sprev[rb:rb + 64, o2 * 2 * SCOLS + st_cols(hh):
                                  o2 * 2 * SCOLS + st_cols(hh) + STW],
                            start=False, stop=True, tile_position=(rb, 0))
                        if h == H - 1:
                            mm.then_inc(s_nm, 1)

    scanes.close()
    po = [psb(f"b_po{i}") for i in range(4)]

    @block.tensor
    def _(tensor):
        tensor.wait_ge(s_no, NT)
        tensor.wait_ge(s_wo, 16 * 8)
        for i in range(NT):
            b, sub = i // 4, i % 4
            if sub == 0:
                dat.wait(tensor, b)
            if i >= 2:
                tensor.wait_ge(s_yo, i - 1)
            for half in range(2):
                mm = None
                for cc in range(8):
                    mm = tensor.matmul(
                        po[2 * (i % 2) + half][:],
                        saot[:, (b % 2) * 8 * 512 + cc * 512 + sub * 128:
                             (b % 2) * 8 * 512 + cc * 512 + sub * 128 + 128],
                        sht[cc // 4][:, (cc % 4) * 1024 + half * 512:
                                     (cc % 4) * 1024 + half * 512 + 512],
                        start=(cc == 0), stop=(cc == 7))
            mm.then_inc(s_po, 1)

    @block.vector
    def _(vector):
        for i in range(NT):
            dyl.wait(vector, i)
            vector.wait_ge(s_po, i + 1)
            o = (i % 2) * E
            vector.tensor_tensor(syt[:, o:o + 512], syt[:, o:o + 512],
                                 po[2 * (i % 2)][:], ALU.add)
            vector.tensor_tensor(syt[:, o + 512:o + E], syt[:, o + 512:o + E],
                                 po[2 * (i % 2) + 1][:], ALU.add).then_inc(s_yo, 1)

    emit_ph_b_debug(p, block)
    bes.close()
    p.es_ab.close()


def emit_ph_b_debug(p, block):
    d, dbg = p.dram, p.dbg
    if not dbg:
        return
    s_dbg = p.sem("s_dbgb")
    n = [0]

    @block.sync
    def _(sync):
        def cp(dst, src):
            sync.dma_start(out=dst, in_=src).then_inc(s_dbg, 16)
            n[0] += 1
        if "y_lin" in dbg:
            for s in range(2):
                sync.wait_ge(p.sem(f"b_dyo{s}"), 16 * (NT // 2))
            cp(dbg["y_lin"][:], d["ybuf"][:])
        if n[0]:
            sync.wait_ge(s_dbg, 16 * n[0])


def ph_c(p, block):
    nc = p.nc
    d = p.dram
    c = p.consts
    ces = contextlib.ExitStack()

    def sb(name, shape, dtype):
        return ces.enter_context(nc.sbuf_tensor(name, shape, dtype))

    def psb(name):
        return ces.enter_context(nc.psum_tensor(name, [128, 512], F32))

    if "h2buf" not in d:
        d["h2buf"] = nc.dram_tensor("h2buf", [NLOC, E], BF16)
        d["mrow"] = nc.dram_tensor("mrow", [1, F], BF16)
        d["sclr"] = nc.dram_tensor("sclr", [1, 8], F32)

    sy = sb("c_sy", [128, 2 * E], F32)
    sy2 = sb("c_sy2", [128, 2 * E], F32)
    sstat = sb("c_sstat", [128, 4 * 128], F32)
    sjunk = sb("c_sjunk", [128, E], F32)
    sh2 = sb("c_sh2", [128, 2 * E], BF16)
    sxm = sb("c_sxm", [128, 64], F32)
    sdp = sb("c_sdp", [128, 32], F32)
    sdpb = sb("c_sdpb", [128, 48], BF16)
    swdp1 = sb("c_swdp1", [128, 8 * 256], BF16)
    swdp2 = sb("c_swdp2", [128, 2], BF16)
    srow = sb("c_srow", [1, F], F32)
    srowb = sb("c_srowb", [1, F], BF16)
    smask = sb("c_smask", [128, F], BF16)
    sgh = sb("c_sgh", [128, F], BF16)
    sg2 = sb("c_sg2", [128, E], F32)
    sh2t = [sb(f"c_sh2t{i}", [128, 8 * 512], BF16) for i in range(2)]
    swg = sb("c_swg", [128, 2 * 8 * 512], BF16)
    swu = sb("c_swu", [128, 2 * 8 * 512], BF16)
    shid = sb("c_shid", [128, 2 * 512], F32)
    shb = sb("c_shb", [128, 2 * 512], BF16)
    ssq = sb("c_ssq", [128, 32 * 9], F32)
    srms = sb("c_srms", [128, 128], F32)
    swd = sb("c_swd", [128, 6 * E], BF16)
    sht = sb("c_sht", [128, 6 * 512], BF16)
    syx = sb("c_syx", [128, 2 * E], F32)
    sxi = sb("c_sxi", [128, 2 * E], F32)
    sxo = sb("c_sxo", [128, 2 * E], F32)
    pg = [psb(f"c_pg{i}") for i in range(2)]
    pu = [psb(f"c_pu{i}") for i in range(2)]
    pxm = [psb(f"c_pxm{i}") for i in range(2)]
    pd = [psb(f"c_pd{i}") for i in range(2)]

    s_sq = p.sem("c_sq")
    s_al = p.sem("c_al")
    s_h2 = p.sem("c_h2")
    s_xm = p.sem("c_xm")
    s_xe = p.sem("c_xe")
    s_cc = p.sem("c_ccd")
    s_dp = p.sem("c_dp")
    s_msk = p.sem("c_msk")
    s_gm = p.sem("c_gm")
    s_sl = p.sem("c_sl")
    s_hd = p.sem("c_hd")
    s_rm = p.sem("c_rm")
    s_dm = p.sem("c_dm")
    s_dc = p.sem("c_dc")
    s_xw = p.sem("c_xw")
    s_ms = p.sem("c_ms")

    dy1 = Dstr(p, "c_dy1", 2)
    dy2 = Dstr(p, "c_dy2", 2)
    dh2 = Dstr(p, "c_dh2", 4)
    dxm = Dstr(p, "c_dxm", 1, per=8)
    ddp = Dstr(p, "c_ddp", 1, per=9)
    dwg = Dstr(p, "c_dwg", 2, per=16)
    dhs = Dstr(p, "c_dhs", 4)
    dht2 = Dstr(p, "c_dht2", 2, per=8)
    dwd = Dstr(p, "c_dwd", 6)
    dhl = Dstr(p, "c_dhl", 6)
    dyx = Dstr(p, "c_dyx", 2, per=2)
    dxo = Dstr(p, "c_dxo", 2)
    dmr = Dstr(p, "c_dmr", 1, per=2)
    dgr = Dstr(p, "c_dgr", 1, per=2)
    dms = Dstr(p, "c_dms", 1, per=2)

    def Rc(j, i):
        return sstat[:, j * 128 + i:j * 128 + i + 1]

    def R(j):
        return sstat[:, j * 128:(j + 1) * 128]

    NHB = 9
    NKK = NBLK * NHB * 4

    def kk_parts(kk):
        b = kk // 36
        hb = (kk // 4) % 9
        sub = kk % 4
        return b, hb, sub, b * 4 + sub

    @block.sync
    def _(sync):
        dgr.issue(sync, 0, [
            (sg2[:], bass.AP(d["g2_row"], 0, [[0, 128], [1, E]]), {}),
            (sgh[:], bass.AP(d["ghid_row"], 0, [[0, 128], [1, F]]), {})])
        for i in range(NT):
            if i >= 2:
                sync.wait_ge(s_sq, i - 1)
            dy1.issue(sync, i, [(sy[:, (i % 2) * E:(i % 2) * E + E],
                                 d["ybuf"][i * 128:(i + 1) * 128, :], {})])
        for i in range(NT):
            if i >= 2:
                sync.wait_ge(s_h2, i - 1)
            dy2.issue(sync, i, [(sy2[:, (i % 2) * E:(i % 2) * E + E],
                                 d["ybuf"][i * 128:(i + 1) * 128, :], {})])
        sync.wait_ge(s_cc, 1)
        if NOCC:
            dxm.issue(sync, 0, [(sxm[:, cc * 8:(cc + 1) * 8],
                                 bass.AP(d["cc_xm_in"], cc * 128, [[1, 128], [0, 8]]),
                                 {"allow_slow_non_contiguous": True}) for cc in range(8)])
        else:
            dxm.issue(sync, 0, [(sxm[:, cc * 8:(cc + 1) * 8],
                                 d["cc_xm_out"].transpose([1, 0])[cc * 128:(cc + 1) * 128, :],
                                 {"allow_slow_non_contiguous": True}) for cc in range(8)])
        dms.wait(sync, 0)
        dmr.issue(sync, 0, [
            (smask[:], bass.AP(d["mrow"], 0, [[0, 128], [1, F]]), {}),
            (srms[:, 96:97], bass.AP(d["sclr"], 0, [[0, 128], [1, 1]]), {})])
        for b in range(NBLK):
            for t in range(4):
                dh2.wait(sync, b * 4 + t)
            if b >= 2:
                sync.wait_ge(s_gm, 2 * 36 * (b - 1))
            dht2.issue(sync, b, [
                (sh2t[b % 2][:, cc * 512:(cc + 1) * 512],
                 d["h2buf"][b * 512:(b + 1) * 512, cc * 128:(cc + 1) * 128],
                 {"transpose": True}) for cc in range(8)])
        for i in range(NT):
            b, sub = i // 4, i % 4
            if sub == 0:
                for cc in range(36):
                    sidx = b * 36 + cc
                    if sidx >= 6:
                        sync.wait_ge(s_dc, sidx - 5)
                    dhl.issue(sync, sidx, [
                        (sht[:, (sidx % 6) * 512:(sidx % 6) * 512 + 512],
                         d["hidbuf"][b * 512:(b + 1) * 512, cc * 128:(cc + 1) * 128],
                         {"transpose": True})])
            if i >= 2:
                sync.wait_ge(s_xw, i - 1)
            dyx.issue(sync, i, [
                (syx[:, (i % 2) * E:(i % 2) * E + E],
                 d["ybuf"][i * 128:(i + 1) * 128, :], {}),
                (sxi[:, (i % 2) * E:(i % 2) * E + E],
                 d["x_in"][i * 128:(i + 1) * 128, :], {})])

    @block.gpsimd
    def _(gpsimd):
        ddp.issue(gpsimd, 0, [(swdp1[:, cc * 256:(cc + 1) * 256],
                               d["w_dp1"][cc * 128:(cc + 1) * 128, :], {})
                              for cc in range(8)] +
                  [(swdp2[:], d["w_dp2"].rearrange("(a b) o -> b (a o)", b=128),
                    {"allow_slow_non_contiguous": True})])
        for i in range(NT):
            gpsimd.wait_ge(s_h2, i + 1)
            dh2.issue(gpsimd, i, [(d["h2buf"][i * 128:(i + 1) * 128, :],
                                   sh2[:, (i % 2) * E:(i % 2) * E + E], {})])
        gpsimd.wait_ge(s_xe, 1)
        gpsimd.dma_start(out=d["cc_xm_in"][:], in_=sxo[0:1, 0:E]).then_inc(p.sem("c_xmd"), 16)
        gpsimd.wait_ge(p.sem("c_xmd"), 16)
        if not NOCC:
            gpsimd.collective_compute(
                "AllGather", ALU.bypass, replica_groups=[[0, 1, 2, 3, 4, 5, 6, 7]],
                ins=[d["cc_xm_in"].ap().opt()], outs=[d["cc_xm_out"].ap().opt()],
            ).then_inc(s_cc, 1)
        else:
            gpsimd.nop().then_inc(s_cc, 1)
        gpsimd.wait_ge(s_msk, 1)
        dms.issue(gpsimd, 0, [(d["mrow"][:], srowb[:], {}),
                              (d["sclr"][0:1, 0:1], sdp[0:1, 19:20], {})])
        def issue_dwg(bh):
            b, hb = bh // NHB, bh % NHB
            dwg.issue(gpsimd, bh, [
                (swg[:, (bh % 2) * 8 * 512 + cc * 512:(bh % 2) * 8 * 512 + (cc + 1) * 512],
                 d["w_gate"][cc * 128:(cc + 1) * 128, hb * 512:(hb + 1) * 512], {})
                for cc in range(8)] + [
                (swu[:, (bh % 2) * 8 * 512 + cc * 512:(bh % 2) * 8 * 512 + (cc + 1) * 512],
                 d["w_up"][cc * 128:(cc + 1) * 128, hb * 512:(hb + 1) * 512], {})
                for cc in range(8)])

        issue_dwg(0)
        for kk in range(NKK + 2):
            if 0 <= kk - 2 < NKK:
                k2 = kk - 2
                b, hb, sub, i = kk_parts(k2)
                gpsimd.wait_ge(s_hd, k2 + 1)
                dhs.issue(gpsimd, k2, [
                    (d["hidbuf"][i * 128:(i + 1) * 128, hb * 512:(hb + 1) * 512],
                     shb[:, (k2 % 2) * 512:(k2 % 2) * 512 + 512], {})])
            if kk % 4 == 0 and kk // 4 + 1 < NBLK * NHB and kk < NKK:
                if kk >= 4:
                    gpsimd.wait_ge(s_gm, 2 * 4 * (kk // 4))
                issue_dwg(kk // 4 + 1)
        for i in range(NT):
            b, sub = i // 4, i % 4
            if sub == 0:
                for cc in range(36):
                    sidx = b * 36 + cc
                    if sidx >= 6:
                        gpsimd.wait_ge(s_dc, sidx - 5)
                    dwd.issue(gpsimd, sidx, [
                        (swd[:, (sidx % 6) * E:(sidx % 6) * E + E],
                         d["w_down"][cc * 128:(cc + 1) * 128, :], {})])
            gpsimd.wait_ge(s_xw, i + 1)
            dxo.issue(gpsimd, i, [(d["xcur"][i * 128:(i + 1) * 128, :],
                                   sxo[:, (i % 2) * E:(i % 2) * E + E], {})])
        p.drains(gpsimd, [dh2, dhs, dxo, dms])

    @block.scalar
    def _(scalar):
        scalar.wait_ge(s_ms, 1)
        for i in range(NT):
            dy1.wait(scalar, i)
            scalar.activation(sjunk[:], sy[:, (i % 2) * E:(i % 2) * E + E],
                              AF.Square, accum_out=Rc(0, i)).then_inc(s_sq, 1)
        scalar.wait_ge(s_al, 1)
        scalar.activation(R(2), R(1), AF.Sqrt).then_inc(s_al, 1)
        scalar.wait_ge(s_dp, 1)
        scalar.activation(sdpb[:, 32:36], pxm[0][:, 0:4], AF.Silu)
        scalar.activation(sdpb[:, 36:40], pxm[1][:, 0:4], AF.Silu).then_inc(s_dp, 1)
        scalar.wait_ge(s_dp, 3)
        scalar.activation(sdp[0:1, 8:12], pd[0][0:1, 0:4], AF.Sigmoid).then_inc(s_dp, 1)
        for kk in range(NKK):
            scalar.wait_ge(s_gm, 2 * kk + 2)
            if kk >= 2:
                scalar.wait_ge(s_hd, kk - 1)
            scalar.activation(shid[:, (kk % 2) * 512:(kk % 2) * 512 + 512],
                              pg[kk % 2][:], AF.Silu).then_inc(s_sl, 1)
        scalar.wait_ge(s_rm, 1)
        scalar.activation(srms[:, 32:64], srms[:, 0:32], AF.Sqrt).then_inc(s_rm, 1)

    @block.vector
    def _(vector):
        vector.memset(sstat[:], 0.0)
        vector.memset(ssq[:], 0.0).then_inc(s_ms, 1)
        stv = selfsync(p, vector, "c_ssv")
        vector.wait_ge(s_sq, NT)
        stv(vector.tensor_scalar(R(1), R(0), 1.0 / E, 1e-6, ALU.mult, ALU.add))
        vector.nop().then_inc(s_al, 1)
        vector.wait_ge(s_al, 2)
        stv(vector.tensor_scalar(R(3), R(2), 1e-6, 1e6, ALU.max, ALU.min))
        stv(vector.reciprocal(R(3), R(3)))
        vector.nop().then_inc(s_al, 1)
        dgr.wait(vector, 0)
        for i in range(NT):
            dy2.wait(vector, i)
            vector.wait_ge(s_al, 3)
            if i >= 2:
                dh2.wait(vector, i - 2)
            vector.tensor_scalar_mul(sy2[:, (i % 2) * E:(i % 2) * E + E],
                                     sy2[:, (i % 2) * E:(i % 2) * E + E],
                                     Rc(3, i))
            vector.tensor_tensor(sh2[:, (i % 2) * E:(i % 2) * E + E],
                                 sy2[:, (i % 2) * E:(i % 2) * E + E],
                                 sg2[:], ALU.mult).then_inc(s_h2, 1)
        vector.wait_ge(s_xm, 1)
        stv(vector.tensor_copy(sxo[0:1, 0:512], pxm[0][0:1, :]))
        stv(vector.tensor_copy(sxo[0:1, 512:1024], pxm[1][0:1, :]))
        vector.nop().then_inc(s_xe, 1)
        ddp.wait(vector, 0)
        vector.nop().then_inc(p.sem("c_dpw"), 1)
        dxm.wait(vector, 0)
        stv(vector.tensor_tensor(
            sdp[:, 0:32],
            sxm[:].rearrange("p (c r) -> p c r", r=8)[:, :, 0:7:2].rearrange("p c r -> p (c r)"),
            sxm[:].rearrange("p (c r) -> p c r", r=8)[:, :, 1:8:2].rearrange("p c r -> p (c r)"),
            ALU.add))
        stv(vector.tensor_scalar_mul(sdp[:, 0:32], sdp[:, 0:32], 1.0 / L))
        stv(vector.tensor_copy(sdpb[:, 0:32], sdp[:, 0:32]))
        vector.nop().then_inc(p.sem("c_xmr"), 1)
        vector.wait_ge(s_dp, 4)
        stv(vector.tensor_scalar(sdp[0:1, 12:16], sdp[0:1, 8:12], 0.5, None, ALU.add))
        stv(vector.tensor_scalar(sdp[0:1, 12:16], sdp[0:1, 12:16], 0.5, 1.5, ALU.max, ALU.min))
        stv(vector.reduce_sum(sdp[0:1, 16:17], sdp[0:1, 12:16], axis=AX.X))
        stv(vector.tensor_scalar_mul(sdp[0:1, 16:17], sdp[0:1, 16:17], 0.25 * 3072.0))
        stv(vector.tensor_scalar(sdp[0:1, 17:18], sdp[0:1, 16:17], 8388608.0, 8388608.0,
                                 ALU.add, ALU.subtract))
        stv(vector.tensor_tensor(sdp[0:1, 20:21], sdp[0:1, 17:18], sdp[0:1, 16:17],
                                 ALU.is_gt))
        stv(vector.tensor_tensor(sdp[0:1, 18:19], sdp[0:1, 17:18], sdp[0:1, 20:21],
                                 ALU.subtract))
        stv(vector.tensor_scalar_max(sdp[0:1, 18:19], sdp[0:1, 18:19], 1.0))
        stv(vector.reciprocal(sdp[0:1, 19:20], sdp[0:1, 18:19]))
        stv(vector.tensor_scalar(srow[:], c["iota_row"][:], -1.0, sdp[0:1, 18:19],
                                 ALU.mult, ALU.add))
        stv(vector.tensor_scalar(srow[:], srow[:], 0.0, 1.0, ALU.max, ALU.min))
        stv(vector.tensor_copy(srowb[:], srow[:]))
        vector.nop().then_inc(s_msk, 1)
        for kk in range(NKK):
            b, hb, sub, i = kk_parts(kk)
            vector.wait_ge(s_sl, kk + 1)
            if kk == 0:
                dmr.wait(vector, 0)
            if kk >= 2:
                dhs.wait(vector, kk - 2)
            sl_ = shid[:, (kk % 2) * 512:(kk % 2) * 512 + 512]
            sb_ = shb[:, (kk % 2) * 512:(kk % 2) * 512 + 512]
            vector.tensor_tensor(sl_, sl_, pu[kk % 2][:], ALU.mult)
            vector.tensor_tensor(sl_, sl_, smask[:, hb * 512:(hb + 1) * 512], ALU.mult)
            vector.tensor_tensor(sb_, sl_, sgh[:, hb * 512:(hb + 1) * 512], ALU.mult)
            vector.scalar_tensor_tensor(
                sl_, sl_, 1.0, sl_, ALU.mult, ALU.mult,
                accum_out=ssq[:, i * NHB + hb:i * NHB + hb + 1]).then_inc(s_hd, 1)
        vector.wait_ge(s_hd, NKK)
        stv(vector.tensor_reduce(srms[:, 64:96],
                                 ssq[:].rearrange("p (i h) -> p i h", h=NHB),
                                 axis=AX.X, op=ALU.add))
        dmr.wait(vector, 0)
        stv(vector.tensor_scalar(srms[:, 0:32], srms[:, 64:96], srms[:, 96:97], 1e-6,
                                 ALU.mult, ALU.add))
        vector.nop().then_inc(s_rm, 1)
        vector.wait_ge(s_rm, 2)
        stv(vector.tensor_scalar(srms[:, 32:64], srms[:, 32:64], 1e-6, 1e6, ALU.max, ALU.min))
        stv(vector.reciprocal(srms[:, 32:64], srms[:, 32:64]))
        vector.nop().then_inc(s_rm, 1)
        for i in range(NT):
            dyx.wait(vector, i)
            vector.wait_ge(s_dc, 36 * (i // 4 + 1))
            vector.wait_ge(s_rm, 3)
            if i >= 2:
                dxo.wait(vector, i - 2)
            o = (i % 2) * E
            vector.tensor_tensor(syx[:, o:o + E], syx[:, o:o + E],
                                 sxi[:, o:o + E], ALU.add)
            banks = [pg[0], pg[1], pu[0], pu[1], pxm[0], pxm[1], pd[0], pd[1]]
            vector.scalar_tensor_tensor(sxo[:, o:o + 512], banks[(i % 4) * 2][:],
                                        srms[:, 32 + i:33 + i],
                                        syx[:, o:o + 512], ALU.mult, ALU.add)
            vector.scalar_tensor_tensor(sxo[:, o + 512:o + E], banks[(i % 4) * 2 + 1][:],
                                        srms[:, 32 + i:33 + i],
                                        syx[:, o + 512:o + E], ALU.mult,
                                        ALU.add).then_inc(s_xw, 1)

    @block.tensor
    def _(tensor):
        for i in range(NT):
            tensor.wait_ge(s_h2, i + 1)
            for half in range(2):
                mm = tensor.matmul(
                    pxm[half][0:1, :], c["ones"][:, 0:1],
                    sh2[:, (i % 2) * E + half * 512:(i % 2) * E + half * 512 + 512],
                    start=(i == 0), stop=(i == NT - 1))
            if i == NT - 1:
                mm.then_inc(s_xm, 1)
        tensor.wait_ge(p.sem("c_dpw"), 1)
        tensor.wait_ge(p.sem("c_xmr"), 1)
        tensor.wait_ge(s_xe, 1)
        mm = None
        for half in range(2):
            for cc in range(8):
                mm = tensor.matmul(pxm[half][:, 0:4],
                                   swdp1[:, cc * 256 + half * 128:cc * 256 + half * 128 + 128],
                                   sdpb[:, cc * 4:(cc + 1) * 4],
                                   start=(cc == 0), stop=(cc == 7))
        mm.then_inc(s_dp, 1)
        tensor.wait_ge(s_dp, 2)
        tensor.matmul(pd[0][0:1, 0:4], swdp2[:, 0:1], sdpb[:, 32:36], start=True, stop=False)
        tensor.matmul(pd[0][0:1, 0:4], swdp2[:, 1:2], sdpb[:, 36:40],
                      start=False, stop=True).then_inc(s_dp, 1)
        for kk in range(NKK):
            b, hb, sub, i = kk_parts(kk)
            if sub == 0:
                dwg.wait(tensor, b * NHB + hb)
                if hb == 0:
                    dht2.wait(tensor, b)
            if kk >= 2:
                tensor.wait_ge(s_hd, kk - 1)
            mm = None
            for cc in range(8):
                mm = tensor.matmul(
                    pg[kk % 2][:],
                    sh2t[b % 2][:, cc * 512 + sub * 128:cc * 512 + sub * 128 + 128],
                    swg[:, ((b * NHB + hb) % 2) * 8 * 512 + cc * 512:
                        ((b * NHB + hb) % 2) * 8 * 512 + (cc + 1) * 512],
                    start=(cc == 0), stop=(cc == 7))
            mm.then_inc(s_gm, 1)
            mm = None
            for cc in range(8):
                mm = tensor.matmul(
                    pu[kk % 2][:],
                    sh2t[b % 2][:, cc * 512 + sub * 128:cc * 512 + sub * 128 + 128],
                    swu[:, ((b * NHB + hb) % 2) * 8 * 512 + cc * 512:
                        ((b * NHB + hb) % 2) * 8 * 512 + (cc + 1) * 512],
                    start=(cc == 0), stop=(cc == 7))
            mm.then_inc(s_gm, 1)
        tensor.wait_ge(s_rm, 1)
        banks = [pg[0], pg[1], pu[0], pu[1], pxm[0], pxm[1], pd[0], pd[1]]
        for b in range(NBLK):
            if b >= 1:
                tensor.wait_ge(s_xw, 4 * b)
            mm = None
            for cc in range(36):
                sidx = b * 36 + cc
                dwd.wait(tensor, sidx)
                dhl.wait(tensor, sidx)
                for sub in range(4):
                    for half in range(2):
                        mm = tensor.matmul(
                            banks[sub * 2 + half][:],
                            sht[:, (sidx % 6) * 512 + sub * 128:
                                (sidx % 6) * 512 + sub * 128 + 128],
                            swd[:, (sidx % 6) * E + half * 512:
                                (sidx % 6) * E + half * 512 + 512],
                            start=(cc == 0), stop=(cc == 35))
                mm.then_inc(s_dc, 1)

    emit_ph_c_debug(p, block)
    ces.close()


def emit_ph_c_debug(p, block):
    d, dbg = p.dram, p.dbg
    if not dbg:
        return
    s_dbg = p.sem("s_dbgc")
    n = [0]

    @block.sync
    def _(sync):
        def cp(dst, src):
            sync.dma_start(out=dst, in_=src).then_inc(s_dbg, 16)
            n[0] += 1
        if "h2" in dbg:
            for s in range(4):
                sync.wait_ge(p.sem(f"c_dh2{s}"), 16 * (NT // 4))
            cp(dbg["h2"][:], d["h2buf"][:])
        if "sizes" in dbg:
            sync.wait_ge(p.sem("c_msk"), 1)
            cp(dbg["sizes"][0:1, 0:1], d["sclr"][0:1, 0:1])
        if "y_ad" in dbg:
            for s in range(2):
                sync.wait_ge(p.sem(f"c_dxo{s}"), 16 * (NT // 2))
            cp(dbg["y_ad"][:], d["xcur"][:])
        if n[0]:
            sync.wait_ge(s_dbg, 16 * n[0])



def ph_ffn(p, block, call):
    nc = p.nc
    d = p.dram
    c = p.consts
    NSL = 4 * S
    NMT = NSL // 128
    OPW = E + 2
    RG = [[0, 1, 2, 3, 4, 5, 6, 7]]

    if call == 0:
        for nm, shp, dt_ in [
                ("snraw", [NSL, E], BF16),
                ("cc_xn_in", [NLOC, E], BF16), ("cc_xn_out", [8 * NLOC, E], BF16),
                ("xnb", [L, E], BF16), ("xnbT", [E, L], BF16),
                ("edisp", [NSL, L], BF16),
                ("slotsn", [NSL, E], BF16),
                ("growf", [2, E], F32),
                ("mspadd", [128, 128], BF16), ("msf2", [16, 128], BF16),
                ("ccpadd", [128, 128], BF16), ("ccpair", [128, 128], BF16),
                ("cc_cn2", [128, 128], BF16),
                ("cc_c_in", [64, 128], BF16), ("cc_c_out", [512, 128], BF16),
                ("cc_op_in", [L, OPW], BF16), ("cc_ops", [NLOC, OPW], BF16),
                ("cc_op_out", [8 * NLOC, OPW], BF16),
                ("mbuf", [NLOC, OPW], BF16), ("pbuf", [NLOC, OPW], BF16)]:
            d[nm] = nc.dram_tensor(nm, shp, dt_)
    Pf = f"f{call}_"
    fes = contextlib.ExitStack()
    p.gr = fes.enter_context(nc.gpsimd.register(Pf + "gr"))

    def sb(name, shape, dtype, es=None):
        return (es or fes).enter_context(nc.sbuf_tensor(Pf + name, shape, dtype))

    def sem(name):
        return p.sem(Pf + name)

    def DS(name, n, per=1):
        return Dstr(p, Pf + name, n, per=per)

    smsc = sb("msc", [128, 128], F32)
    sccl = sb("ccl", [128, 128], F32)
    mspadb = sb("mspadb", [128, 128], BF16)
    sfrow = sb("sfrow", [128, NSL], BF16)
    smsrowb = sb("smsrowb", [128, NSL], BF16)
    scmT = sb("scmT", [128, 64], F32)
    scol = sb("scol", [128, 64], F32)
    sgg = sb("sgg", [128, 8], F32)
    sgn32 = sb("sgn32", [128, 8], F32)
    sb1 = sb("sb1", [128, 128], F32)
    strt = sb("strt", [128, 128], BF16)

    s_gio = sem("gio")
    s_tr = sem("tr"); s_te = sem("te"); s_xtr = sem("xtr")
    s_sq = sem("sq"); s_al = sem("al"); s_xn = sem("xn"); s_cc1 = sem("cc1")
    s_xb = sem("xb"); s_gg = sem("gg"); s_init = sem("init")
    s_sn0 = sem("sn0"); s_sn1 = sem("sn1"); s_sn2 = sem("sn2"); s_snr = sem("snr")
    s_sjf = sem("sjf"); s_lg = sem("lg"); s_le = sem("le"); s_mx = sem("mx")
    s_ex = sem("ex")
    s_mt = sem("mt"); s_mk = sem("mk"); s_sf = sem("sf"); s_fc = sem("fc")
    s_rd = sem("rd")
    s_sl = sem("sl"); s_se = sem("se"); s_cs = sem("cs"); s_cl = sem("cl")
    s_ct = sem("ct"); s_cg = sem("cg"); s_cc2 = sem("cc2"); s_sc2 = sem("sc2")
    s_sn3 = sem("sn3"); s_sld = sem("sld")
    s_sf2 = sem("sf2"); s_hm = sem("hm"); s_he = sem("he"); s_em = sem("em")
    s_ee = sem("ee")
    s_cb = sem("cb"); s_oe = sem("oe"); s_cc3 = sem("cc3"); s_f5 = sem("f5")

    xes = contextlib.ExitStack()
    sstat = sb("stat", [128, 8 * 128], F32, xes)
    sjunk = sb("junk", [128, E], F32, xes)
    sx = sb("sx", [128, 2 * E], F32, xes)
    sxn = sb("sxn", [128, 2 * E], BF16, xes)
    sgr = sb("sgr", [1, 2 * E], F32, xes)
    srt = sb("srt", [1, 2 * E], F32, xes)
    dx = DS("dx", 2)
    dxn = DS("dxn", 2)

    @block.vector
    def _(vector):
        vector.memset(sstat[:], 0.0)
        vector.memset(smsc[:], 0.0)
        vector.memset(mspadb[:], 0.0).then_inc(s_init, 1)
        vector.wait_ge(s_gio, 32)
        stg = selfsync(p, vector, Pf + "ssv")
        stg(vector.tensor_tensor(srt[0:1, 0:E], sgr[0:1, 0:E], sgr[0:1, E:2 * E],
                                 ALU.mult))
        stg(vector.tensor_scalar_mul(srt[0:1, 0:E], srt[0:1, 0:E], 1024.0))
        stg(vector.tensor_scalar_mul(srt[0:1, E:2 * E], sgr[0:1, 0:E], 32.0))
        vector.nop().then_inc(s_gg, 1)

    @block.gpsimd
    def _(gpsimd):
        gpsimd.dma_start(out=sgr[0:1, 0:E], in_=d["gn_row"][:]).then_inc(s_gio, 16)
        gpsimd.dma_start(out=sgr[0:1, E:2 * E],
                         in_=d["gs_row"][:]).then_inc(s_gio, 16)
        gpsimd.wait_ge(s_gg, 1)
        gpsimd.dma_start(out=d["growf"][0:1, :],
                         in_=srt[0:1, 0:E]).then_inc(s_gio, 16)
        gpsimd.dma_start(out=d["growf"][1:2, :],
                         in_=srt[0:1, E:2 * E]).then_inc(s_gio, 16)
        gpsimd.wait_ge(s_gio, 64)
        gpsimd.dma_start(out=sgg[:],
                         in_=bass.AP(d["growf"], 0, [[1, 128], [128, 8]]),
                         allow_slow_non_contiguous=True).then_inc(s_gio, 16)
        gpsimd.dma_start(out=sgn32[:],
                         in_=bass.AP(d["growf"], E, [[1, 128], [128, 8]]),
                         allow_slow_non_contiguous=True).then_inc(s_gio, 16)
        gpsimd.wait_ge(s_gio, 96)
        gpsimd.nop().then_inc(s_gg, 1)

    @block.sync
    def _(sync):
        for i in range(NT):
            if i >= 2:
                sync.wait_ge(s_sq, 2 * (i - 1))
            dx.issue(sync, i, [(sx[:, (i % 2) * E:(i % 2) * E + E],
                                d["xcur"][i * 128:(i + 1) * 128, :], {})])
        for i in range(NT):
            if i >= 2:
                sync.wait_ge(s_xn, i - 1)
            dx.issue(sync, NT + i, [(sx[:, (i % 2) * E:(i % 2) * E + E],
                                     d["xcur"][i * 128:(i + 1) * 128, :], {})])

    @block.scalar
    def _(scalar):
        scalar.wait_ge(s_init, 1)
        for i in range(NT):
            dx.wait(scalar, i)
            xt = sx[:, (i % 2) * E:(i % 2) * E + E]
            scalar.activation(sjunk[:], xt, AF.Copy,
                              accum_out=sstat[:, i:i + 1]).then_inc(s_sq, 1)
            scalar.activation(sjunk[:], xt, AF.Square,
                              accum_out=sstat[:, 128 + i:128 + i + 1]).then_inc(s_sq, 1)
        scalar.wait_ge(s_al, 1)
        scalar.activation(sstat[:, 5 * 128:6 * 128], sstat[:, 4 * 128:5 * 128],
                          AF.Sqrt)
        scalar.activation(sstat[:, 6 * 128:7 * 128], sstat[:, 3 * 128:4 * 128],
                          AF.Sqrt).then_inc(s_al, 1)

    @block.vector
    def _(vector):
        R = lambda j: sstat[:, j * 128:(j + 1) * 128]
        stv = selfsync(p, vector, Pf + "ssv")
        vector.wait_ge(s_sq, 2 * NT)
        stv(vector.tensor_scalar_mul(R(2), R(0), 1.0 / E))
        stv(vector.tensor_scalar_mul(R(3), R(1), 1.0 / E))
        stv(vector.tensor_tensor(R(4), R(2), R(2), ALU.mult))
        stv(vector.tensor_tensor(R(3), R(3), R(4), ALU.subtract))
        stv(vector.tensor_scalar_add(R(4), R(3), 1e-5))
        stv(vector.reciprocal(R(7), R(4)))
        stv(vector.tensor_scalar_mul(R(3), R(3), float(E)))
        stv(vector.tensor_tensor(R(3), R(3), R(7), ALU.mult))
        vector.nop().then_inc(s_al, 1)
        vector.wait_ge(s_al, 2)
        stv(vector.reciprocal(R(5), R(5)))
        stv(vector.tensor_scalar_max(R(6), R(6), 1e-12))
        stv(vector.reciprocal(R(6), R(6)))
        stv(vector.tensor_tensor(R(6), R(6), R(5), ALU.mult))
        stv(vector.scalar_tensor_tensor(R(7), R(2), -1.0, R(6), ALU.mult,
                                        ALU.mult))
        vector.nop().then_inc(s_al, 1)
        for i in range(NT):
            vector.wait_ge(s_al, 3)
            dx.wait(vector, NT + i)
            if i >= 2:
                dxn.wait(vector, i - 2)
            vector.tensor_scalar(sxn[:, (i % 2) * E:(i % 2) * E + E],
                                 sx[:, (i % 2) * E:(i % 2) * E + E],
                                 sstat[:, 6 * 128 + i:6 * 128 + i + 1],
                                 sstat[:, 7 * 128 + i:7 * 128 + i + 1],
                                 ALU.mult, ALU.add).then_inc(s_xn, 1)

    @block.gpsimd
    def _(gpsimd):
        for i in range(NT):
            gpsimd.wait_ge(s_xn, i + 1)
            dxn.issue(gpsimd, i, [(d["cc_xn_in"][i * 128:(i + 1) * 128, :],
                                   sxn[:, (i % 2) * E:(i % 2) * E + E], {})])
        p.drains(gpsimd, [dxn])
        if not NOCC:
            gpsimd.collective_compute(
                "AllGather", ALU.bypass, replica_groups=RG,
                ins=[d["cc_xn_in"].ap().opt()], outs=[d["cc_xn_out"].ap().opt()],
            ).then_inc(s_cc1, 1)
        else:
            gpsimd.nop().then_inc(s_cc1, 1)
        gpsimd.wait_ge(s_cc1, 1)
        if NOCC:
            gpsimd.dma_start(out=d["xnb"][0:NLOC, :],
                             in_=d["cc_xn_in"][:]).then_inc(s_xb, 16)
            gpsimd.dma_start(out=d["xnb"][NLOC:L, :],
                             in_=d["cc_xn_in"][:]).then_inc(s_xb, 16)
        else:
            gpsimd.reg_load(p.gr, c["offs"][0:1, 0:1])
            gpsimd.dma_start(out=d["xnb"][:],
                             in_=bass.AP(d["cc_xn_out"], p.gr, [[E, L], [1, E]])
                             ).then_inc(s_xb, 16)
        gpsimd.wait_ge(s_xb, 32 if NOCC else 16)
        gpsimd.nop().then_inc(sem("xbr"), 1)

    if call == 0:
        sse = sb("se_", [128, 2 * E], F32, xes)
        ssn = sb("ssn", [128, 2 * E], BF16, xes)
        sst2 = sb("sst2", [128, 32], F32, xes)
        dse = DS("dse", 2)
        dsn = DS("dsn", 2)

        @block.sync
        def _(sync):
            for i in range(NMT):
                if i >= 2:
                    sync.wait_ge(s_sn0, i - 1)
                dse.issue(sync, i, [(sse[:, (i % 2) * E:(i % 2) * E + E],
                                     d["slot_emb"][i * 128:(i + 1) * 128, :], {})])
            for i in range(NMT):
                if i >= 2:
                    sync.wait_ge(s_sn2, i - 1)
                dse.issue(sync, NMT + i,
                          [(sse[:, (i % 2) * E:(i % 2) * E + E],
                            d["slot_emb"][i * 128:(i + 1) * 128, :], {})])

        @block.scalar
        def _(scalar):
            for i in range(NMT):
                dse.wait(scalar, i)
                scalar.activation(sjunk[:], sse[:, (i % 2) * E:(i % 2) * E + E],
                                  AF.Square,
                                  accum_out=sst2[:, i:i + 1]).then_inc(s_sn0, 1)
            scalar.wait_ge(s_sn1, 1)
            scalar.activation(sst2[:, 16:32], sst2[:, 0:16],
                              AF.Sqrt).then_inc(s_sn1, 1)

        @block.vector
        def _(vector):
            stv = selfsync(p, vector, Pf + "ssv")
            vector.wait_ge(s_sn0, NMT)
            vector.nop().then_inc(s_sn1, 1)
            vector.wait_ge(s_sn1, 2)
            stv(vector.tensor_scalar_max(sst2[:, 16:32], sst2[:, 16:32], 1e-12))
            stv(vector.reciprocal(sst2[:, 16:32], sst2[:, 16:32]))
            vector.nop().then_inc(s_sn1, 1)
            for i in range(NMT):
                vector.wait_ge(s_sn1, 3)
                dse.wait(vector, NMT + i)
                if i >= 2:
                    dsn.wait(vector, i - 2)
                vector.tensor_scalar_mul(ssn[:, (i % 2) * E:(i % 2) * E + E],
                                         sse[:, (i % 2) * E:(i % 2) * E + E],
                                         sst2[:, 16 + i:17 + i]).then_inc(s_sn2, 1)

        @block.gpsimd
        def _(gpsimd):
            for i in range(NMT):
                gpsimd.wait_ge(s_sn2, i + 1)
                dsn.issue(gpsimd, i, [(d["snraw"][i * 128:(i + 1) * 128, :],
                                       ssn[:, (i % 2) * E:(i % 2) * E + E], {})])
            p.drains(gpsimd, [dsn])
            gpsimd.nop().then_inc(s_snr, 1)
    else:
        @block.gpsimd
        def _(gpsimd):
            gpsimd.nop().then_inc(s_snr, 1)

    sxr = sb("sxr", [128, 2 * E], BF16, xes)
    sxtT = sb("sxtT", [128, 8 * 512], BF16, xes)
    ptr = [xes.enter_context(nc.psum_tensor(Pf + f"pt{i}", [128, 512], F32))
           for i in range(2)]
    dxr = DS("dxr", 2)
    dxw = DS("dxw", 2, per=8)

    @block.sync
    def _(sync):
        sync.wait_ge(sem("xbr"), 1)
        for i in range(64):
            if i >= 2:
                sync.wait_ge(s_tr, 8 * (i - 1))
            dxr.issue(sync, i, [(sxr[:, (i % 2) * E:(i % 2) * E + E],
                                 d["xnb"][i * 128:(i + 1) * 128, :], {})])

    @block.tensor
    def _(tensor):
        for i in range(64):
            dxr.wait(tensor, i)
            for dc in range(8):
                g = i * 8 + dc
                if g >= 2:
                    tensor.wait_ge(s_te, g - 1)
                tensor.matmul(ptr[g % 2][:, 0:128],
                              sxr[:, (i % 2) * E + dc * 128:
                                  (i % 2) * E + (dc + 1) * 128],
                              c["identb"][:],
                              start=True, stop=True).then_inc(s_tr, 1)

    @block.scalar
    def _(scalar):
        for i in range(64):
            r = i // 4
            if i % 4 == 0 and r >= 1:
                dxw.wait(scalar, r - 1)
            for dc in range(8):
                g = i * 8 + dc
                scalar.wait_ge(s_tr, g + 1)
                scalar.activation(
                    sxtT[:, dc * 512 + (i % 4) * 128:
                         dc * 512 + (i % 4) * 128 + 128],
                    ptr[g % 2][:, 0:128], AF.Copy).then_inc(s_te, 1)

    @block.gpsimd
    def _(gpsimd):
        for r in range(16):
            gpsimd.wait_ge(s_te, 8 * 4 * (r + 1))
            dxw.issue(gpsimd, r, [
                (d["xnbT"][dc * 128:(dc + 1) * 128, r * 512:(r + 1) * 512],
                 sxtT[:, dc * 512:(dc + 1) * 512], {}) for dc in range(8)])
        p.drains(gpsimd, [dxw])
        gpsimd.nop().then_inc(s_xtr, 1)
    xes.close()

    aes = contextlib.ExitStack()
    ssj = sb("ssj", [128, 2 * 16 * 128], BF16, aes)
    sxt = sb("sxt", [128, 2 * 8 * 1024], BF16, aes)
    slb = sb("slb", [128, 3 * L], F32, aes)
    seb = sb("seb", [128, L], BF16, aes)
    pl = [aes.enter_context(nc.psum_tensor(Pf + f"pl{i}", [128, 512], F32))
          for i in range(4)]
    dsj = DS("dsj", 2, per=16)
    dxt = DS("dxt", 2, per=8)
    des = DS("des", 2)

    def ssj_t(pp, jj, t):
        return ssj[:, ((pp % 2) * 16 + jj * 8 + t) * 128:
                   ((pp % 2) * 16 + jj * 8 + t) * 128 + 128]

    def sxt_t(g2, t):
        return sxt[:, ((g2 % 2) * 8 + t) * 1024:((g2 % 2) * 8 + t) * 1024 + 1024]

    def slb_r(j):
        return slb[:, (j % 3) * L:(j % 3) * L + L]

    @block.sync
    def _(sync):
        sync.wait_ge(s_xtr, 1)
        sync.wait_ge(s_snr, 1)
        sync.dma_start(out=sb1[:], in_=d["b1c"][:]).then_inc(s_gio, 16)
        for pp in range(8):
            if pp >= 2:
                sync.wait_ge(s_lg, 32 * (pp - 1))
            dsj.issue(sync, pp, [
                (ssj_t(pp, jj, t),
                 d["snraw"][(2 * pp + jj) * 128:(2 * pp + jj) * 128 + 128,
                            t * 128:(t + 1) * 128],
                 {"transpose": True}) for jj in range(2) for t in range(8)])
            for np2 in range(8):
                g2 = pp * 8 + np2
                if g2 >= 2:
                    sync.wait_ge(s_lg, 4 * (g2 - 1))
                dxt.issue(sync, g2, [
                    (sxt_t(g2, t),
                     d["xnbT"][t * 128:(t + 1) * 128,
                               np2 * 1024:(np2 + 1) * 1024], {})
                    for t in range(8)])

    @block.vector
    def _(vector):
        def fold(pp):
            dsj.wait(vector, pp)
            vector.wait_ge(s_gg, 2)
            for jj in range(2):
                for t in range(8):
                    vector.tensor_scalar_mul(ssj_t(pp, jj, t), ssj_t(pp, jj, t),
                                             sgg[:, t:t + 1]).then_inc(s_sjf, 1)
        fold(0)
        fold(1)
        for pp in range(8):
            for jj in range(2):
                j = 2 * pp + jj
                vector.wait_ge(s_le, 32 * (pp + 1))
                stv = selfsync(p, vector, Pf + "ssv")
                stv(vector.reduce_max(smsc[:, 48 + j:49 + j], slb_r(j), axis=AX.X))
                stv(vector.tensor_copy(mspadb[:, j:j + 1], smsc[:, 48 + j:49 + j]))
                stv(vector.tensor_copy(smsc[:, 48 + j:49 + j], mspadb[:, j:j + 1]))
                stv(vector.tensor_scalar_mul(smsc[:, j:j + 1],
                                             smsc[:, 48 + j:49 + j], -1.0))
                vector.nop().then_inc(s_mx, 1)
            if pp + 2 < 8:
                fold(pp + 2)

    @block.tensor
    def _(tensor):
        for pp in range(8):
            dsj.wait(tensor, pp)
            tensor.wait_ge(s_sjf, 16 * (pp + 1))
            for nb in range(16):
                g2 = pp * 8 + nb // 2
                if nb % 2 == 0:
                    dxt.wait(tensor, g2)
                for jj in range(2):
                    gg = (pp * 16 + nb) * 2 + jj
                    if gg >= 4:
                        tensor.wait_ge(s_le, gg - 3)
                    bank = pl[(nb % 2) * 2 + jj]
                    mm = None
                    for t in range(8):
                        mm = tensor.matmul(
                            bank[:], ssj_t(pp, jj, t),
                            sxt_t(g2, t)[:, (nb % 2) * 512:(nb % 2) * 512 + 512],
                            start=(t == 0), stop=(t == 7))
                    mm.then_inc(s_lg, 1)

    @block.scalar
    def _(scalar):
        for pp in range(8):
            for nb in range(16):
                g = pp * 16 + nb
                for jj in range(2):
                    gg = g * 2 + jj
                    j = 2 * pp + jj
                    scalar.wait_ge(s_lg, gg + 1)
                    if j >= 3:
                        scalar.wait_ge(s_ex, j - 2)
                    scalar.activation(slb_r(j)[:, nb * 512:(nb + 1) * 512],
                                      pl[(nb % 2) * 2 + jj][:],
                                      AF.Copy).then_inc(s_le, 1)
            for jj in range(2):
                j = 2 * pp + jj
                scalar.wait_ge(s_mx, j + 1)
                if j >= 1:
                    for q in range(4):
                        des.wait(scalar, 4 * (j - 1) + q)
                scalar.activation(seb[:], slb_r(j), AF.Exp,
                                  bias=smsc[:, j:j + 1],
                                  accum_out=smsc[:, 16 + j:17 + j]).then_inc(s_ex, 1)

    @block.gpsimd
    def _(gpsimd):
        for j in range(NMT):
            gpsimd.wait_ge(s_ex, j + 1)
            for q in range(4):
                des.issue(gpsimd, 4 * j + q,
                          [(d["edisp"][j * 128:(j + 1) * 128,
                                       q * 2048:(q + 1) * 2048],
                            seb[:, q * 2048:(q + 1) * 2048], {})])
        p.drains(gpsimd, [des])
        gpsimd.nop().then_inc(s_ex, 1)
        gpsimd.wait_ge(s_mx, NMT)
        gpsimd.dma_start(out=d["mspadd"][:], in_=mspadb[:]).then_inc(s_mt, 16)
    aes.close()

    bes = contextlib.ExitStack()
    sxk = sb("sxk", [128, 2 * 4 * E], BF16, bes)
    set_ = sb("set", [128, 2 * 4 * 2048], BF16, bes)
    acc = sb("acc", [128, NMT * E], F32, bes)
    tmpE = sb("tmpE", [128, 2048], BF16, bes)
    sslt = sb("sslt", [128, 2 * E], BF16, bes)
    ps4 = [bes.enter_context(nc.psum_tensor(Pf + f"ps{i}", [128, 512], F32))
           for i in range(4)]
    dke = DS("dke", 2, per=8)
    dso = DS("dso", 2)

    def set_t(hs, t):
        return set_[:, ((hs % 2) * 4 + t) * 2048:((hs % 2) * 4 + t) * 2048 + 2048]

    def sxk_t(hs, t):
        return sxk[:, ((hs % 2) * 4 + t) * E:((hs % 2) * 4 + t) * E + E]

    @block.sync
    def _(sync):
        sync.wait_ge(s_mt, 16)
        sync.dma_start(out=strt[:], in_=d["mspadd"][:],
                       transpose=True).then_inc(sem("mts"), 16)
        sync.wait_ge(s_ex, 17)
        for hs in range(16):
            if hs >= 2:
                sync.wait_ge(s_sl, 16 * (hs // 2))
                sync.wait_ge(s_cs, 4 * (hs - 1))
            dke.issue(sync, hs, [(sxk_t(hs, t),
                                  d["xnb"][(hs * 4 + t) * 128:(hs * 4 + t + 1) * 128, :],
                                  {}) for t in range(4)] +
                      [(set_t(hs, t),
                        d["edisp"][:, (hs * 4 + t) * 128:(hs * 4 + t + 1) * 128],
                        {"transpose": True}) for t in range(4)])

    @block.gpsimd
    def _(gpsimd):
        gpsimd.wait_ge(sem("mts"), 16)
        gpsimd.dma_start(out=d["msf2"][:], in_=strt[0:16, :]).then_inc(s_mt, 16)
        gpsimd.wait_ge(s_mt, 32)
        gpsimd.dma_start(out=smsrowb[:],
                         in_=bass.AP(d["msf2"], 0, [[0, 128], [1, NSL]])
                         ).then_inc(s_mt, 16)

    @block.vector
    def _(vector):
        vector.wait_ge(s_mt, 48)
        stv = selfsync(p, vector, Pf + "ssv")
        stv(vector.reduce_max(smsc[:, 80:81], smsrowb[:], axis=AX.X))
        stv(vector.tensor_scalar(smsc[:, 81:82], smsc[:, 80:81], -1.0, 40.0,
                                 ALU.mult, ALU.add))
        stv(vector.tensor_scalar_add(smsc[:, 82:83], smsc[:, 80:81], -40.0))
        vector.nop().then_inc(s_mk, 1)
        vector.wait_ge(s_ex, NMT)
        stv(vector.reciprocal(smsc[:, 32:48], smsc[:, 16:32]))
        vector.nop().then_inc(s_rd, 1)

    @block.scalar
    def _(scalar):
        scalar.wait_ge(s_mk, 1)
        scalar.activation(sfrow[:], smsrowb[:], AF.Exp,
                          bias=smsc[:, 81:82]).then_inc(s_sf, 1)
        scalar.activation(smsc[:, 84:100], smsc[:, 48:64], AF.Exp,
                          bias=smsc[:, 81:82]).then_inc(s_fc, 1)

    @block.tensor
    def _(tensor):
        for k in range(8):
            for hs in (2 * k, 2 * k + 1):
                dke.wait(tensor, hs)
            for j in range(NMT):
                gid = k * 16 + j
                if gid >= 2:
                    tensor.wait_ge(s_se, gid - 1)
                for half in range(2):
                    bank = ps4[(gid % 2) * 2 + half]
                    mm = None
                    for hh2 in range(2):
                        hs = 2 * k + hh2
                        for t in range(4):
                            mm = tensor.matmul(
                                bank[:],
                                set_t(hs, t)[:, j * 128:(j + 1) * 128],
                                sxk_t(hs, t)[:, half * 512:half * 512 + 512],
                                start=(hh2 == 0 and t == 0),
                                stop=(hh2 == 1 and t == 3))
                    if half == 1:
                        mm.then_inc(s_sl, 1)

    @block.vector
    def _(vector):
        for k in range(8):
            for j in range(NMT):
                gid = k * 16 + j
                vector.wait_ge(s_sl, gid + 1)
                for half in range(2):
                    dst = acc[:, j * E + half * 512:j * E + half * 512 + 512]
                    bank = ps4[(gid % 2) * 2 + half]
                    if k == 0:
                        ins = vector.tensor_copy(dst, bank[:])
                    else:
                        ins = vector.tensor_tensor(dst, dst, bank[:], ALU.add)
                    if half == 1:
                        ins.then_inc(s_se, 1)
            vector.wait_ge(s_sf, 1)
            for hh2 in range(2):
                hs = 2 * k + hh2
                for t in range(4):
                    ci = hs * 4 + t
                    vector.tensor_tensor(tmpE[:], set_t(hs, t), sfrow[:],
                                         ALU.mult)
                    vector.reduce_max(sccl[:, ci:ci + 1], tmpE[:],
                                      axis=AX.X).then_inc(s_cs, 1)
        vector.wait_ge(s_rd, 1)
        for j in range(NMT):
            if j >= 2:
                dso.wait(vector, j - 2)
            vector.tensor_scalar_mul(sslt[:, (j % 2) * E:(j % 2) * E + E],
                                     acc[:, j * E:(j + 1) * E],
                                     smsc[:, 32 + j:33 + j]).then_inc(s_sn3, 1)
        vector.wait_ge(s_cl, 1)
        stc = selfsync(p, vector, Pf + "ssv")
        stc(vector.tensor_scalar(sccl[:, 64:128], sccl[:, 64:128],
                                 smsc[:, 82:83], None, ALU.add))
        stc(vector.tensor_copy(mspadb[:, 0:64], sccl[:, 64:128]))
        vector.nop().then_inc(sem("ctv"), 1)

    @block.scalar
    def _(scalar):
        scalar.wait_ge(s_cs, 64)
        scalar.activation(sccl[:, 64:128], sccl[:, 0:64],
                          AF.Ln).then_inc(s_cl, 1)

    @block.gpsimd
    def _(gpsimd):
        for j in range(NMT):
            gpsimd.wait_ge(s_sn3, j + 1)
            dso.issue(gpsimd, j, [(d["slotsn"][j * 128:(j + 1) * 128, :],
                                   sslt[:, (j % 2) * E:(j % 2) * E + E], {})])
        p.drains(gpsimd, [dso])
        gpsimd.nop().then_inc(s_sld, 1)
        gpsimd.wait_ge(sem("ctv"), 1)
        gpsimd.dma_start(out=d["ccpadd"][:], in_=mspadb[:]).then_inc(s_ct, 16)

    @block.sync
    def _(sync):
        sync.wait_ge(s_ct, 16)
        sync.dma_start(out=strt[:], in_=d["ccpadd"][:],
                       transpose=True).then_inc(sem("mts"), 16)

    @block.gpsimd
    def _(gpsimd):
        gpsimd.wait_ge(sem("mts"), 32)
        gpsimd.dma_start(out=d["cc_c_in"][:], in_=strt[0:64, :]).then_inc(s_ct, 16)
        gpsimd.dma_start(out=d["cc_cn2"][0:64, :],
                         in_=strt[0:64, :]).then_inc(s_ct, 16)
        gpsimd.dma_start(out=d["cc_cn2"][64:128, :],
                         in_=strt[0:64, :]).then_inc(s_ct, 16)
        gpsimd.wait_ge(s_ct, 64)
        if not NOCC:
            gpsimd.collective_compute(
                "AllGather", ALU.bypass, replica_groups=RG,
                ins=[d["cc_c_in"].ap().opt()], outs=[d["cc_c_out"].ap().opt()],
            ).then_inc(s_cc2, 1)
        else:
            gpsimd.nop().then_inc(s_cc2, 1)
    bes.close()

    les = contextlib.ExitStack()
    seo = sb("seo", [128, NMT * OPW], BF16, les)
    ces = contextlib.ExitStack()
    ssnT = sb("ssnT", [128, 2 * 8 * 512], BF16, ces)
    sw1 = sb("sw1", [128, 2 * 8 * 128], BF16, ces)
    sht = sb("sht", [128, 32 * 512], BF16, ces)
    sw2 = sb("sw2", [128, 2 * 512], BF16, ces)
    sb2 = sb("sb2", [128, E], F32, ces)
    steo = sb("steo", [128, 2 * 512], F32, ces)
    ph_ = [ces.enter_context(nc.psum_tensor(Pf + f"ph{i}", [128, 512], F32))
           for i in range(2)]
    peo = [ces.enter_context(nc.psum_tensor(Pf + f"pe{i}", [128, 512], F32))
           for i in range(4)]
    dsl = DS("dsl", 2, per=8)
    dw1 = DS("dw1", 2, per=8)
    dw2 = DS("dw2", 2)
    db2 = DS("db2", 1)

    def ssnT_t(q, dc):
        return ssnT[:, ((q % 2) * 8 + dc) * 512:((q % 2) * 8 + dc) * 512 + 512]

    def sw1_t(idx, dc):
        return sw1[:, ((idx % 2) * 8 + dc) * 128:((idx % 2) * 8 + dc) * 128 + 128]

    @block.sync
    def _(sync):
        sync.wait_ge(s_sld, 1)
        sync.wait_ge(p.sem("s_wcv", protect=True), 260)
        for q in range(4):
            if q >= 2:
                sync.wait_ge(s_hm, 32 * (q - 1))
            dsl.issue(sync, q, [
                (ssnT_t(q, dc),
                 d["slotsn"][q * 512:(q + 1) * 512, dc * 128:(dc + 1) * 128],
                 {"transpose": True}) for dc in range(8)])
            for ht in range(32):
                idx = q * 32 + ht
                if idx >= 2:
                    sync.wait_ge(s_hm, idx - 1)
                dw1.issue(sync, idx, [
                    (sw1_t(idx, dc),
                     d["w1b"][q * E + dc * 128:q * E + dc * 128 + 128,
                              ht * 128:(ht + 1) * 128], {})
                    for dc in range(8)])
            for half in range(2):
                for hc in range(32):
                    w2i = q * 64 + half * 32 + hc
                    if w2i >= 2:
                        sync.wait_ge(s_em, w2i - 1)
                    dw2.issue(sync, w2i, [
                        (sw2[:, (w2i % 2) * 512:(w2i % 2) * 512 + 512],
                         d["w2b"][q * HID + hc * 128:q * HID + hc * 128 + 128,
                                  half * 512:half * 512 + 512], {})])

    @block.gpsimd
    def _(gpsimd):
        for q in range(4):
            gpsimd.wait_ge(s_ee, 8 * q)
            db2.issue(gpsimd, q, [(sb2[:],
                                   bass.AP(d["b2"], q * E, [[0, 128], [1, E]]), {})])

    @block.vector
    def _(vector):
        vector.memset(seo[:], 0.0)
        vector.wait_ge(s_fc, 1)
        fview = seo.rearrange("p (s w) -> p s w", w=OPW)[:, :, 1024:1025] \
            .rearrange("p s w -> p (s w)")
        vector.tensor_copy(fview, smsc[:, 84:100]).then_inc(sem("eoc"), 1)
        for q in range(4):
            dsl.wait(vector, q)
            vector.wait_ge(s_gg, 2)
            for dc in range(8):
                vector.tensor_scalar_mul(ssnT_t(q, dc), ssnT_t(q, dc),
                                         sgn32[:, dc:dc + 1]).then_inc(s_sf2, 1)
            for half in range(2):
                for m in range(4):
                    sc = q * 4 + m
                    vector.wait_ge(s_em, q * 64 + (half + 1) * 32)
                    db2.wait(vector, q)
                    dst = steo[:, (m % 2) * 512:(m % 2) * 512 + 512]
                    ste = selfsync(p, vector, Pf + "ssv")
                    ste(vector.tensor_tensor(dst, peo[m][:],
                                             sb2[:, half * 512:half * 512 + 512],
                                             ALU.add))
                    vector.tensor_scalar_mul(
                        seo[:, sc * OPW + half * 512:sc * OPW + half * 512 + 512],
                        dst, smsc[:, 84 + sc:85 + sc]).then_inc(s_ee, 1)

    @block.tensor
    def _(tensor):
        for q in range(4):
            dsl.wait(tensor, q)
            tensor.wait_ge(s_sf2, 8 * (q + 1))
            for ht in range(32):
                idx = q * 32 + ht
                dw1.wait(tensor, idx)
                if idx >= 2:
                    tensor.wait_ge(s_he, idx - 1)
                mm = None
                for dc in range(8):
                    mm = tensor.matmul(ph_[idx % 2][:], sw1_t(idx, dc),
                                       ssnT_t(q, dc),
                                       start=(dc == 0), stop=(dc == 7))
                mm.then_inc(s_hm, 1)
            for half in range(2):
                qh = q * 2 + half
                if qh >= 1:
                    tensor.wait_ge(s_ee, 4 * qh)
                for hc in range(32):
                    w2i = q * 64 + half * 32 + hc
                    dw2.wait(tensor, w2i)
                    if hc == 0:
                        tensor.wait_ge(s_he, q * 32 + 32)
                    mm = None
                    for m in range(4):
                        mm = tensor.matmul(
                            peo[m][:],
                            sht[:, hc * 512 + m * 128:hc * 512 + m * 128 + 128],
                            sw2[:, (w2i % 2) * 512:(w2i % 2) * 512 + 512],
                            start=(hc == 0), stop=(hc == 31))
                    mm.then_inc(s_em, 1)

    @block.scalar
    def _(scalar):
        scalar.wait_ge(s_gio, 112)
        for q in range(4):
            for ht in range(32):
                idx = q * 32 + ht
                scalar.wait_ge(s_hm, idx + 1)
                if q >= 1:
                    scalar.wait_ge(s_em, 64 * q)
                scalar.activation(sht[:, ht * 512:(ht + 1) * 512],
                                  ph_[idx % 2][:], AF.Gelu,
                                  bias=sb1[:, idx:idx + 1]).then_inc(s_he, 1)
    ces.close()

    des2 = contextlib.ExitStack()
    sed = sb("sed", [128, 2 * 16 * 512], BF16, des2)
    sop = sb("sop", [128, 2 * OPW], BF16, des2)
    poA = [des2.enter_context(nc.psum_tensor(Pf + f"pa{i}", [128, 512], F32))
           for i in range(2)]
    poB = [des2.enter_context(nc.psum_tensor(Pf + f"pb{i}", [128, 512], F32))
           for i in range(2)]
    poD = [des2.enter_context(nc.psum_tensor(Pf + f"pd{i}", [128, 16], F32))
           for i in range(2)]
    dlt = DS("dlt", 4)
    dop = DS("dop", 2)

    @block.gpsimd
    def _(gpsimd):
        gpsimd.wait_ge(s_cc2, 1)
        if not NOCC:
            gpsimd.reg_load(p.gr, c["offs"][0:1, 2:3])
            gpsimd.dma_start(out=d["ccpair"][:],
                             in_=bass.AP(d["cc_c_out"], p.gr, [[128, 128], [1, 128]])
                             ).then_inc(sem("ccp"), 16)
        else:
            gpsimd.dma_start(out=d["ccpair"][:],
                             in_=d["cc_cn2"][:]).then_inc(sem("ccp"), 16)
        gpsimd.wait_ge(sem("ccp"), 16)
        gpsimd.nop().then_inc(s_cc2, 100)

    @block.sync
    def _(sync):
        sync.wait_ge(s_cc2, 100)
        sync.dma_start(out=strt[:], in_=d["ccpair"][:],
                       transpose=True).then_inc(s_cg, 16)

    @block.vector
    def _(vector):
        vector.wait_ge(s_cg, 16)
        vector.tensor_tensor(scmT[:], strt[:, 0:64], strt[:, 64:128],
                             ALU.max).then_inc(sem("ctv"), 1)

    @block.scalar
    def _(scalar):
        scalar.wait_ge(sem("ctv"), 2)
        scalar.activation(scol[:], scmT[:], AF.Exp, scale=-1.0,
                          bias=smsc[:, 82:83]).then_inc(s_sc2, 1)

    @block.sync
    def _(sync):
        sync.wait_ge(s_em, 256)
        sync.wait_ge(s_ee, 32)
        for mg in range(16):
            for sc in range(16):
                g = mg * 16 + sc
                if g >= 32:
                    sync.wait_ge(s_cb, 16 * 4 * (mg - 1))
                dlt.issue(sync, g, [
                    (sed[:, ((mg % 2) * 16 + sc) * 512:
                         ((mg % 2) * 16 + sc) * 512 + 512],
                     d["edisp"][sc * 128:(sc + 1) * 128,
                                mg * 512:(mg + 1) * 512], {})])

    @block.tensor
    def _(tensor):
        tensor.wait_ge(s_ee, 32)
        tensor.wait_ge(sem("eoc"), 1)
        for m in range(64):
            mg = m // 4
            if m >= 2:
                tensor.wait_ge(s_oe, m - 1)
            for sc in range(16):
                if m % 4 == 0:
                    dlt.wait(tensor, mg * 16 + sc)
                lt = sed[:, ((mg % 2) * 16 + sc) * 512 + (m % 4) * 128:
                         ((mg % 2) * 16 + sc) * 512 + (m % 4) * 128 + 128]
                tensor.matmul(poA[m % 2][:], lt, seo[:, sc * OPW:sc * OPW + 512],
                              start=(sc == 0), stop=(sc == 15))
                tensor.matmul(poB[m % 2][:], lt,
                              seo[:, sc * OPW + 512:sc * OPW + 1024],
                              start=(sc == 0), stop=(sc == 15))
                tensor.matmul(poD[m % 2][:, 0:2], lt,
                              seo[:, sc * OPW + 1024:sc * OPW + 1026],
                              start=(sc == 0), stop=(sc == 15)).then_inc(s_cb, 1)

    @block.scalar
    def _(scalar):
        scalar.wait_ge(s_sc2, 1)
        for m in range(64):
            scalar.wait_ge(s_cb, 16 * (m + 1))
            if m >= 2:
                dop.wait(scalar, m - 2)
            o = (m % 2) * OPW
            scalar.activation(sop[:, o:o + 512], poA[m % 2][:], AF.Copy,
                              scale=scol[:, m:m + 1])
            scalar.activation(sop[:, o + 512:o + 1024], poB[m % 2][:], AF.Copy,
                              scale=scol[:, m:m + 1])
            scalar.activation(sop[:, o + 1024:o + 1026], poD[m % 2][:, 0:2],
                              AF.Copy, scale=scol[:, m:m + 1]).then_inc(s_oe, 1)

    @block.gpsimd
    def _(gpsimd):
        for m in range(64):
            gpsimd.wait_ge(s_oe, m + 1)
            dop.issue(gpsimd, m, [(d["cc_op_in"][m * 128:(m + 1) * 128, :],
                                   sop[:, (m % 2) * OPW:(m % 2) * OPW + OPW], {})])
        p.drains(gpsimd, [dop])
        gpsimd.reg_load(p.gr, c["offs"][0:1, 6:7])
        gpsimd.dma_start(out=d["cc_ops"][:],
                         in_=bass.AP(d["cc_op_in"], p.gr, [[OPW, NLOC], [1, OPW]])
                         ).then_inc(sem("ops"), 16)
        gpsimd.wait_ge(sem("ops"), 16)
        if not NOCC:
            gpsimd.collective_compute(
                "AllGather", ALU.bypass, replica_groups=RG,
                ins=[d["cc_ops"].ap().opt()], outs=[d["cc_op_out"].ap().opt()],
            ).then_inc(s_cc3, 1)
        else:
            gpsimd.nop().then_inc(s_cc3, 1)
        gpsimd.wait_ge(s_cc3, 1)
        gpsimd.reg_load(p.gr, c["offs"][0:1, 4:5])
        gpsimd.dma_start(out=d["mbuf"][:],
                         in_=bass.AP(d["cc_op_in"], p.gr, [[OPW, NLOC], [1, OPW]])
                         ).then_inc(sem("ccp"), 16)
        if NOCC:
            gpsimd.reg_load(p.gr, c["offs"][0:1, 4:5])
            gpsimd.dma_start(out=d["pbuf"][:],
                             in_=bass.AP(d["cc_op_in"], p.gr,
                                         [[OPW, NLOC], [1, OPW]])
                             ).then_inc(sem("ccp"), 16)
        else:
            gpsimd.reg_load(p.gr, c["offs"][0:1, 5:6])
            gpsimd.dma_start(out=d["pbuf"][:],
                             in_=bass.AP(d["cc_op_out"], p.gr,
                                         [[OPW, NLOC], [1, OPW]])
                             ).then_inc(sem("ccp"), 16)
        gpsimd.wait_ge(sem("ccp"), 48)
        gpsimd.nop().then_inc(s_cc3, 100)
    des2.close()
    les.close()

    ees = contextlib.ExitStack()
    smi = sb("smi", [128, 2 * OPW], BF16, ees)
    spa = sb("spa", [128, 2 * OPW], BF16, ees)
    sxc = sb("sxc", [128, 2 * E], F32, ees)
    sf5 = sb("sf5", [128, 2 * OPW], F32, ees)
    dft = DS("dft", 2, per=3)
    dxo = DS("dxo", 2)

    @block.sync
    def _(sync):
        sync.wait_ge(s_cc3, 100)
        for i in range(NT):
            if i >= 2:
                sync.wait_ge(s_f5, i - 1)
            dft.issue(sync, i, [
                (smi[:, (i % 2) * OPW:(i % 2) * OPW + OPW],
                 d["mbuf"][i * 128:(i + 1) * 128, :], {}),
                (spa[:, (i % 2) * OPW:(i % 2) * OPW + OPW],
                 d["pbuf"][i * 128:(i + 1) * 128, :], {}),
                (sxc[:, (i % 2) * E:(i % 2) * E + E],
                 d["xcur"][i * 128:(i + 1) * 128, :], {})])

    @block.vector
    def _(vector):
        for i in range(NT):
            dft.wait(vector, i)
            if i >= 2:
                dxo.wait(vector, i - 2)
            o = (i % 2) * OPW
            t = sf5[:, o:o + OPW]
            stf = selfsync(p, vector, Pf + "ssv")
            stf(vector.tensor_tensor(t, smi[:, o:o + OPW], spa[:, o:o + OPW],
                                     ALU.add))
            stf(vector.reciprocal(t[:, 1024:1025], t[:, 1024:1025]))
            stf(vector.tensor_scalar_mul(t[:, 0:1024], t[:, 0:1024],
                                         t[:, 1024:1025]))
            vector.tensor_tensor(t[:, 0:1024], t[:, 0:1024],
                                 sxc[:, (i % 2) * E:(i % 2) * E + E],
                                 ALU.add).then_inc(s_f5, 1)

    @block.gpsimd
    def _(gpsimd):
        for i in range(NT):
            gpsimd.wait_ge(s_f5, i + 1)
            dxo.issue(gpsimd, i, [(d["xcur"][i * 128:(i + 1) * 128, :],
                                   sf5[:, (i % 2) * OPW:(i % 2) * OPW + 1024], {})])
        p.drains(gpsimd, [dxo])
        gpsimd.nop().then_inc(s_f5, 100)

    for name in ("sync", "vector", "scalar", "tensor"):
        def fin(eng):
            eng.wait_ge(s_f5, NT + 100)
        getattr(block, name)(fin)
    ees.close()

    if call == 0 and DEBUG:
        emit_ffn_debug(p, block)
    fes.close()


def emit_ffn_debug(p, block):
    d = p.dram
    sdb = p.sem("s_fdbg")
    n = [0]

    @block.sync
    def _(sync):
        def cp(dst, src):
            sync.dma_start(out=dst, in_=src).then_inc(sdb, 16)
            n[0] += 1
        if "xn" in p.dbg:
            cp(p.dbg["xn"][:], d["xnb"][0:NLOC, :])
        if "edisp" in p.dbg:
            cp(p.dbg["edisp"][:], d["edisp"][0:128, :])
        if "slots" in p.dbg:
            for ii in range(4):
                cp(p.dbg["slots"][ii * 512:(ii + 1) * 512, :],
                   d["slotsn"][ii * 512:(ii + 1) * 512, :])
        if "outp" in p.dbg:
            for ii in range(4):
                cp(p.dbg["outp"][ii * 2048:(ii + 1) * 2048, :],
                   d["cc_op_in"][ii * 2048:(ii + 1) * 2048, :])
        if "cn" in p.dbg:
            cp(p.dbg["cn"][:], d["ccpair"][:])
        if "mbufd" in p.dbg:
            cp(p.dbg["mbufd"][:], d["mbuf"][:])
        if "pbufd" in p.dbg:
            cp(p.dbg["pbufd"][:], d["pbuf"][:])
        if "dm" in p.dbg:
            cp(p.dbg["dm"][:], d["msf2"][:])
        if "ffn1" in p.dbg:
            cp(p.dbg["ffn1"][:], d["xcur"][:])
        sync.wait_ge(sdb, 16 * n[0])


def ph_stub(p, block):
    d = p.dram
    s_st = p.sem("s_stub", protect=True)

    @block.sync
    def _(sync):
        for i in range(4):
            sync.dma_start(out=d["xcur"][i * 1024:(i + 1) * 1024, :],
                           in_=d["x_in"][i * 1024:(i + 1) * 1024, :]).then_inc(s_st, 16)
        sync.wait_ge(s_st, 64)



def ph_attnb(p, block):
    nc = p.nc
    d = p.dram
    c = p.consts
    RG = [[0, 1, 2, 3, 4, 5, 6, 7]]
    VW = 132
    KQR = 9 * 128

    for nm, shp, dt_ in [
            ("ablnb", [NLOC, E], BF16),
            ("cc_ab_in", [KQR, NLOC], BF16), ("cc_ab_out", [8 * KQR, NLOC], BF16),
            ("ccvloc", [NLOC, VW], BF16), ("ccvgath", [8 * NLOC, VW], BF16),
            ("cc_o_in", [4 * L, DB], BF16), ("cc_o_out", [32 * L, DB], BF16),
            ("abo", [NLOC, E], BF16)]:
        d[nm] = nc.dram_tensor(nm, shp, dt_)

    fes = contextlib.ExitStack()
    p.gr = fes.enter_context(nc.gpsimd.register("ab_gr"))

    def sb(name, shape, dtype, es=None):
        return (es or fes).enter_context(nc.sbuf_tensor("ab_" + name, shape, dtype))

    def sem(name):
        return p.sem("ab_" + name)

    def DS(name, n, per=1):
        return Dstr(p, "ab_" + name, n, per=per)

    swk = sb("swk", [128, 8 * 128], BF16)
    swv = sb("swv", [128, 8 * 128], BF16)
    swq = sb("swq", [128, 64 * 128], BF16)
    swp = sb("swp", [128, 8 * E], BF16)
    sbp = sb("sbp", [128, E], F32)

    s_sq = sem("sq"); s_al = sem("al"); s_xn = sem("xn"); s_xs = sem("xs")
    s_wl = sem("wl")
    s_km = sem("km"); s_ke = sem("ke"); s_vm = sem("vm"); s_ve = sem("ve")
    s_qm = sem("qm"); s_qe = sem("qe"); s_kl = sem("kl"); s_kq = sem("kq")
    s_cc1 = sem("cc1")
    s_sc = sem("sc"); s_pe = sem("pe"); s_pr = sem("pr"); s_am = sem("am")
    s_od = sem("od")
    s_cc3 = sem("cc3"); s_ab2 = sem("ab2"); s_ab2r = sem("ab2r")
    s_pm2 = sem("pm2"); s_pe2 = sem("pe2"); s_fin = sem("fin")

    sched = [(b2, Q, kc) for b2 in range(4) for Q in range(16)
             for kc in range(4 * Q + 4)]
    gidx_of = {t: i for i, t in enumerate(sched)}
    pr_of = {}
    npr = 0
    for t in sched:
        b2, Q, kc = t
        if kc >= 4 * Q:
            npr += 1
        pr_of[t] = npr

    xes = contextlib.ExitStack()
    sstat = sb("stat", [128, 8 * 128], F32, xes)
    sjunk = sb("junk", [128, E], F32, xes)
    sx = sb("sx", [128, 2 * E], F32, xes)
    sxn = sb("sxn", [128, 2 * E], BF16, xes)
    dx = DS("dx", 2)
    dxn = DS("dxn", 2)

    @block.sync
    def _(sync):
        for i in range(NT):
            if i >= 2:
                sync.wait_ge(s_sq, 2 * (i - 1))
            dx.issue(sync, i, [(sx[:, (i % 2) * E:(i % 2) * E + E],
                                d["xcur"][i * 128:(i + 1) * 128, :], {})])
        for i in range(NT):
            if i >= 2:
                sync.wait_ge(s_xn, i - 1)
            dx.issue(sync, NT + i, [(sx[:, (i % 2) * E:(i % 2) * E + E],
                                     d["xcur"][i * 128:(i + 1) * 128, :], {})])

    @block.gpsimd
    def _(gpsimd):
        for dc in range(8):
            gpsimd.dma_start(out=swk[:, dc * 128:(dc + 1) * 128],
                             in_=d["wk"][dc * 128:(dc + 1) * 128, :]
                             ).then_inc(s_wl, 16)
            gpsimd.dma_start(out=swv[:, dc * 128:(dc + 1) * 128],
                             in_=d["wv"][dc * 128:(dc + 1) * 128, :]
                             ).then_inc(s_wl, 16)
        for h in range(8):
            for dc in range(8):
                gpsimd.dma_start(
                    out=swq[:, (h * 8 + dc) * 128:(h * 8 + dc) * 128 + 128],
                    in_=d["wqf"][h * E + dc * 128:h * E + dc * 128 + 128, :]
                    ).then_inc(s_wl, 16)
        for dc in range(8):
            gpsimd.dma_start(out=swp[:, dc * E:(dc + 1) * E],
                             in_=d["wp"][dc * 128:(dc + 1) * 128, :]
                             ).then_inc(s_wl, 16)
        gpsimd.dma_start(out=sbp[:],
                         in_=bass.AP(d["bp_row"], 0, [[0, 128], [1, E]])
                         ).then_inc(s_wl, 16)
        for i in range(NT):
            gpsimd.wait_ge(s_xn, i + 1)
            dxn.issue(gpsimd, i, [(d["ablnb"][i * 128:(i + 1) * 128, :],
                                   sxn[:, (i % 2) * E:(i % 2) * E + E], {})])
        p.drains(gpsimd, [dxn])
        gpsimd.nop().then_inc(s_xs, 1)

    @block.scalar
    def _(scalar):
        for i in range(NT):
            dx.wait(scalar, i)
            xt = sx[:, (i % 2) * E:(i % 2) * E + E]
            scalar.activation(sjunk[:], xt, AF.Copy,
                              accum_out=sstat[:, i:i + 1]).then_inc(s_sq, 1)
            scalar.activation(sjunk[:], xt, AF.Square,
                              accum_out=sstat[:, 128 + i:128 + i + 1]).then_inc(s_sq, 1)
        scalar.wait_ge(s_al, 1)
        scalar.activation(sstat[:, 5 * 128:6 * 128], sstat[:, 4 * 128:5 * 128],
                          AF.Sqrt).then_inc(s_al, 1)

    @block.vector
    def _(vector):
        vector.memset(sstat[:], 0.0).then_inc(sem("init"), 1)
        R = lambda j: sstat[:, j * 128:(j + 1) * 128]
        stv = selfsync(p, vector, "ab_ssx")
        vector.wait_ge(s_sq, 2 * NT)
        stv(vector.tensor_scalar_mul(R(2), R(0), 1.0 / E))
        stv(vector.tensor_scalar_mul(R(3), R(1), 1.0 / E))
        stv(vector.tensor_tensor(R(4), R(2), R(2), ALU.mult))
        stv(vector.tensor_tensor(R(3), R(3), R(4), ALU.subtract))
        stv(vector.tensor_scalar_add(R(4), R(3), 1e-5))
        vector.nop().then_inc(s_al, 1)
        vector.wait_ge(s_al, 2)
        stv(vector.reciprocal(R(6), R(5)))
        stv(vector.scalar_tensor_tensor(R(7), R(2), -1.0, R(6), ALU.mult,
                                        ALU.mult))
        vector.nop().then_inc(s_al, 1)
        for i in range(NT):
            vector.wait_ge(s_al, 3)
            dx.wait(vector, NT + i)
            if i >= 2:
                dxn.wait(vector, i - 2)
            vector.tensor_scalar(sxn[:, (i % 2) * E:(i % 2) * E + E],
                                 sx[:, (i % 2) * E:(i % 2) * E + E],
                                 sstat[:, 6 * 128 + i:6 * 128 + i + 1],
                                 sstat[:, 7 * 128 + i:7 * 128 + i + 1],
                                 ALU.mult, ALU.add).then_inc(s_xn, 1)
    xes.close()

    pes = contextlib.ExitStack()
    sxt = sb("sxt", [128, 2 * 8 * 512], BF16, pes)
    skq = sb("skq", [128, 2 * 512], BF16, pes)
    sVb = sb("sVb", [128, 4 * VW], BF16, pes)
    pk = [pes.enter_context(nc.psum_tensor(f"ab_pk{i}", [128, 512], F32))
          for i in range(2)]
    pv = [pes.enter_context(nc.psum_tensor(f"ab_pv{i}", [128, 512], F32))
          for i in range(2)]
    pq = [pes.enter_context(nc.psum_tensor(f"ab_pq{i}", [128, 512], F32))
          for i in range(2)]
    dxt = DS("dxt", 2, per=8)
    dkv = DS("dkv", 4)
    dvv = DS("dvv", 4)

    def sxt_t(tb, dc):
        return sxt[:, ((tb % 2) * 8 + dc) * 512:((tb % 2) * 8 + dc) * 512 + 512]

    @block.sync
    def _(sync):
        sync.wait_ge(s_xs, 1)
        for tb in range(8):
            if tb >= 2:
                sync.wait_ge(s_qm, 8 * (tb - 1))
            dxt.issue(sync, tb, [
                (sxt_t(tb, dc),
                 d["ablnb"][tb * 512:(tb + 1) * 512, dc * 128:(dc + 1) * 128],
                 {"transpose": True}) for dc in range(8)])

    @block.tensor
    def _(tensor):
        tensor.wait_ge(s_wl, 16 * (16 + 64 + 8 + 1))
        for tb in range(8):
            dxt.wait(tensor, tb)
            if tb >= 2:
                tensor.wait_ge(s_ke, tb - 1)
            mm = None
            for dc in range(8):
                mm = tensor.matmul(pk[tb % 2][:], swk[:, dc * 128:(dc + 1) * 128],
                                   sxt_t(tb, dc), start=(dc == 0), stop=(dc == 7))
            mm.then_inc(s_km, 1)
            for sub in range(4):
                vg = tb * 4 + sub
                if vg >= 2:
                    tensor.wait_ge(s_ve, vg - 1)
                mm = None
                for dc in range(8):
                    mm = tensor.matmul(
                        pv[vg % 2][:, 0:128],
                        sxt_t(tb, dc)[:, sub * 128:sub * 128 + 128],
                        swv[:, dc * 128:(dc + 1) * 128],
                        start=(dc == 0), stop=(dc == 7))
                mm.then_inc(s_vm, 1)
            for h in range(8):
                qg = tb * 8 + h
                if qg >= 2:
                    tensor.wait_ge(s_qe, qg - 1)
                mm = None
                for dc in range(8):
                    mm = tensor.matmul(
                        pq[qg % 2][:],
                        swq[:, (h * 8 + dc) * 128:(h * 8 + dc) * 128 + 128],
                        sxt_t(tb, dc), start=(dc == 0), stop=(dc == 7))
                mm.then_inc(s_qm, 1)

    @block.scalar
    def _(scalar):
        for tb in range(8):
            scalar.wait_ge(s_km, tb + 1)
            if tb >= 1:
                dkv.wait(scalar, 9 * (tb - 1))
            scalar.activation(skq[:, 0:512], pk[tb % 2][:],
                              AF.Copy).then_inc(s_ke, 1)
            for sub in range(4):
                vg = tb * 4 + sub
                scalar.wait_ge(s_vm, vg + 1)
                scalar.wait_ge(sem("vz"), 1)
                if vg >= 4:
                    dvv.wait(scalar, vg - 4)
                scalar.activation(sVb[:, (vg % 4) * VW:(vg % 4) * VW + 128],
                                  pv[vg % 2][:, 0:128], AF.Copy).then_inc(s_ve, 1)
            for h in range(8):
                qg = tb * 8 + h
                scalar.wait_ge(s_qm, qg + 1)
                prev = 9 * tb + h if h >= 1 else 9 * tb - 1
                if qg >= 1:
                    dkv.wait(scalar, prev)
                scalar.activation(skq[:, 512:1024], pq[qg % 2][:],
                                  AF.Copy).then_inc(s_qe, 1)

    @block.vector
    def _(vector):
        vector.wait_ge(s_xs, 1)
        for sl in range(4):
            vector.memset(sVb[:, sl * VW + 128:sl * VW + 129], 1.0)
            vector.memset(sVb[:, sl * VW + 129:sl * VW + VW], 0.0)
        vector.nop().then_inc(sem("vz"), 1)

    @block.gpsimd
    def _(gpsimd):
        for tb in range(8):
            gpsimd.wait_ge(s_ke, tb + 1)
            dkv.issue(gpsimd, 9 * tb,
                      [(d["cc_ab_in"][0:128, tb * 512:(tb + 1) * 512],
                        skq[:, 0:512], {})])
            for sub in range(4):
                vg = tb * 4 + sub
                gpsimd.wait_ge(s_ve, vg + 1)
                dvv.issue(gpsimd, vg,
                          [(d["ccvloc"][vg * 128:(vg + 1) * 128, :],
                            sVb[:, (vg % 4) * VW:(vg % 4) * VW + VW], {})])
            for h in range(8):
                qg = tb * 8 + h
                gpsimd.wait_ge(s_qe, qg + 1)
                dkv.issue(gpsimd, 9 * tb + h + 1,
                          [(d["cc_ab_in"][128 + h * 128:256 + h * 128,
                                          tb * 512:(tb + 1) * 512],
                            skq[:, 512:1024], {})])
        p.drains(gpsimd, [dkv, dvv])
        if not NOCC and (CCMASK & 1):
            gpsimd.collective_compute(
                "AllGather", ALU.bypass, replica_groups=RG,
                ins=[d["cc_ab_in"].ap().opt()], outs=[d["cc_ab_out"].ap().opt()],
            ).then_inc(s_cc1, 1)
        else:
            gpsimd.nop().then_inc(s_cc1, 1)
        gpsimd.wait_ge(s_cc1, 1)
        if not NOCC and (CCMASK & 2):
            gpsimd.collective_compute(
                "AllGather", ALU.bypass, replica_groups=RG,
                ins=[d["ccvloc"].ap().opt()], outs=[d["ccvgath"].ap().opt()],
            ).then_inc(s_cc1, 1)
        else:
            gpsimd.nop().then_inc(s_cc1, 1)
    pes.close()

    aes = contextlib.ExitStack()
    skt = sb("skt", [128, L], BF16, aes)
    sqt = sb("sqt", [128, L], BF16, aes)
    sv = sb("sv", [128, 6 * VW], BF16, aes)
    spb = sb("spb", [128, 2 * 512], BF16, aes)
    soA = sb("soA", [128, 4 * 128], BF16, aes)
    srcp = sb("srcp", [128, 8], F32, aes)
    psc = [aes.enter_context(nc.psum_tensor(f"ab_ps{i}", [128, 512], F32))
           for i in range(2)]
    pav = [aes.enter_context(nc.psum_tensor(f"ab_pa{i}", [128, VW], F32))
           for i in range(4)]
    dsv = DS("dsv", 6)
    dso2 = DS("dso2", 4, per=4)

    @block.sync
    def _(sync):
        sync.wait_ge(s_cc1, 2)
        for b2 in range(4):
            if b2 >= 1:
                sync.wait_ge(s_sc, gidx_of[(b2 - 1, 15, 63)] + 1)
            if NOCC or (ABN & 1):
                for pair in [(skt[:, 0:NLOC], d["cc_ab_in"][0:128, :]),
                             (skt[:, NLOC:L], d["cc_ab_in"][0:128, :]),
                             (sqt[:, 0:NLOC], d["cc_ab_in"][128:256, :]),
                             (sqt[:, NLOC:L], d["cc_ab_in"][128:256, :])]:
                    sync.dma_start(out=pair[0], in_=pair[1]).then_inc(s_kl, 16)
            else:
                for hf in range(2):
                    r = 2 * b2 + hf
                    sync.dma_start(
                        out=skt[:, hf * NLOC:(hf + 1) * NLOC],
                        in_=d["cc_ab_out"][r * KQR:r * KQR + 128, :]
                        ).then_inc(s_kl, 16)
            for Q in range(16):
                for kc in range(4 * Q + 4):
                    gi = gidx_of[(b2, Q, kc)]
                    if gi >= 6:
                        sync.wait_ge(s_am, gi - 5)
                    kb = kc % 32
                    kh = kc // 32
                    if NOCC or (ABN & 2):
                        src = d["ccvloc"][kb * 128:(kb + 1) * 128, :]
                    else:
                        src = d["ccvgath"][(2 * b2 + kh) * NLOC + kb * 128:
                                            (2 * b2 + kh) * NLOC + (kb + 1) * 128, :]
                    dsv.issue(sync, gi, [(sv[:, (gi % 6) * VW:(gi % 6) * VW + VW],
                                          src, {})])

    @block.gpsimd
    def _(gpsimd):
        gpsimd.wait_ge(s_cc1, 2)
        for b2 in range(4):
            if not (NOCC or (ABN & 1)):
                if b2 >= 1:
                    gpsimd.wait_ge(s_sc, gidx_of[(b2 - 1, 15, 63)] + 1)
                gpsimd.reg_load(p.gr, c["offs"][0:1, 8 + b2:9 + b2])
                gpsimd.dma_start(
                    out=sqt[:],
                    in_=bass.AP(d["cc_ab_out"], p.gr,
                                [[NLOC, 128], [KQR * NLOC, 2], [1, NLOC]])
                    ).then_inc(s_kq, 16)
            for Q in range(16):
                qg = b2 * 16 + Q
                gpsimd.wait_ge(s_od, 4 * (qg + 1))
                dso2.issue(gpsimd, qg, [
                    (d["cc_o_in"][b2 * L + Q * 512 + sub * 128:
                                  b2 * L + Q * 512 + (sub + 1) * 128, :],
                     soA[:, sub * 128:(sub + 1) * 128], {}) for sub in range(4)])
        p.drains(gpsimd, [dso2])
        if not NOCC and (CCMASK & 4):
            gpsimd.collective_compute(
                "AllGather", ALU.bypass, replica_groups=RG,
                ins=[d["cc_o_in"].ap().opt()], outs=[d["cc_o_out"].ap().opt()],
            ).then_inc(s_cc3, 1)
        else:
            gpsimd.nop().then_inc(s_cc3, 1)
        gpsimd.wait_ge(s_cc3, 1)
        if NOCC or (ABN & 4):
            for h in range(8):
                gpsimd.dma_start(
                    out=bass.AP(d["abo"], h * DB, [[E, NLOC], [1, DB]]),
                    in_=d["cc_o_in"][0:NLOC, :]).then_inc(s_ab2, 16)
            gpsimd.wait_ge(s_ab2, 16 * 8)
        else:
            for h in range(8):
                gpsimd.reg_load(p.gr, c["offs"][0:1, 16 + h:17 + h])
                gpsimd.dma_start(
                    out=bass.AP(d["abo"], h * DB, [[E, NLOC], [1, DB]]),
                    in_=bass.AP(d["cc_o_out"], p.gr, [[DB, NLOC], [1, DB]])
                    ).then_inc(s_ab2, 16)
            gpsimd.wait_ge(s_ab2, 128)
        gpsimd.nop().then_inc(s_ab2r, 1)

    @block.tensor
    def _(tensor):
        for b2 in range(4):
            tensor.wait_ge(s_kl, (64 if (NOCC or (ABN & 1)) else 32) * (b2 + 1))
            if not (NOCC or (ABN & 1)):
                tensor.wait_ge(s_kq, 16 * (b2 + 1))
            for Q in range(16):
                qg = b2 * 16 + Q
                nk = 4 * Q + 4
                if qg >= 1:
                    tensor.wait_ge(s_od, 4 * qg)
                for step in range(nk + 1):
                    if step < nk:
                        kc = step
                        gi = gidx_of[(b2, Q, kc)]
                        if gi >= 2:
                            tensor.wait_ge(s_pe, gi - 1)
                        tensor.matmul(psc[kc % 2][:],
                                      skt[:, kc * 128:(kc + 1) * 128],
                                      sqt[:, Q * 512:(Q + 1) * 512],
                                      start=True, stop=True).then_inc(s_sc, 1)
                    if step >= 1:
                        kc = step - 1
                        gi = gidx_of[(b2, Q, kc)]
                        dsv.wait(tensor, gi)
                        if kc >= 4 * Q:
                            tensor.wait_ge(s_pr, pr_of[(b2, Q, kc)])
                        else:
                            tensor.wait_ge(s_pe, gi + 1)
                        mm = None
                        for sub in range(4):
                            mm = tensor.matmul(
                                pav[sub][:],
                                spb[:, (kc % 2) * 512 + sub * 128:
                                    (kc % 2) * 512 + (sub + 1) * 128],
                                sv[:, (gi % 6) * VW:(gi % 6) * VW + VW],
                                start=(kc == 0), stop=(kc == nk - 1))
                        mm.then_inc(s_am, 1)

    @block.scalar
    def _(scalar):
        for b2 in range(4):
            for Q in range(16):
                for kc in range(4 * Q + 4):
                    gi = gidx_of[(b2, Q, kc)]
                    scalar.wait_ge(s_sc, gi + 1)
                    if gi >= 2:
                        scalar.wait_ge(s_am, gi - 1)
                    scalar.activation(spb[:, (kc % 2) * 512:(kc % 2) * 512 + 512],
                                      psc[kc % 2][:], AF.Exp,
                                      scale=1.0 / 32.0).then_inc(s_pe, 1)

    @block.vector
    def _(vector):
        for b2 in range(4):
            for Q in range(16):
                qg = b2 * 16 + Q
                nk = 4 * Q + 4
                for kc in range(4 * Q, nk):
                    o = kc - 4 * Q
                    gi = gidx_of[(b2, Q, kc)]
                    vector.wait_ge(s_pe, gi + 1)
                    vector.tensor_tensor(
                        spb[:, (kc % 2) * 512:(kc % 2) * 512 + 512],
                        spb[:, (kc % 2) * 512:(kc % 2) * 512 + 512],
                        c["diagmask"][:, o * 512:(o + 1) * 512],
                        ALU.mult).then_inc(s_pr, 1)
                vector.wait_ge(s_am, gidx_of[(b2, Q, nk - 1)] + 1)
                if qg >= 1:
                    dso2.wait(vector, qg - 1)
                stq = selfsync(p, vector, "ab_ssq")
                for sub in range(4):
                    stq(vector.reciprocal(srcp[:, sub:sub + 1],
                                          pav[sub][:, 128:129]))
                    vector.tensor_scalar_mul(soA[:, sub * 128:(sub + 1) * 128],
                                             pav[sub][:, 0:128],
                                             srcp[:, sub:sub + 1]).then_inc(s_od, 1)
    aes.close()

    oes = contextlib.ExitStack()
    sot = sb("sot", [128, 2 * 8 * 512], BF16, oes)
    sxc = sb("sxc", [128, 2 * E], F32, oes)
    sfo = sb("sfo", [128, 2 * E], F32, oes)
    pop = [oes.enter_context(nc.psum_tensor(f"ab_po{i}", [128, 512], F32))
           for i in range(4)]
    dot = DS("dot", 2, per=8)
    dxc3 = DS("dxc3", 2)
    dxo2 = DS("dxo2", 2)

    def sot_t(tb, dc):
        return sot[:, ((tb % 2) * 8 + dc) * 512:((tb % 2) * 8 + dc) * 512 + 512]

    @block.sync
    def _(sync):
        sync.wait_ge(s_ab2r, 1)
        for i in range(NT):
            if i % 4 == 0:
                tb = i // 4
                if tb >= 2:
                    sync.wait_ge(s_pm2, 8 * (tb - 1))
                dot.issue(sync, tb, [
                    (sot_t(tb, dc),
                     d["abo"][tb * 512:(tb + 1) * 512, dc * 128:(dc + 1) * 128],
                     {"transpose": True}) for dc in range(8)])
            if i >= 2:
                sync.wait_ge(s_fin, i - 1)
            dxc3.issue(sync, i, [(sxc[:, (i % 2) * E:(i % 2) * E + E],
                                  d["xcur"][i * 128:(i + 1) * 128, :], {})])

    @block.tensor
    def _(tensor):
        for tb in range(8):
            dot.wait(tensor, tb)
            for sub in range(4):
                i = tb * 4 + sub
                for half in range(2):
                    og = i * 2 + half
                    if og >= 4:
                        tensor.wait_ge(s_pe2, og - 3)
                    mm = None
                    for dc in range(8):
                        mm = tensor.matmul(
                            pop[og % 4][:],
                            sot_t(tb, dc)[:, sub * 128:sub * 128 + 128],
                            swp[:, dc * E + half * 512:dc * E + half * 512 + 512],
                            start=(dc == 0), stop=(dc == 7))
                    mm.then_inc(s_pm2, 1)

    @block.vector
    def _(vector):
        for i in range(NT):
            dxc3.wait(vector, i)
            if i >= 2:
                dxo2.wait(vector, i - 2)
            sto = selfsync(p, vector, "ab_sso")
            for half in range(2):
                og = i * 2 + half
                vector.wait_ge(s_pm2, og + 1)
                t = sfo[:, (i % 2) * E + half * 512:(i % 2) * E + half * 512 + 512]
                sto(vector.tensor_tensor(t, pop[og % 4][:],
                                         sbp[:, half * 512:half * 512 + 512],
                                         ALU.add))
                sto(vector.tensor_tensor(
                    t, t, sxc[:, (i % 2) * E + half * 512:
                              (i % 2) * E + half * 512 + 512],
                    ALU.add))
                vector.nop().then_inc(s_pe2, 1)
            vector.nop().then_inc(s_fin, 1)

    @block.gpsimd
    def _(gpsimd):
        for i in range(NT):
            gpsimd.wait_ge(s_fin, i + 1)
            dxo2.issue(gpsimd, i, [(d["xcur"][i * 128:(i + 1) * 128, :],
                                    sfo[:, (i % 2) * E:(i % 2) * E + E], {})])
        p.drains(gpsimd, [dxo2])
        gpsimd.nop().then_inc(s_fin, 100)

    for name in ("sync", "vector", "scalar", "tensor"):
        def fin2(eng):
            eng.wait_ge(s_fin, NT + 100)
        getattr(block, name)(fin2)
    oes.close()

    if DEBUG:
        sdb = p.sem("ab_dbg")
        n = [0]

        @block.sync
        def _(sync):
            def cp(dst, src):
                sync.dma_start(out=dst, in_=src).then_inc(sdb, 16)
                n[0] += 1
            if "xlnb" in p.dbg:
                cp(p.dbg["xlnb"][:], d["ablnb"][:])
            if "ob" in p.dbg:
                for ii in range(8):
                    cp(p.dbg["ob"][ii * NLOC:(ii + 1) * NLOC, :],
                       d["cc_o_in"][ii * NLOC:(ii + 1) * NLOC, :])
            if "attnb" in p.dbg:
                cp(p.dbg["attnb"][:], d["xcur"][:])
            sync.wait_ge(sdb, 16 * n[0])
    fes.close()



def ph_finish(p, block):
    nc = p.nc
    d = p.dram
    s_out = p.sem("s_out")
    srct = d["xcur"] if (N_PHASES >= 3 or MK_ONLY) else d["ybuf"]

    @block.sync
    def _(sync):
        for i in range(4):
            sync.dma_start(out=d["x_out"][i * 1024:(i + 1) * 1024, :],
                           in_=srct[i * 1024:(i + 1) * 1024, :]).then_inc(s_out, 16)
        sync.wait_ge(s_out, 16 * 4)



def _shard_inputs(inputs):
    x = np.asarray(inputs["L0_attn_a_x"])
    g1 = np.asarray(inputs["L0_attn_a_g1"])
    g2 = np.asarray(inputs["L0_attn_a_g2"])
    gn = np.asarray(inputs["L0_ffn_norm_gamma"])
    gs = np.asarray(inputs["L0_ffn_slot_norm_gamma"])
    ghid = np.asarray(inputs["L0_attn_a_g_hidden"])

    def cols(v, nch):
        return np.ascontiguousarray(v.reshape(nch, 128).T.astype(np.float32))

    pos_all = np.arange(L, dtype=np.float64)
    inv_freq = 1.0 / (10000.0 ** (np.arange(0, DH, 2, dtype=np.float64) / DH))
    emb = np.concatenate([pos_all[:, None] * inv_freq[None, :]] * 2, axis=1)
    cos_t = np.cos(emb).astype(np.float32)
    sin_t = np.sin(emb).astype(np.float32)
    sin_sign = sin_t.copy()
    sin_sign[:, 0::2] *= -1.0
    cos_rep_all = np.tile(cos_t, (1, H)).astype(np.float32)
    sin_rep_all = np.tile(sin_sign, (1, H)).astype(np.float32)

    tril = np.triu(np.ones((128, 128), np.float32)).astype(bf16)
    dm = np.zeros((4, 128, 512), np.float32)
    kk = np.arange(128)[:, None]
    qq = np.arange(512)[None, :]
    for o in range(4):
        dm[o] = (o * 128 + kk <= qq)
    diagmask = dm.reshape(512, 512).astype(bf16)
    ones_b = np.ones((128, 128), np.float32).astype(bf16)
    onesf = np.ones((128, 8), np.float32)
    iota_row = np.arange(F, dtype=np.float32).reshape(1, F)
    ident = np.eye(128, dtype=np.float32)

    base = {
        "w_qkv": inputs["L0_attn_a_w_qkv"], "w_out": inputs["L0_attn_a_w_out"],
        "g1c": cols(g1, 8), "g2c": cols(g2, 8),
        "w_dp1": inputs["L0_attn_a_w_dp1"], "w_dp2": inputs["L0_attn_a_w_dp2"],
        "w_gate": inputs["L0_attn_a_w_gate"], "w_up": inputs["L0_attn_a_w_up"],
        "w_down": inputs["L0_attn_a_w_down"], "ghidc": cols(ghid, 36),
        "wk": inputs["L0_attn_b_Wk"], "wv": inputs["L0_attn_b_Wv"],
        "wp": inputs["L0_attn_b_Wp"],
        "bp_row": np.asarray(inputs["L0_attn_b_bp"]).reshape(1, E),
        "gn_row": gn.reshape(1, E).astype(np.float32),
        "gs_row": gs.reshape(1, E).astype(np.float32),
        "tril": tril, "diagmask": diagmask, "ones": ones_b, "onesf": onesf,
        "iota_row": iota_row, "identb": ident.astype(bf16), "identf": ident,
        "onesrowf": np.ones((1, 128), np.float32),
        "g2_row": g2.reshape(1, E).astype(np.float32),
        "ghid_row": ghid.reshape(1, F).astype(bf16),
    }
    base["wqf"] = np.asarray(inputs["L0_attn_b_Wq"]).reshape(8 * E, DB)
    base = {k: np.ascontiguousarray(np.asarray(v)) for k, v in base.items()}

    slot_emb = np.asarray(inputs["L0_ffn_slot_embeds"])
    w1 = np.asarray(inputs["L0_ffn_w1"])
    b1 = np.asarray(inputs["L0_ffn_b1"])
    w2 = np.asarray(inputs["L0_ffn_w2"])
    b2 = np.asarray(inputs["L0_ffn_b2"])

    in_maps = []
    for cid in range(8):
        bb, hh = cid >> 1, cid & 1
        sl = slice(hh * NLOC, (hh + 1) * NLOC)
        wstate = np.zeros((128, 8), np.float32)
        if hh == 1:
            wstate[:, cid - 1] = 1.0
        offs = np.zeros((1, 32), np.int32)
        offs[0, 0] = bb * (2 * NLOC * E)
        offs[0, 2] = 2 * bb * 64 * 128
        offs[0, 3] = (bb * L + hh * NLOC) * DB
        offs[0, 4] = hh * NLOC * 1026
        offs[0, 5] = (cid ^ 1) * (NLOC * 1026)
        offs[0, 6] = (1 - hh) * NLOC * 1026
        for qb in range(4):
            offs[0, 8 + qb] = (2 * qb * 9 * 128 + 128 + cid * 128) * NLOC
        for hq in range(8):
            offs[0, 16 + hq] = (hq * 4 * L + bb * L + hh * NLOC) * DB
        m = dict(base)
        m["x_in"] = np.ascontiguousarray(x[bb, sl])
        m["cos_rep"] = np.ascontiguousarray(cos_rep_all[sl])
        m["sin_rep"] = np.ascontiguousarray(sin_rep_all[sl])
        m["wstate"] = wstate
        m["offs"] = offs
        ex = slice(4 * hh, 4 * hh + 4)
        m["slot_emb"] = np.ascontiguousarray(slot_emb[ex].reshape(4 * S, E))
        m["w1f"] = np.ascontiguousarray(w1[ex].reshape(4 * E, HID))
        m["b1c"] = np.ascontiguousarray(
            b1[ex].reshape(4, 32, 128).transpose(2, 0, 1).reshape(128, 128))
        m["w2f"] = np.ascontiguousarray(w2[ex].reshape(4 * HID, E))
        m["b2"] = np.ascontiguousarray(b2[ex])
        in_maps.append(m)
    return in_maps


_CACHE = {}


def kernel(**inputs):
    if "prog" not in _CACHE:
        _CACHE["prog"] = build_program()
    p = _CACHE["prog"]
    in_maps = _shard_inputs(inputs)
    res = run_bass_kernel_spmd(p.nc, in_maps, core_ids=list(range(8)))
    out = np.empty((B, L, E), np.float32)
    for cid in range(8):
        bb, hh = cid >> 1, cid & 1
        out[bb, hh * NLOC:(hh + 1) * NLOC] = res.results[cid]["x_out"]
    if DEBUG:
        return out, [res.results[cid] for cid in range(8)]
    return out


# version 06
# speedup vs baseline: 1.0071x; 1.0071x over previous
import contextlib
import os
import numpy as np
import ml_dtypes
import concourse.bass as bass
import concourse.mybir as mybir
from concourse.bass_utils import run_bass_kernel_spmd

F32 = mybir.dt.float32
BF16 = mybir.dt.bfloat16
I32 = mybir.dt.int32
AF = mybir.ActivationFunctionType
ALU = mybir.AluOpType
AX = mybir.AxisListType
bf16 = ml_dtypes.bfloat16

B, L, E = 4, 8192, 1024
NLOC = 4096
H, DH = 16, 64
F = 4608
NE, S = 8, 512
HID = 4096
NQ, DB = 8, 128
NT = NLOC // 128
NBLK = NLOC // 512
NPAIR = H // 2
STW = 65
STP = NPAIR // 2 * STW * 2
SCOLS = 4 * STW

N_PHASES = int(os.environ.get("MK_PHASES", "99"))
CCMASK = int(os.environ.get("MK_CCMASK", "7"))
ABN = int(os.environ.get("MK_ABN", "0"))
MK_ONLY = set(int(s) for s in os.environ.get("MK_ONLY", "").split(",") if s)
NOCC = os.environ.get("MK_NOCC", "") == "1"
DEBUG = [s for s in os.environ.get("MK_DEBUG", "").split(",") if s]


class Prog:
    def __init__(self, nc):
        self.nc = nc
        self.es = contextlib.ExitStack()
        self.sems = {}
        self.dram = {}
        self.consts = {}
        self.dbg = {}
        self.bar_count = 0
        self.clear_count = 0
        self.phase_sems = []
        self.protected = set()

    def sem(self, name, protect=False):
        if name not in self.sems:
            self.sems[name] = self.es.enter_context(self.nc.semaphore(name))
            if protect:
                self.protected.add(name)
            else:
                self.phase_sems.append((name, self.sems[name]))
        return self.sems[name]

    def drains(self, eng, streams):
        for ds in streams:
            for s in range(ds.n):
                if ds.issued[s]:
                    eng.wait_ge(ds.sems[s], 16 * ds.per * ds.issued[s])


def barrier(p, block):
    sem = p.sem("s_bar", protect=True)
    bsem = p.sem("s_barc", protect=True)
    p.bar_count += 1
    target = 5 * p.bar_count
    for name in ("sync", "gpsimd", "vector", "scalar", "tensor"):
        def f(eng):
            eng.nop().then_inc(sem, 1)
            eng.wait_ge(sem, target)
        getattr(block, name)(f)
    to_clear = [h for (n, h) in p.phase_sems if n not in p.protected]
    names = [n for (n, h) in p.phase_sems if n not in p.protected]
    if to_clear:
        p.clear_count += 1

        @block.gpsimd
        def f2(gpsimd):
            p.nc.clear_and_free_semaphores(to_clear)
            gpsimd.nop().then_inc(bsem, 1)

        for name in ("sync", "vector", "scalar", "tensor"):
            def f3(eng):
                eng.wait_ge(bsem, p.clear_count)
            getattr(block, name)(f3)
        for n in names:
            del p.sems[n]
    p.phase_sems = []



def emit_wconv(p, block):
    nc = p.nc
    d = p.dram
    s_wcv = p.sem("s_wcv", protect=True)
    dcv = Dstr(p, "wcv", 2)

    @block.gpsimd
    def _(gpsimd):
        n = 0
        for ch in range(32):
            dcv.issue(gpsimd, n, [(d["w1b"][ch * 128:(ch + 1) * 128, :],
                                   d["w1f"][ch * 128:(ch + 1) * 128, :], {})])
            gpsimd.nop().then_inc(s_wcv, 1)
            n += 1
        for ch in range(128):
            dcv.issue(gpsimd, n, [(d["w2b"][ch * 128:(ch + 1) * 128, :],
                                   d["w2f"][ch * 128:(ch + 1) * 128, :], {})])
            gpsimd.nop().then_inc(s_wcv, 1)
            n += 1
        p.drains(gpsimd, [dcv])
        gpsimd.nop().then_inc(s_wcv, 100)


class Dstr:

    def __init__(self, p, name, nslots, per=1):
        self.sems = [p.sem(f"{name}{i}") for i in range(nslots)]
        self.n = nslots
        self.per = per
        self.issued = [0] * nslots

    def issue(self, eng, idx, pairs):
        s = idx % self.n
        r = self.issued[s]
        if r >= 1:
            eng.wait_ge(self.sems[s], 16 * self.per * r)
        for out, in_, kw in pairs:
            eng.dma_start(out=out, in_=in_, **kw).then_inc(self.sems[s], 16)
        self.issued[s] += 1

    def wait(self, eng, idx):
        s = idx % self.n
        eng.wait_ge(self.sems[s], 16 * self.per * (idx // self.n + 1))



def selfsync(p, eng, name):
    sem = p.sem(name)
    base = getattr(p, "_ss_" + name, 0)

    def step(ins):
        nonlocal base
        base += 1
        setattr(p, "_ss_" + name, base)
        ins.then_inc(sem, 1)
        eng.wait_ge(sem, base)
        return ins
    return step



def build_program():
    nc = bass.Bass("TRN2", target_bir_lowering=False)
    p = Prog(nc)
    d = p.dram

    def inp(name, shape, dtype=F32):
        d[name] = nc.dram_tensor(name, shape, dtype, kind="ExternalInput")

    inp("x_in", [NLOC, E])
    inp("w_qkv", [E, 3 * E])
    inp("w_out", [E, E])
    inp("g1c", [128, 8])
    inp("g2c", [128, 8])
    inp("w_dp1", [E, 256])
    inp("w_dp2", [256, 1])
    inp("w_gate", [E, F])
    inp("w_up", [E, F])
    inp("w_down", [F, E])
    inp("ghidc", [128, 36])
    inp("wk", [E, DB])
    inp("wv", [E, DB])
    inp("wqf", [NQ * E, DB])
    inp("wp", [E, E])
    inp("bp_row", [1, E])
    inp("gn_row", [1, E])
    inp("gs_row", [1, E])
    inp("slot_emb", [4 * S, E])
    inp("w1f", [4 * E, HID])
    inp("b1c", [128, 4 * 32])
    inp("w2f", [4 * HID, E])
    inp("b2", [4, E])
    inp("cos_rep", [NLOC, E])
    inp("sin_rep", [NLOC, E])
    inp("tril", [128, 128], BF16)
    inp("diagmask", [512, 512], BF16)
    inp("ones", [128, 128], BF16)
    inp("onesf", [128, 8], F32)
    inp("onesrowf", [1, 128], F32)
    inp("g2_row", [1, E], F32)
    inp("ghid_row", [1, F], BF16)
    inp("iota_row", [1, F])
    inp("wstate", [128, 8])
    inp("offs", [1, 32], I32)
    inp("identb", [128, 128], BF16)
    inp("identf", [128, 128], F32)

    d["x_out"] = nc.dram_tensor("x_out", [NLOC, E], F32, kind="ExternalOutput")

    def scr(name, shape, dtype):
        d[name] = nc.dram_tensor(name, shape, dtype)

    scr("xcur", [NLOC, E], F32)
    scr("w1b", [4 * E, HID], BF16)
    scr("w2b", [4 * HID, E], BF16)
    scr("ybuf", [NLOC, E], F32)
    scr("hbuf", [NLOC, E], BF16)
    scr("kbuf", [NLOC, E], BF16)
    scr("qbuf", [NLOC, E], BF16)
    scr("vext", [NLOC, H * STW], BF16)
    scr("aobuf", [NLOC, E], BF16)
    scr("hidbuf", [NLOC, F], BF16)
    scr("cc_st_in", [128, 2 * SCOLS], F32)
    scr("cc_st_out", [8 * 128, 2 * SCOLS], F32)
    scr("cc_xm_in", [1, E], F32)
    scr("cc_xm_out", [8, E], F32)

    for name in DEBUG:
        shp = {"y_ln": [NLOC, E], "h1": [NLOC, E], "kv": [NLOC, E + H * STW],
               "h2f": [NLOC, E],
               "states": [128, 2 * SCOLS], "stin": [128, 2 * SCOLS],
               "y_lin": [NLOC, E], "h2": [NLOC, E], "xmean": [8, E],
               "sizes": [1, 8], "y_ad": [NLOC, E], "xn": [NLOC, E],
               "edisp": [128, 8192], "dm": [16, 128],
               "slots": [4 * S, E], "cn": [128, 128], "eo": [128, 16 * (E + 2)],
               "outp": [L, E + 2], "ffn1": [NLOC, E],
               "xlnb": [NLOC, E], "kvb": [NLOC, 2 * DB], "qh": [NLOC, DB],
               "ob": [4 * L, DB], "attnb": [NLOC, E],
               "mbufd": [NLOC, E + 2], "pbufd": [NLOC, E + 2]}[name]
        dt = {"kv": BF16, "h1": BF16, "h2": BF16, "edisp": BF16, "xn": BF16,
              "eo": BF16, "outp": BF16, "dm": BF16, "cn": BF16, "slots": BF16,
              "mbufd": BF16, "pbufd": BF16,
              "xlnb": BF16, "kvb": BF16, "qh": BF16, "ob": BF16}.get(name, F32)
        p.dbg[name] = nc.dram_tensor("dbg_" + name, shp, dt, kind="ExternalOutput")

    with nc.Block() as block:
        ph_setup(p, block)
        if MK_ONLY:
            ph_stub(p, block)
            emit_wconv(p, block)
            barrier(p, block)
            for phn in sorted(MK_ONLY):
                if phn == 4:
                    ph_ffn(p, block, 0)
                elif phn == 5:
                    ph_attnb(p, block)
                elif phn == 6:
                    ph_ffn(p, block, 1)
                barrier(p, block)
        else:
            if N_PHASES >= 1:
                ph_a(p, block)
            barrier(p, block)
            if N_PHASES >= 2:
                ph_b(p, block)
                emit_wconv(p, block)
                barrier(p, block)
            if N_PHASES >= 3:
                ph_c(p, block)
                barrier(p, block)
            if N_PHASES >= 4:
                ph_ffn(p, block, 0)
                barrier(p, block)
            if N_PHASES >= 5:
                ph_attnb(p, block)
                barrier(p, block)
            if N_PHASES >= 6:
                ph_ffn(p, block, 1)
                barrier(p, block)
        ph_finish(p, block)
    p.es.close()
    return p



def ph_setup(p, block):
    nc = p.nc
    d = p.dram
    c = p.consts
    names = ["g1c", "g2c", "ghidc", "tril", "diagmask", "ones", "onesf", "onesrowf",
             "iota_row", "wstate", "offs", "identb", "identf", "bp_row"]
    shapes = {"g1c": ([128, 8], F32), "g2c": ([128, 8], F32),
              "ghidc": ([128, 36], F32), "tril": ([128, 128], BF16),
              "diagmask": ([128, 4 * 512], BF16), "ones": ([128, 128], BF16),
              "onesf": ([128, 8], F32), "onesrowf": ([1, 128], F32),
              "iota_row": ([1, F], F32),
              "wstate": ([128, 8], F32), "offs": ([1, 32], I32),
              "identb": ([128, 128], BF16), "identf": ([128, 128], F32),
              "bp_row": ([1, E], F32)}
    for n in names:
        c[n] = nc.alloc_sbuf_tensor("c_" + n, *shapes[n])
    s_ld = p.sem("s_setup")
    nl = len(names) + 3

    @block.sync
    def _(sync):
        for n in names:
            if n == "diagmask":
                for o in range(4):
                    sync.dma_start(out=c[n][:, o * 512:(o + 1) * 512],
                                   in_=d[n][o * 128:(o + 1) * 128, :]).then_inc(s_ld, 16)
            else:
                sync.dma_start(out=c[n][:], in_=d[n][:]).then_inc(s_ld, 16)

    for name in ("sync", "gpsimd", "vector", "scalar", "tensor"):
        def f(eng):
            eng.wait_ge(s_ld, 16 * nl)
        getattr(block, name)(f)



def ph_a(p, block):
    nc = p.nc
    d = p.dram
    c = p.consts
    aes = contextlib.ExitStack()
    p.es_ab = contextlib.ExitStack()
    sstates = p.es_ab.enter_context(
        nc.sbuf_tensor("a_sstates", [128, NT * 2 * SCOLS], BF16))

    def sb(name, shape, dtype):
        return aes.enter_context(nc.sbuf_tensor(name, shape, dtype))

    def psb(name):
        return aes.enter_context(nc.psum_tensor(name, [128, 512], F32))

    swkv = [sb(f"a_swkv{cc}", [128, 2 * E], BF16) for cc in range(8)]
    sx = sb("a_sx", [128, 2 * E], F32)
    sx2 = sb("a_sx2", [128, 2 * E], F32)
    sstat = sb("a_sstat", [128, 13 * 128], F32)
    sjunk = sb("a_sjunk", [128, E], F32)
    sy = sb("a_sy", [128, 2 * E], F32)
    shp = sb("a_shp", [128, 2 * E], BF16)
    sht = [sb(f"a_sht{i}", [128, 8 * 512], BF16) for i in range(2)]
    skv = sb("a_skv", [128, 3 * E], F32)
    skb = sb("a_skb", [128, 2 * E], BF16)
    svb = sb("a_svb", [128, 2 * H * STW], BF16)
    srop = sb("a_srop", [128, 4 * E], F32)
    stmp = sb("a_stmp", [128, 4 * E], F32)
    scur = sb("a_scur", [128, 2 * SCOLS], F32)
    ppk = [psb(f"a_ppk{i}") for i in range(4)]
    pst = [psb(f"a_pst{i}") for i in range(4)]
    p.a_handles = dict(sstates=sstates)
    p.a_es = aes

    s_w = p.sem("a_w")
    s_fw = p.sem("a_fw")
    s_acc = p.sem("a_acc")
    s_sa = p.sem("a_sa")
    s_sb = p.sem("a_sb")
    s_sc = p.sem("a_sc")
    s_sd = p.sem("a_sd")
    s_fin = p.sem("a_fin")
    s_pk = p.sem("a_pk")
    s_ev = p.sem("a_ev")
    s_r1 = p.sem("a_r1")
    s_r2 = p.sem("a_r2")
    s_rp = p.sem("a_rp")
    s_sm = p.sem("a_sm")
    s_sn = p.sem("a_sn")
    s_cc = p.sem("a_cc")
    p.sem("a_cd", protect=True)

    dx = Dstr(p, "a_dx", 2)
    dx2 = Dstr(p, "a_dx2", 2)
    dht = Dstr(p, "a_dht", 2, per=8)
    dhs = Dstr(p, "a_dhs", 4)
    dys = Dstr(p, "a_dys", 2)
    drl = Dstr(p, "a_drl", 2, per=2)
    dsp = Dstr(p, "a_dsp", 2, per=2)

    def R(j):
        return sstat[:, j * 128:(j + 1) * 128]

    def Rc(j, i):
        return sstat[:, j * 128 + i:j * 128 + i + 1]

    def vext_cols(o, h0, nh):
        v = svb[:, o * H * STW + h0 * STW:]
        return v.rearrange("p (h w) -> p h w", w=STW)[:, 0:nh, 0:DH]

    def vext_ones(o):
        v = svb[:, o * H * STW:]
        return v.rearrange("p (h w) -> p h w", w=STW)[:, 0:H, DH:STW]

    @block.sync
    def _(sync):
        for i in range(NT):
            if i >= 2:
                sync.wait_ge(s_acc, i - 1)
            dx.issue(sync, i, [(sx[:, (i % 2) * E:(i % 2) * E + E],
                               d["x_in"][i * 128:(i + 1) * 128, :], {})])
        for i in range(NT):
            if i >= 2:
                sync.wait_ge(s_fin, i - 1)
            dx2.issue(sync, i, [(sx2[:, (i % 2) * E:(i % 2) * E + E],
                                d["x_in"][i * 128:(i + 1) * 128, :], {})])
        for b in range(NBLK):
            for t in range(4):
                dhs.wait(sync, b * 4 + t)
            if b >= 2:
                sync.wait_ge(s_pk, 8 * (b - 1))
            dht.issue(sync, b, [
                (sht[b % 2][:, cc * 512:(cc + 1) * 512],
                 d["hbuf"][b * 512:(b + 1) * 512, cc * 128:(cc + 1) * 128],
                 {"transpose": True}) for cc in range(8)])

    @block.gpsimd
    def _(gpsimd):
        for cc in range(8):
            gpsimd.dma_start(out=swkv[cc][:],
                             in_=d["w_qkv"][cc * 128:(cc + 1) * 128, E:3 * E]).then_inc(s_w, 16)
        for i in range(NT):
            gpsimd.wait_ge(s_fin, i + 1)
            o = (i % 2) * E
            dys.issue(gpsimd, i, [(d["ybuf"][i * 128:(i + 1) * 128, :], sy[:, o:o + E], {})])
            dhs.issue(gpsimd, i, [(d["hbuf"][i * 128:(i + 1) * 128, :], shp[:, o:o + E], {})])
        for step in range(NT + 2):
            if step < NT:
                i = step
                if i >= 2:
                    gpsimd.wait_ge(s_r1, i - 1)
                o = (i % 2) * 2 * E
                drl.issue(gpsimd, i, [
                    (srop[:, o:o + E], d["cos_rep"][i * 128:(i + 1) * 128, :], {}),
                    (srop[:, o + E:o + 2 * E], d["sin_rep"][i * 128:(i + 1) * 128, :], {})])
            if 0 <= step - 2 < NT:
                i = step - 2
                gpsimd.wait_ge(s_rp, i + 1)
                o2 = (i % 2)
                dsp.issue(gpsimd, i, [
                    (d["kbuf"][i * 128:(i + 1) * 128, :], skb[:, o2 * E:o2 * E + E], {}),
                    (d["vext"][i * 128:(i + 1) * 128, :],
                     svb[:, o2 * H * STW:(o2 + 1) * H * STW], {})])
        gpsimd.wait_ge(s_sn, NT)
        p.drains(gpsimd, [dys, dhs, dsp])
        gpsimd.dma_start(out=d["cc_st_in"][:], in_=scur[:]).then_inc(s_cc, 16)
        gpsimd.wait_ge(s_cc, 16)
        if not NOCC:
            gpsimd.collective_compute(
                "AllGather", ALU.bypass, replica_groups=[[0, 1, 2, 3, 4, 5, 6, 7]],
                ins=[d["cc_st_in"].ap().opt()], outs=[d["cc_st_out"].ap().opt()],
            ).then_inc(p.sem("a_cd"), 1)
        else:
            gpsimd.nop().then_inc(p.sem("a_cd"), 1)

    @block.scalar
    def _(scalar):
        scalar.wait_ge(p.sem("a_ms"), 1)
        for i in range(NT):
            dx.wait(scalar, i)
            xt = sx[:, (i % 2) * E:(i % 2) * E + E]
            scalar.activation(sjunk[:], xt, AF.Copy, accum_out=Rc(0, i))
            scalar.activation(sjunk[:], xt, AF.Square,
                              accum_out=Rc(1, i)).then_inc(s_acc, 1)
        scalar.wait_ge(s_sb, 1)
        scalar.activation(R(6), R(5), AF.Sqrt)
        scalar.activation(R(9), R(8), AF.Sqrt).then_inc(s_sc, 1)
        for step in range(NT + 1):
            if step < NT:
                i = step
                scalar.wait_ge(s_pk, 2 * i + 2)
                if i >= 1:
                    scalar.wait_ge(s_r1, i)
                if i >= 2:
                    dsp.wait(scalar, i - 2)
                    scalar.wait_ge(s_sm, i - 1)
                scalar.activation(skv[:, 0:512], ppk[0][:], AF.Copy)
                scalar.activation(skv[:, 512:1024], ppk[1][:], AF.Copy)
                o2 = (i % 2)
                scalar.activation(vext_cols(o2, 0, 8),
                                  ppk[2][:].rearrange("p (h e) -> p h e", e=DH), AF.Copy)
                scalar.activation(vext_cols(o2, 8, 8),
                                  ppk[3][:].rearrange("p (h e) -> p h e", e=DH),
                                  AF.Copy).then_inc(s_ev, 1)
            if 0 <= step - 1 < NT:
                i = step - 1
                scalar.wait_ge(s_r1, i + 1)
                if i >= 2:
                    scalar.wait_ge(s_rp, i - 1)
                eo = E + (i % 2) * E
                scalar.activation(skv[:, eo:eo + E],
                                  stmp[:, (i % 2) * 2 * E + E:(i % 2) * 2 * E + 2 * E],
                                  AF.Exp).then_inc(s_r2, 1)

    @block.vector
    def _(vector):
        vector.memset(sstat[:], 0.0).then_inc(p.sem("a_ms"), 1)
        vector.wait_ge(s_w, 16 * 8)
        for cc in range(8):
            ins = vector.tensor_scalar_mul(swkv[cc][:], swkv[cc][:], c["g1c"][:, cc:cc + 1])
        ins.then_inc(s_fw, 1)
        for o in range(2):
            vector.memset(vext_ones(o), 1.0)
        vector.wait_ge(s_acc, NT)
        st = selfsync(p, vector, "a_ssv")
        st(vector.tensor_scalar_mul(R(2), R(0), 1.0 / E))
        st(vector.tensor_tensor(R(3), R(2), R(2), ALU.mult))
        st(vector.tensor_scalar_mul(R(4), R(1), 1.0 / E))
        st(vector.tensor_tensor(R(4), R(4), R(3), ALU.subtract))
        st(vector.tensor_scalar_add(R(5), R(4), 1e-5))
        st(vector.reciprocal(R(5), R(5)))
        st(vector.tensor_tensor(R(8), R(4), R(5), ALU.mult))
        st(vector.tensor_scalar_add(R(8), R(8), 1e-6))
        vector.nop().then_inc(s_sb, 1)
        vector.wait_ge(s_sc, 1)
        st(vector.scalar_tensor_tensor(R(7), R(2), -1.0, R(6), ALU.mult, ALU.mult))
        st(vector.tensor_scalar(R(10), R(9), 1e-6, 1e6, ALU.max, ALU.min))
        st(vector.reciprocal(R(10), R(10)))
        st(vector.tensor_tensor(R(11), R(6), R(10), ALU.mult))
        st(vector.tensor_tensor(R(12), R(7), R(10), ALU.mult))
        vector.nop().then_inc(s_sd, 1)
        for i in range(NT):
            dx2.wait(vector, i)
            if i >= 2:
                dys.wait(vector, i - 2)
                dhs.wait(vector, i - 2)
            xt = sx2[:, (i % 2) * E:(i % 2) * E + E]
            o = (i % 2) * E
            vector.tensor_scalar(sy[:, o:o + E], xt, Rc(6, i), Rc(7, i),
                                 ALU.mult, ALU.add)
            vector.tensor_scalar(shp[:, o:o + E], xt, Rc(11, i), Rc(12, i),
                                 ALU.mult, ALU.add).then_inc(s_fin, 1)
        for step in range(NT + 2):
            if step < NT:
                i = step
                vector.wait_ge(s_ev, i + 1)
                drl.wait(vector, i)
                ot = (i % 2) * 2 * E
                orp = (i % 2) * 2 * E
                kt = skv[:, 0:E]
                cosb = srop[:, orp:orp + E]
                sinb = srop[:, orp + E:orp + 2 * E]
                t0 = stmp[:, ot:ot + E]
                t1 = stmp[:, ot + E:ot + 2 * E]
                vector.tensor_copy(t0[:, 0:E - 1:2], kt[:, 1:E:2])
                vector.tensor_copy(t0[:, 1:E:2], kt[:, 0:E - 1:2])
                vector.tensor_tensor(t0, t0, sinb, ALU.mult)
                vector.tensor_tensor(t1, kt, cosb, ALU.mult)
                vector.tensor_tensor(t0, t0, t1, ALU.add)
                vector.tensor_scalar_min(t1, t0, 0.0)
                vector.tensor_scalar_max(t0, t0, 0.0).then_inc(s_r1, 1)
            if 0 <= step - 1 < NT:
                i = step - 1
                vector.wait_ge(s_r2, i + 1)
                if i >= 2:
                    vector.wait_ge(s_sm, i - 1)
                    dsp.wait(vector, i - 2)
                eo = E + (i % 2) * E
                ins = vector.tensor_tensor(skb[:, (i % 2) * E:(i % 2) * E + E],
                                           stmp[:, (i % 2) * 2 * E:(i % 2) * 2 * E + E],
                                           skv[:, eo:eo + E], ALU.add)
                ins.then_inc(s_rp, 1)
            if 0 <= step - 2 < NT:
                j = step - 2
                vector.wait_ge(s_sm, j + 1)
                b0, b1 = pst[2 * (j % 2)], pst[2 * (j % 2) + 1]
                if j == 0:
                    vector.tensor_copy(scur[:, 0:SCOLS], b0[:, 0:SCOLS])
                    vector.tensor_copy(scur[:, SCOLS:2 * SCOLS], b1[:, 0:SCOLS])
                else:
                    vector.tensor_tensor(scur[:, 0:SCOLS], b0[:, 0:SCOLS],
                                         scur[:, 0:SCOLS], ALU.add)
                    vector.tensor_tensor(scur[:, SCOLS:2 * SCOLS], b1[:, 0:SCOLS],
                                         scur[:, SCOLS:2 * SCOLS], ALU.add)
                ins = vector.tensor_copy(
                    sstates[:, (2 * j) * SCOLS:(2 * j + 2) * SCOLS], scur[:])
                ins.then_inc(s_sn, 1)

    @block.tensor
    def _(tensor):
        tensor.wait_ge(s_fw, 1)

        def state_mms(j):
            tensor.wait_ge(s_rp, j + 1)
            if j >= 2:
                tensor.wait_ge(s_sn, j - 1)
            o2 = (j % 2)
            mm = None
            for h in range(H):
                pair = h // 2
                bank = pst[2 * (j % 2) + pair // 4]
                rowbase = (h % 2) * 64
                colbase = (pair % 4) * STW
                mm = tensor.matmul(
                    bank[rowbase:rowbase + 64, colbase:colbase + STW],
                    skb[:, o2 * E + h * DH:o2 * E + (h + 1) * DH],
                    svb[:, o2 * H * STW + h * STW:o2 * H * STW + (h + 1) * STW],
                    start=True, stop=True,
                    tile_position=(0, rowbase))
            mm.then_inc(s_sm, 1)

        for i in range(NT):
            b, sub = i // 4, i % 4
            if sub == 0:
                dht.wait(tensor, b)
            if i >= 1:
                tensor.wait_ge(s_ev, i)
            mm = None
            for half in range(2):
                for cc in range(8):
                    mm = tensor.matmul(
                        ppk[half][:],
                        sht[b % 2][:, cc * 512 + sub * 128:cc * 512 + sub * 128 + 128],
                        swkv[cc][:, half * 512:(half + 1) * 512],
                        start=(cc == 0), stop=(cc == 7))
            mm.then_inc(s_pk, 1)
            for half in range(2):
                for cc in range(8):
                    mm = tensor.matmul(
                        ppk[2 + half][:],
                        sht[b % 2][:, cc * 512 + sub * 128:cc * 512 + sub * 128 + 128],
                        swkv[cc][:, E + half * 512:E + (half + 1) * 512],
                        start=(cc == 0), stop=(cc == 7))
            mm.then_inc(s_pk, 1)
            if i >= 1:
                state_mms(i - 1)
        state_mms(NT - 1)

    emit_ph_a_debug(p, block)
    aes.close()


def emit_ph_a_debug(p, block):
    d, dbg = p.dram, p.dbg
    if not dbg:
        return
    s_dbg = p.sem("s_dbg")
    n = [0]

    @block.sync
    def _(sync):
        def cp(dst, src):
            sync.dma_start(out=dst, in_=src).then_inc(s_dbg, 16)
            n[0] += 1
        if "y_ln" in dbg:
            for s in range(2):
                sync.wait_ge(p.sem(f"a_dys{s}"), 16 * (NT // 2))
            cp(dbg["y_ln"][:], d["ybuf"][:])
        if "h1" in dbg:
            for s in range(4):
                sync.wait_ge(p.sem(f"a_dhs{s}"), 16 * (NT // 4))
            cp(dbg["h1"][:], d["hbuf"][:])
        if "kv" in dbg:
            for s in range(2):
                sync.wait_ge(p.sem(f"a_dsp{s}"), 16 * 2 * (NT // 2))
            cp(dbg["kv"][:, 0:E], d["kbuf"][:])
            cp(dbg["kv"][:, E:E + H * STW], d["vext"][:])
        if "states" in dbg:
            sync.wait_ge(p.sem("a_cc"), 16)
            sync.wait_ge(p.sem("a_cd"), 1)
            cp(dbg["states"][:], d["cc_st_in"][:])
        sync.wait_ge(s_dbg, 16 * n[0])


def ph_b(p, block):
    nc = p.nc
    d = p.dram
    c = p.consts
    bes = contextlib.ExitStack()
    scanes = contextlib.ExitStack()

    def sb(name, shape, dtype, es=None):
        return (es or bes).enter_context(nc.sbuf_tensor(name, shape, dtype))

    def psb(name, es=None):
        return (es or bes).enter_context(nc.psum_tensor(name, [128, 512], F32))

    sstates = p.a_handles["sstates"]
    swq = [sb(f"b_swq{cc}", [128, E], BF16) for cc in range(8)]
    sht = [sb(f"b_sht{i}", [128, 8 * 512], BF16) for i in range(2)]
    sgath = sb("b_sgath", [128, 2 * 2 * SCOLS], F32)
    sin_ = sb("b_sin", [128, 2 * SCOLS], F32)
    sprev = sb("b_sprev", [128, 2 * 2 * SCOLS], BF16)
    sq = sb("b_sq", [128, 3 * E], F32)
    sqb = sb("b_sqb", [128, 2 * E], BF16)
    srop = sb("b_srop", [128, 4 * E], F32)
    stmp = sb("b_stmp", [128, 4 * E], F32)
    sqt = sb("b_sqt", [128, 2 * E], BF16)
    skt = sb("b_skt", [128, 2 * E], BF16)
    svx = sb("b_svx", [128, 2 * H * STW], BF16)
    sA = sb("b_sA", [128, 4 * 128], BF16)
    sden = sb("b_sden", [128, 64], F32)
    sao = sb("b_sao", [128, 2 * E], BF16)
    saot = sb("b_saot", [128, 2 * 8 * 512], BF16)
    syt = sb("b_syt", [128, 2 * E], F32)
    pq = [psb(f"b_pq{i}", scanes) for i in range(2)]
    pA = [psb(f"b_pA{i}", scanes) for i in range(2)]
    pn = [psb(f"b_pn{i}", scanes) for i in range(4)]

    s_w = p.sem("b_w")
    s_fw = p.sem("b_fw")
    s_si = p.sem("b_si")
    s_ga = p.sem("b_gacc")
    s_pq = p.sem("b_pq")
    s_qe = p.sem("b_qe")
    s_r1 = p.sem("b_r1")
    s_r2 = p.sem("b_r2")
    s_qp = p.sem("b_qp")
    s_sv = p.sem("b_sv")
    s_am = p.sem("b_am")
    s_ni = p.sem("b_ni")
    s_ae = p.sem("b_ae")
    s_nm = p.sem("b_nm")
    s_no = p.sem("b_no")
    s_po = p.sem("b_po")
    s_yo = p.sem("b_yo")
    s_wo = p.sem("b_wo")

    dht = Dstr(p, "b_dht", 2, per=8)
    dqs = Dstr(p, "b_dqs", 2)
    dqt = Dstr(p, "b_dqt", 2, per=8)
    dkt = Dstr(p, "b_dkt", 2, per=8)
    dvx = Dstr(p, "b_dvx", 2)
    drl = Dstr(p, "b_drl", 2, per=2)
    dgt = Dstr(p, "b_dgt", 2)
    das = Dstr(p, "b_das", 2)
    dat = Dstr(p, "b_dat", 2, per=8)
    dyl = Dstr(p, "b_dyl", 2)
    dyo = Dstr(p, "b_dyo", 2)

    def st_cols(pair):
        return (pair % 4) * STW + (pair // 4) * SCOLS

    @block.sync
    def _(sync):
        sync.wait_ge(p.sem("a_cd"), 1)
        for r in range(8):
            if r >= 2:
                sync.wait_ge(s_ga, r - 1)
            gsrc = d["cc_st_in"][:] if NOCC else d["cc_st_out"][r * 128:(r + 1) * 128, :]
            dgt.issue(sync, r, [(sgath[:, (r % 2) * 2 * SCOLS:(r % 2 + 1) * 2 * SCOLS],
                                 gsrc, {})])
        def issue_dht(b):
            dht.issue(sync, b, [
                (sht[b % 2][:, cc * 512:(cc + 1) * 512],
                 d["hbuf"][b * 512:(b + 1) * 512, cc * 128:(cc + 1) * 128],
                 {"transpose": True}) for cc in range(8)])

        issue_dht(0)
        for j in range(NT):
            if j % 4 == 0 and j // 4 + 1 < NBLK:
                b = j // 4
                if b >= 1:
                    sync.wait_ge(s_pq, 8 * b)
                issue_dht(b + 1)
            if j >= 2:
                sync.wait_ge(s_nm, j - 1)
            dkt.issue(sync, j, [
                (skt[:, (j % 2) * E + hh * 128:(j % 2) * E + (hh + 1) * 128],
                 d["kbuf"][j * 128:(j + 1) * 128, hh * 128:(hh + 1) * 128],
                 {"transpose": True}) for hh in range(8)])
            dvx.issue(sync, j, [(svx[:, (j % 2) * H * STW:(j % 2 + 1) * H * STW],
                                 d["vext"][j * 128:(j + 1) * 128, :], {})])
            dqs.wait(sync, j)
            dqt.issue(sync, j, [
                (sqt[:, (j % 2) * E + hh * 128:(j % 2) * E + (hh + 1) * 128],
                 d["qbuf"][j * 128:(j + 1) * 128, hh * 128:(hh + 1) * 128],
                 {"transpose": True}) for hh in range(8)])
        for i in range(NT):
            if i % 4 == 0:
                b = i // 4
                for t in range(4):
                    das.wait(sync, b * 4 + t)
                if b >= 2:
                    sync.wait_ge(s_po, 4 * (b - 1))
                dat.issue(sync, b, [
                    (saot[:, (b % 2) * 8 * 512 + cc * 512:(b % 2) * 8 * 512 + (cc + 1) * 512],
                     d["aobuf"][b * 512:(b + 1) * 512, cc * 128:(cc + 1) * 128],
                     {"transpose": True}) for cc in range(8)])
            if i >= 2:
                sync.wait_ge(s_yo, i - 1)
                dyo.wait(sync, i - 2)
            dyl.issue(sync, i, [(syt[:, (i % 2) * E:(i % 2) * E + E],
                                 d["ybuf"][i * 128:(i + 1) * 128, :], {})])

    @block.gpsimd
    def _(gpsimd):
        for cc in range(8):
            gpsimd.dma_start(out=swq[cc][:],
                             in_=d["w_qkv"][cc * 128:(cc + 1) * 128, 0:E]).then_inc(s_w, 16)
        for step in range(NT + 5):
            if step < NT:
                i = step
                if i >= 2:
                    gpsimd.wait_ge(s_r1, i - 1)
                o = (i % 2) * 2 * E
                drl.issue(gpsimd, i, [
                    (srop[:, o:o + E], d["cos_rep"][i * 128:(i + 1) * 128, :], {}),
                    (srop[:, o + E:o + 2 * E], d["sin_rep"][i * 128:(i + 1) * 128, :], {})])
            if 0 <= step - 2 < NT:
                i = step - 2
                gpsimd.wait_ge(s_qp, i + 1)
                o2 = (i % 2)
                dqs.issue(gpsimd, i, [(d["qbuf"][i * 128:(i + 1) * 128, :],
                                       sqb[:, o2 * E:o2 * E + E], {})])
            if 0 <= step - 5 < NT:
                j = step - 5
                gpsimd.wait_ge(s_no, j + 1)
                das.issue(gpsimd, j, [(d["aobuf"][j * 128:(j + 1) * 128, :],
                                       sao[:, (j % 2) * E:(j % 2) * E + E], {})])
        gpsimd.wait_ge(s_pq, 2 * NT)
        for cc in range(8):
            gpsimd.dma_start(out=sht[cc // 4][:, (cc % 4) * 1024:(cc % 4) * 1024 + 1024],
                             in_=d["w_out"][cc * 128:(cc + 1) * 128, :]).then_inc(s_wo, 16)
        for i in range(NT):
            gpsimd.wait_ge(s_yo, i + 1)
            dyo.issue(gpsimd, i, [(d["ybuf"][i * 128:(i + 1) * 128, :],
                                   syt[:, (i % 2) * E:(i % 2) * E + E], {})])
        p.drains(gpsimd, [dqs, das, dyo])

    @block.scalar
    def _(scalar):
        for step in range(NT + 1):
            if 0 <= step - 1 < NT:
                i = step - 1
                scalar.wait_ge(s_r1, i + 1)
                if i >= 2:
                    scalar.wait_ge(s_qp, i - 1)
                eo = E + (i % 2) * E
                scalar.activation(sq[:, eo:eo + E],
                                  stmp[:, (i % 2) * 2 * E + E:(i % 2) * 2 * E + 2 * E],
                                  AF.Exp).then_inc(s_r2, 1)
            if step < NT:
                i = step
                scalar.wait_ge(s_pq, 2 * i + 2)
                if i >= 1:
                    scalar.wait_ge(s_r1, i)
                scalar.activation(sq[:, 0:512], pq[0][:], AF.Copy)
                scalar.activation(sq[:, 512:1024], pq[1][:], AF.Copy).then_inc(s_qe, 1)

    @block.vector
    def _(vector):
        vector.wait_ge(s_w, 16 * 8)
        for cc in range(8):
            ins = vector.tensor_scalar_mul(swq[cc][:], swq[cc][:], c["g1c"][:, cc:cc + 1])
        ins.then_inc(s_fw, 1)
        stp = selfsync(p, vector, "b_ssv")
        for r in range(8):
            dgt.wait(vector, r)
            g = sgath[:, (r % 2) * 2 * SCOLS:(r % 2 + 1) * 2 * SCOLS]
            if r == 0:
                stp(vector.tensor_scalar_mul(sin_[:], g, c["wstate"][:, 0:1]))
            else:
                stp(vector.scalar_tensor_tensor(sin_[:], g, c["wstate"][:, r:r + 1],
                                                sin_[:], ALU.mult, ALU.add))
            vector.nop().then_inc(s_ga, 1)
        vector.nop().then_inc(s_si, 1)
        for step in range(NT + 4):
            if 0 <= step - 4 < NT:
                j = step - 4
                vector.wait_ge(s_nm, j + 1)
                if j >= 2:
                    das.wait(vector, j - 2)
                sts = selfsync(p, vector, "b_ssn")
                for bk in range(4):
                    sts(vector.tensor_copy(
                        sden[:, bk * 4:bk * 4 + 4],
                        pn[bk][:, 0:4 * STW].rearrange("p (h w) -> p h w", w=STW)
                        [:, :, DH:STW].rearrange("p h w -> p (h w)")))
                sts(vector.tensor_scalar_max(sden[:, 16:32], sden[:, 0:16], 1e-6))
                sts(vector.reciprocal(sden[:, 32:48], sden[:, 16:32]))
                for h in range(H):
                    vector.tensor_scalar_mul(
                        sao[:, (j % 2) * E + h * DH:(j % 2) * E + (h + 1) * DH],
                        pn[h // 4][:, (h % 4) * STW:(h % 4) * STW + DH],
                        sden[:, 32 + h:33 + h])
                vector.nop().then_inc(s_no, 1)
            if 0 <= step - 3 < NT:
                j = step - 3
                for h in range(H):
                    g = j * H + h
                    vector.wait_ge(s_am, g + 1)
                    if g >= 4:
                        vector.wait_ge(s_ni, g - 3)
                    vector.tensor_tensor(
                        sA[:, (g % 4) * 128:(g % 4) * 128 + 128],
                        pA[h % 2][:, 0:128], c["tril"][:], ALU.mult).then_inc(s_ae, 1)
            if 0 <= step - 2 < NT:
                j = step - 2
                vector.wait_ge(s_si, 1)
                if j >= 2:
                    vector.wait_ge(s_nm, j - 1)
                dstp = sprev[:, (j % 2) * 2 * SCOLS:(j % 2 + 1) * 2 * SCOLS]
                if j == 0:
                    vector.tensor_copy(dstp, sin_[:]).then_inc(s_sv, 1)
                else:
                    vector.tensor_tensor(
                        dstp, sstates[:, (2 * j - 2) * SCOLS:(2 * j) * SCOLS],
                        sin_[:], ALU.add).then_inc(s_sv, 1)
            if 0 <= step - 1 < NT:
                i = step - 1
                vector.wait_ge(s_r2, i + 1)
                if i >= 2:
                    dqs.wait(vector, i - 2)
                eo = E + (i % 2) * E
                vector.tensor_tensor(sqb[:, (i % 2) * E:(i % 2) * E + E],
                                     stmp[:, (i % 2) * 2 * E:(i % 2) * 2 * E + E],
                                     sq[:, eo:eo + E], ALU.add).then_inc(s_qp, 1)
            if step < NT:
                i = step
                vector.wait_ge(s_qe, i + 1)
                drl.wait(vector, i)
                ot = (i % 2) * 2 * E
                qt_ = sq[:, 0:E]
                cosb = srop[:, ot:ot + E]
                sinb = srop[:, ot + E:ot + 2 * E]
                t0 = stmp[:, ot:ot + E]
                t1 = stmp[:, ot + E:ot + 2 * E]
                vector.tensor_copy(t0[:, 0:E - 1:2], qt_[:, 1:E:2])
                vector.tensor_copy(t0[:, 1:E:2], qt_[:, 0:E - 1:2])
                vector.tensor_tensor(t0, t0, sinb, ALU.mult)
                vector.tensor_tensor(t1, qt_, cosb, ALU.mult)
                vector.tensor_tensor(t0, t0, t1, ALU.add)
                vector.tensor_scalar_min(t1, t0, 0.0)
                vector.tensor_scalar_max(t0, t0, 0.0).then_inc(s_r1, 1)

    @block.tensor
    def _(tensor):
        tensor.wait_ge(s_fw, 1)
        for step in range(NT + 3):
            if step < NT:
                i = step
                b, sub = i // 4, i % 4
                if sub == 0:
                    dht.wait(tensor, b)
                if i >= 1:
                    tensor.wait_ge(s_qe, i)
                for half in range(2):
                    mm = None
                    for cc in range(8):
                        mm = tensor.matmul(
                            pq[half][:],
                            sht[b % 2][:, cc * 512 + sub * 128:cc * 512 + sub * 128 + 128],
                            swq[cc][:, half * 512:(half + 1) * 512],
                            start=(cc == 0), stop=(cc == 7))
                    mm.then_inc(s_pq, 1)
            if 0 <= step - 3 < NT:
                j = step - 3
                dkt.wait(tensor, j)
                dqt.wait(tensor, j)
                dvx.wait(tensor, j)
                tensor.wait_ge(s_sv, j + 1)
                o2 = (j % 2)
                for hs in range(H + 2):
                    if hs < H:
                        h = hs
                        rb = (h % 2) * 64
                        hh = h // 2
                        if j * H + h >= 2:
                            tensor.wait_ge(s_ae, j * H + h - 1)
                        tensor.matmul(
                            pA[h % 2][:, 0:128],
                            skt[rb:rb + 64, o2 * E + hh * 128:o2 * E + hh * 128 + 128],
                            sqt[rb:rb + 64, o2 * E + hh * 128:o2 * E + hh * 128 + 128],
                            start=True, stop=True,
                            tile_position=(rb, 0)).then_inc(s_am, 1)
                    if hs - 2 >= 0:
                        h = hs - 2
                        rb = (h % 2) * 64
                        hh = h // 2
                        bank = pn[h // 4]
                        col = (h % 4) * STW
                        vslice = svx[:, o2 * H * STW + h * STW:o2 * H * STW + (h + 1) * STW]
                        tensor.wait_ge(s_ae, j * H + h + 1)
                        if j >= 1 and h == 0:
                            tensor.wait_ge(s_no, j)
                        g = j * H + h
                        tensor.matmul(bank[:, col:col + STW],
                                      sA[:, (g % 4) * 128:(g % 4) * 128 + 128],
                                      vslice, start=True, stop=False).then_inc(s_ni, 1)
                        mm = tensor.matmul(
                            bank[:, col:col + STW],
                            sqt[rb:rb + 64, o2 * E + hh * 128:o2 * E + hh * 128 + 128],
                            sprev[rb:rb + 64, o2 * 2 * SCOLS + st_cols(hh):
                                  o2 * 2 * SCOLS + st_cols(hh) + STW],
                            start=False, stop=True, tile_position=(rb, 0))
                        if h == H - 1:
                            mm.then_inc(s_nm, 1)

    scanes.close()
    po = [psb(f"b_po{i}") for i in range(4)]

    @block.tensor
    def _(tensor):
        tensor.wait_ge(s_no, NT)
        tensor.wait_ge(s_wo, 16 * 8)
        for i in range(NT):
            b, sub = i // 4, i % 4
            if sub == 0:
                dat.wait(tensor, b)
            if i >= 2:
                tensor.wait_ge(s_yo, i - 1)
            for half in range(2):
                mm = None
                for cc in range(8):
                    mm = tensor.matmul(
                        po[2 * (i % 2) + half][:],
                        saot[:, (b % 2) * 8 * 512 + cc * 512 + sub * 128:
                             (b % 2) * 8 * 512 + cc * 512 + sub * 128 + 128],
                        sht[cc // 4][:, (cc % 4) * 1024 + half * 512:
                                     (cc % 4) * 1024 + half * 512 + 512],
                        start=(cc == 0), stop=(cc == 7))
            mm.then_inc(s_po, 1)

    @block.vector
    def _(vector):
        for i in range(NT):
            dyl.wait(vector, i)
            vector.wait_ge(s_po, i + 1)
            o = (i % 2) * E
            vector.tensor_tensor(syt[:, o:o + 512], syt[:, o:o + 512],
                                 po[2 * (i % 2)][:], ALU.add)
            vector.tensor_tensor(syt[:, o + 512:o + E], syt[:, o + 512:o + E],
                                 po[2 * (i % 2) + 1][:], ALU.add).then_inc(s_yo, 1)

    emit_ph_b_debug(p, block)
    bes.close()
    p.es_ab.close()


def emit_ph_b_debug(p, block):
    d, dbg = p.dram, p.dbg
    if not dbg:
        return
    s_dbg = p.sem("s_dbgb")
    n = [0]

    @block.sync
    def _(sync):
        def cp(dst, src):
            sync.dma_start(out=dst, in_=src).then_inc(s_dbg, 16)
            n[0] += 1
        if "y_lin" in dbg:
            for s in range(2):
                sync.wait_ge(p.sem(f"b_dyo{s}"), 16 * (NT // 2))
            cp(dbg["y_lin"][:], d["ybuf"][:])
        if n[0]:
            sync.wait_ge(s_dbg, 16 * n[0])


def ph_c(p, block):
    nc = p.nc
    d = p.dram
    c = p.consts
    ces = contextlib.ExitStack()

    def sb(name, shape, dtype):
        return ces.enter_context(nc.sbuf_tensor(name, shape, dtype))

    def psb(name):
        return ces.enter_context(nc.psum_tensor(name, [128, 512], F32))

    if "h2buf" not in d:
        d["h2buf"] = nc.dram_tensor("h2buf", [NLOC, E], BF16)
        d["mrow"] = nc.dram_tensor("mrow", [1, F], BF16)
        d["sclr"] = nc.dram_tensor("sclr", [1, 8], F32)

    sy = sb("c_sy", [128, 2 * E], F32)
    sy2 = sb("c_sy2", [128, 2 * E], F32)
    sstat = sb("c_sstat", [128, 4 * 128], F32)
    sjunk = sb("c_sjunk", [128, E], F32)
    sh2 = sb("c_sh2", [128, 2 * E], BF16)
    sxm = sb("c_sxm", [128, 64], F32)
    sdp = sb("c_sdp", [128, 32], F32)
    sdpb = sb("c_sdpb", [128, 48], BF16)
    swdp1 = sb("c_swdp1", [128, 8 * 256], BF16)
    swdp2 = sb("c_swdp2", [128, 2], BF16)
    srow = sb("c_srow", [1, F], F32)
    srowb = sb("c_srowb", [1, F], BF16)
    smask = sb("c_smask", [128, F], BF16)
    sgh = sb("c_sgh", [128, F], BF16)
    sg2 = sb("c_sg2", [128, E], F32)
    sh2t = [sb(f"c_sh2t{i}", [128, 8 * 512], BF16) for i in range(2)]
    swg = sb("c_swg", [128, 2 * 8 * 512], BF16)
    swu = sb("c_swu", [128, 2 * 8 * 512], BF16)
    shid = sb("c_shid", [128, 2 * 512], F32)
    shb = sb("c_shb", [128, 2 * 512], BF16)
    ssq = sb("c_ssq", [128, 32 * 9], F32)
    srms = sb("c_srms", [128, 128], F32)
    swd = sb("c_swd", [128, 6 * E], BF16)
    sht = sb("c_sht", [128, 6 * 512], BF16)
    syx = sb("c_syx", [128, 2 * E], F32)
    sxi = sb("c_sxi", [128, 2 * E], F32)
    sxo = sb("c_sxo", [128, 2 * E], F32)
    pg = [psb(f"c_pg{i}") for i in range(2)]
    pu = [psb(f"c_pu{i}") for i in range(2)]
    pxm = [psb(f"c_pxm{i}") for i in range(2)]
    pd = [psb(f"c_pd{i}") for i in range(2)]

    s_sq = p.sem("c_sq")
    s_al = p.sem("c_al")
    s_h2 = p.sem("c_h2")
    s_xm = p.sem("c_xm")
    s_xe = p.sem("c_xe")
    s_cc = p.sem("c_ccd")
    s_dp = p.sem("c_dp")
    s_msk = p.sem("c_msk")
    s_gm = p.sem("c_gm")
    s_sl = p.sem("c_sl")
    s_hd = p.sem("c_hd")
    s_rm = p.sem("c_rm")
    s_dm = p.sem("c_dm")
    s_dc = p.sem("c_dc")
    s_xw = p.sem("c_xw")
    s_ms = p.sem("c_ms")

    dy1 = Dstr(p, "c_dy1", 2)
    dy2 = Dstr(p, "c_dy2", 2)
    dh2 = Dstr(p, "c_dh2", 4)
    dxm = Dstr(p, "c_dxm", 1, per=8)
    ddp = Dstr(p, "c_ddp", 1, per=9)
    dwg = Dstr(p, "c_dwg", 2, per=16)
    dhs = Dstr(p, "c_dhs", 4)
    dht2 = Dstr(p, "c_dht2", 2, per=8)
    dwd = Dstr(p, "c_dwd", 6)
    dhl = Dstr(p, "c_dhl", 6)
    dyx = Dstr(p, "c_dyx", 2, per=2)
    dxo = Dstr(p, "c_dxo", 2)
    dmr = Dstr(p, "c_dmr", 1, per=2)
    dgr = Dstr(p, "c_dgr", 1, per=2)
    dms = Dstr(p, "c_dms", 1, per=2)

    def Rc(j, i):
        return sstat[:, j * 128 + i:j * 128 + i + 1]

    def R(j):
        return sstat[:, j * 128:(j + 1) * 128]

    NHB = 9
    NKK = NBLK * NHB * 4

    def kk_parts(kk):
        b = kk // 36
        hb = (kk // 4) % 9
        sub = kk % 4
        return b, hb, sub, b * 4 + sub

    @block.sync
    def _(sync):
        dgr.issue(sync, 0, [
            (sg2[:], bass.AP(d["g2_row"], 0, [[0, 128], [1, E]]), {}),
            (sgh[:], bass.AP(d["ghid_row"], 0, [[0, 128], [1, F]]), {})])
        for i in range(NT):
            if i >= 2:
                sync.wait_ge(s_sq, i - 1)
            dy1.issue(sync, i, [(sy[:, (i % 2) * E:(i % 2) * E + E],
                                 d["ybuf"][i * 128:(i + 1) * 128, :], {})])
        for i in range(NT):
            if i >= 2:
                sync.wait_ge(s_h2, i - 1)
            dy2.issue(sync, i, [(sy2[:, (i % 2) * E:(i % 2) * E + E],
                                 d["ybuf"][i * 128:(i + 1) * 128, :], {})])
        sync.wait_ge(s_cc, 1)
        if NOCC:
            dxm.issue(sync, 0, [(sxm[:, cc * 8:(cc + 1) * 8],
                                 bass.AP(d["cc_xm_in"], cc * 128, [[1, 128], [0, 8]]),
                                 {"allow_slow_non_contiguous": True}) for cc in range(8)])
        else:
            dxm.issue(sync, 0, [(sxm[:, cc * 8:(cc + 1) * 8],
                                 d["cc_xm_out"].transpose([1, 0])[cc * 128:(cc + 1) * 128, :],
                                 {"allow_slow_non_contiguous": True}) for cc in range(8)])
        dms.wait(sync, 0)
        dmr.issue(sync, 0, [
            (smask[:], bass.AP(d["mrow"], 0, [[0, 128], [1, F]]), {}),
            (srms[:, 96:97], bass.AP(d["sclr"], 0, [[0, 128], [1, 1]]), {})])
        for b in range(NBLK):
            for t in range(4):
                dh2.wait(sync, b * 4 + t)
            if b >= 2:
                sync.wait_ge(s_gm, 2 * 36 * (b - 1))
            dht2.issue(sync, b, [
                (sh2t[b % 2][:, cc * 512:(cc + 1) * 512],
                 d["h2buf"][b * 512:(b + 1) * 512, cc * 128:(cc + 1) * 128],
                 {"transpose": True}) for cc in range(8)])
        for i in range(NT):
            b, sub = i // 4, i % 4
            if sub == 0:
                for cc in range(36):
                    sidx = b * 36 + cc
                    if sidx >= 6:
                        sync.wait_ge(s_dc, sidx - 5)
                    dhl.issue(sync, sidx, [
                        (sht[:, (sidx % 6) * 512:(sidx % 6) * 512 + 512],
                         d["hidbuf"][b * 512:(b + 1) * 512, cc * 128:(cc + 1) * 128],
                         {"transpose": True})])
            if i >= 2:
                sync.wait_ge(s_xw, i - 1)
            dyx.issue(sync, i, [
                (syx[:, (i % 2) * E:(i % 2) * E + E],
                 d["ybuf"][i * 128:(i + 1) * 128, :], {}),
                (sxi[:, (i % 2) * E:(i % 2) * E + E],
                 d["x_in"][i * 128:(i + 1) * 128, :], {})])

    @block.gpsimd
    def _(gpsimd):
        ddp.issue(gpsimd, 0, [(swdp1[:, cc * 256:(cc + 1) * 256],
                               d["w_dp1"][cc * 128:(cc + 1) * 128, :], {})
                              for cc in range(8)] +
                  [(swdp2[:], d["w_dp2"].rearrange("(a b) o -> b (a o)", b=128),
                    {"allow_slow_non_contiguous": True})])
        for i in range(NT):
            gpsimd.wait_ge(s_h2, i + 1)
            dh2.issue(gpsimd, i, [(d["h2buf"][i * 128:(i + 1) * 128, :],
                                   sh2[:, (i % 2) * E:(i % 2) * E + E], {})])
        gpsimd.wait_ge(s_xe, 1)
        gpsimd.dma_start(out=d["cc_xm_in"][:], in_=sxo[0:1, 0:E]).then_inc(p.sem("c_xmd"), 16)
        gpsimd.wait_ge(p.sem("c_xmd"), 16)
        if not NOCC:
            gpsimd.collective_compute(
                "AllGather", ALU.bypass, replica_groups=[[0, 1, 2, 3, 4, 5, 6, 7]],
                ins=[d["cc_xm_in"].ap().opt()], outs=[d["cc_xm_out"].ap().opt()],
            ).then_inc(s_cc, 1)
        else:
            gpsimd.nop().then_inc(s_cc, 1)
        gpsimd.wait_ge(s_msk, 1)
        dms.issue(gpsimd, 0, [(d["mrow"][:], srowb[:], {}),
                              (d["sclr"][0:1, 0:1], sdp[0:1, 19:20], {})])
        def issue_dwg(bh):
            b, hb = bh // NHB, bh % NHB
            dwg.issue(gpsimd, bh, [
                (swg[:, (bh % 2) * 8 * 512 + cc * 512:(bh % 2) * 8 * 512 + (cc + 1) * 512],
                 d["w_gate"][cc * 128:(cc + 1) * 128, hb * 512:(hb + 1) * 512], {})
                for cc in range(8)] + [
                (swu[:, (bh % 2) * 8 * 512 + cc * 512:(bh % 2) * 8 * 512 + (cc + 1) * 512],
                 d["w_up"][cc * 128:(cc + 1) * 128, hb * 512:(hb + 1) * 512], {})
                for cc in range(8)])

        issue_dwg(0)
        for kk in range(NKK + 2):
            if 0 <= kk - 2 < NKK:
                k2 = kk - 2
                b, hb, sub, i = kk_parts(k2)
                gpsimd.wait_ge(s_hd, k2 + 1)
                dhs.issue(gpsimd, k2, [
                    (d["hidbuf"][i * 128:(i + 1) * 128, hb * 512:(hb + 1) * 512],
                     shb[:, (k2 % 2) * 512:(k2 % 2) * 512 + 512], {})])
            if kk % 4 == 0 and kk // 4 + 1 < NBLK * NHB and kk < NKK:
                if kk >= 4:
                    gpsimd.wait_ge(s_gm, 2 * 4 * (kk // 4))
                issue_dwg(kk // 4 + 1)
        for i in range(NT):
            b, sub = i // 4, i % 4
            if sub == 0:
                for cc in range(36):
                    sidx = b * 36 + cc
                    if sidx >= 6:
                        gpsimd.wait_ge(s_dc, sidx - 5)
                    dwd.issue(gpsimd, sidx, [
                        (swd[:, (sidx % 6) * E:(sidx % 6) * E + E],
                         d["w_down"][cc * 128:(cc + 1) * 128, :], {})])
            gpsimd.wait_ge(s_xw, i + 1)
            dxo.issue(gpsimd, i, [(d["xcur"][i * 128:(i + 1) * 128, :],
                                   sxo[:, (i % 2) * E:(i % 2) * E + E], {})])
        p.drains(gpsimd, [dh2, dhs, dxo, dms])

    @block.scalar
    def _(scalar):
        scalar.wait_ge(s_ms, 1)
        for i in range(NT):
            dy1.wait(scalar, i)
            scalar.activation(sjunk[:], sy[:, (i % 2) * E:(i % 2) * E + E],
                              AF.Square, accum_out=Rc(0, i)).then_inc(s_sq, 1)
        scalar.wait_ge(s_al, 1)
        scalar.activation(R(2), R(1), AF.Sqrt).then_inc(s_al, 1)
        scalar.wait_ge(s_dp, 1)
        scalar.activation(sdpb[:, 32:36], pxm[0][:, 0:4], AF.Silu)
        scalar.activation(sdpb[:, 36:40], pxm[1][:, 0:4], AF.Silu).then_inc(s_dp, 1)
        scalar.wait_ge(s_dp, 3)
        scalar.activation(sdp[0:1, 8:12], pd[0][0:1, 0:4], AF.Sigmoid).then_inc(s_dp, 1)
        for kk in range(NKK):
            scalar.wait_ge(s_gm, 2 * kk + 2)
            if kk >= 2:
                scalar.wait_ge(s_hd, kk - 1)
            scalar.activation(shid[:, (kk % 2) * 512:(kk % 2) * 512 + 512],
                              pg[kk % 2][:], AF.Silu).then_inc(s_sl, 1)
        scalar.wait_ge(s_rm, 1)
        scalar.activation(srms[:, 32:64], srms[:, 0:32], AF.Sqrt).then_inc(s_rm, 1)

    @block.vector
    def _(vector):
        vector.memset(sstat[:], 0.0)
        vector.memset(ssq[:], 0.0).then_inc(s_ms, 1)
        stv = selfsync(p, vector, "c_ssv")
        vector.wait_ge(s_sq, NT)
        stv(vector.tensor_scalar(R(1), R(0), 1.0 / E, 1e-6, ALU.mult, ALU.add))
        vector.nop().then_inc(s_al, 1)
        vector.wait_ge(s_al, 2)
        stv(vector.tensor_scalar(R(3), R(2), 1e-6, 1e6, ALU.max, ALU.min))
        stv(vector.reciprocal(R(3), R(3)))
        vector.nop().then_inc(s_al, 1)
        dgr.wait(vector, 0)
        for i in range(NT):
            dy2.wait(vector, i)
            vector.wait_ge(s_al, 3)
            if i >= 2:
                dh2.wait(vector, i - 2)
            vector.tensor_scalar_mul(sy2[:, (i % 2) * E:(i % 2) * E + E],
                                     sy2[:, (i % 2) * E:(i % 2) * E + E],
                                     Rc(3, i))
            vector.tensor_tensor(sh2[:, (i % 2) * E:(i % 2) * E + E],
                                 sy2[:, (i % 2) * E:(i % 2) * E + E],
                                 sg2[:], ALU.mult).then_inc(s_h2, 1)
        vector.wait_ge(s_xm, 1)
        stv(vector.tensor_copy(sxo[0:1, 0:512], pxm[0][0:1, :]))
        stv(vector.tensor_copy(sxo[0:1, 512:1024], pxm[1][0:1, :]))
        vector.nop().then_inc(s_xe, 1)
        ddp.wait(vector, 0)
        vector.nop().then_inc(p.sem("c_dpw"), 1)
        dxm.wait(vector, 0)
        stv(vector.tensor_tensor(
            sdp[:, 0:32],
            sxm[:].rearrange("p (c r) -> p c r", r=8)[:, :, 0:7:2].rearrange("p c r -> p (c r)"),
            sxm[:].rearrange("p (c r) -> p c r", r=8)[:, :, 1:8:2].rearrange("p c r -> p (c r)"),
            ALU.add))
        stv(vector.tensor_scalar_mul(sdp[:, 0:32], sdp[:, 0:32], 1.0 / L))
        stv(vector.tensor_copy(sdpb[:, 0:32], sdp[:, 0:32]))
        vector.nop().then_inc(p.sem("c_xmr"), 1)
        vector.wait_ge(s_dp, 4)
        stv(vector.tensor_scalar(sdp[0:1, 12:16], sdp[0:1, 8:12], 0.5, None, ALU.add))
        stv(vector.tensor_scalar(sdp[0:1, 12:16], sdp[0:1, 12:16], 0.5, 1.5, ALU.max, ALU.min))
        stv(vector.reduce_sum(sdp[0:1, 16:17], sdp[0:1, 12:16], axis=AX.X))
        stv(vector.tensor_scalar_mul(sdp[0:1, 16:17], sdp[0:1, 16:17], 0.25 * 3072.0))
        stv(vector.tensor_scalar(sdp[0:1, 17:18], sdp[0:1, 16:17], 8388608.0, 8388608.0,
                                 ALU.add, ALU.subtract))
        stv(vector.tensor_tensor(sdp[0:1, 20:21], sdp[0:1, 17:18], sdp[0:1, 16:17],
                                 ALU.is_gt))
        stv(vector.tensor_tensor(sdp[0:1, 18:19], sdp[0:1, 17:18], sdp[0:1, 20:21],
                                 ALU.subtract))
        stv(vector.tensor_scalar_max(sdp[0:1, 18:19], sdp[0:1, 18:19], 1.0))
        stv(vector.reciprocal(sdp[0:1, 19:20], sdp[0:1, 18:19]))
        stv(vector.tensor_scalar(srow[:], c["iota_row"][:], -1.0, sdp[0:1, 18:19],
                                 ALU.mult, ALU.add))
        stv(vector.tensor_scalar(srow[:], srow[:], 0.0, 1.0, ALU.max, ALU.min))
        stv(vector.tensor_copy(srowb[:], srow[:]))
        vector.nop().then_inc(s_msk, 1)
        for kk in range(NKK):
            b, hb, sub, i = kk_parts(kk)
            vector.wait_ge(s_sl, kk + 1)
            if kk == 0:
                dmr.wait(vector, 0)
            if kk >= 2:
                dhs.wait(vector, kk - 2)
            sl_ = shid[:, (kk % 2) * 512:(kk % 2) * 512 + 512]
            sb_ = shb[:, (kk % 2) * 512:(kk % 2) * 512 + 512]
            vector.tensor_tensor(sl_, sl_, pu[kk % 2][:], ALU.mult)
            vector.tensor_tensor(sl_, sl_, smask[:, hb * 512:(hb + 1) * 512], ALU.mult)
            vector.tensor_tensor(sb_, sl_, sgh[:, hb * 512:(hb + 1) * 512], ALU.mult)
            vector.scalar_tensor_tensor(
                sl_, sl_, 1.0, sl_, ALU.mult, ALU.mult,
                accum_out=ssq[:, i * NHB + hb:i * NHB + hb + 1]).then_inc(s_hd, 1)
        vector.wait_ge(s_hd, NKK)
        stv(vector.tensor_reduce(srms[:, 64:96],
                                 ssq[:].rearrange("p (i h) -> p i h", h=NHB),
                                 axis=AX.X, op=ALU.add))
        dmr.wait(vector, 0)
        stv(vector.tensor_scalar(srms[:, 0:32], srms[:, 64:96], srms[:, 96:97], 1e-6,
                                 ALU.mult, ALU.add))
        vector.nop().then_inc(s_rm, 1)
        vector.wait_ge(s_rm, 2)
        stv(vector.tensor_scalar(srms[:, 32:64], srms[:, 32:64], 1e-6, 1e6, ALU.max, ALU.min))
        stv(vector.reciprocal(srms[:, 32:64], srms[:, 32:64]))
        vector.nop().then_inc(s_rm, 1)
        for i in range(NT):
            dyx.wait(vector, i)
            vector.wait_ge(s_dc, 36 * (i // 4 + 1))
            vector.wait_ge(s_rm, 3)
            if i >= 2:
                dxo.wait(vector, i - 2)
            o = (i % 2) * E
            vector.tensor_tensor(syx[:, o:o + E], syx[:, o:o + E],
                                 sxi[:, o:o + E], ALU.add)
            banks = [pg[0], pg[1], pu[0], pu[1], pxm[0], pxm[1], pd[0], pd[1]]
            vector.scalar_tensor_tensor(sxo[:, o:o + 512], banks[(i % 4) * 2][:],
                                        srms[:, 32 + i:33 + i],
                                        syx[:, o:o + 512], ALU.mult, ALU.add)
            vector.scalar_tensor_tensor(sxo[:, o + 512:o + E], banks[(i % 4) * 2 + 1][:],
                                        srms[:, 32 + i:33 + i],
                                        syx[:, o + 512:o + E], ALU.mult,
                                        ALU.add).then_inc(s_xw, 1)

    @block.tensor
    def _(tensor):
        for i in range(NT):
            tensor.wait_ge(s_h2, i + 1)
            for half in range(2):
                mm = tensor.matmul(
                    pxm[half][0:1, :], c["ones"][:, 0:1],
                    sh2[:, (i % 2) * E + half * 512:(i % 2) * E + half * 512 + 512],
                    start=(i == 0), stop=(i == NT - 1))
            if i == NT - 1:
                mm.then_inc(s_xm, 1)
        tensor.wait_ge(p.sem("c_dpw"), 1)
        tensor.wait_ge(p.sem("c_xmr"), 1)
        tensor.wait_ge(s_xe, 1)
        mm = None
        for half in range(2):
            for cc in range(8):
                mm = tensor.matmul(pxm[half][:, 0:4],
                                   swdp1[:, cc * 256 + half * 128:cc * 256 + half * 128 + 128],
                                   sdpb[:, cc * 4:(cc + 1) * 4],
                                   start=(cc == 0), stop=(cc == 7))
        mm.then_inc(s_dp, 1)
        tensor.wait_ge(s_dp, 2)
        tensor.matmul(pd[0][0:1, 0:4], swdp2[:, 0:1], sdpb[:, 32:36], start=True, stop=False)
        tensor.matmul(pd[0][0:1, 0:4], swdp2[:, 1:2], sdpb[:, 36:40],
                      start=False, stop=True).then_inc(s_dp, 1)
        for kk in range(NKK):
            b, hb, sub, i = kk_parts(kk)
            if sub == 0:
                dwg.wait(tensor, b * NHB + hb)
                if hb == 0:
                    dht2.wait(tensor, b)
            if kk >= 2:
                tensor.wait_ge(s_hd, kk - 1)
            mm = None
            for cc in range(8):
                mm = tensor.matmul(
                    pg[kk % 2][:],
                    sh2t[b % 2][:, cc * 512 + sub * 128:cc * 512 + sub * 128 + 128],
                    swg[:, ((b * NHB + hb) % 2) * 8 * 512 + cc * 512:
                        ((b * NHB + hb) % 2) * 8 * 512 + (cc + 1) * 512],
                    start=(cc == 0), stop=(cc == 7))
            mm.then_inc(s_gm, 1)
            mm = None
            for cc in range(8):
                mm = tensor.matmul(
                    pu[kk % 2][:],
                    sh2t[b % 2][:, cc * 512 + sub * 128:cc * 512 + sub * 128 + 128],
                    swu[:, ((b * NHB + hb) % 2) * 8 * 512 + cc * 512:
                        ((b * NHB + hb) % 2) * 8 * 512 + (cc + 1) * 512],
                    start=(cc == 0), stop=(cc == 7))
            mm.then_inc(s_gm, 1)
        tensor.wait_ge(s_rm, 1)
        banks = [pg[0], pg[1], pu[0], pu[1], pxm[0], pxm[1], pd[0], pd[1]]
        for b in range(NBLK):
            if b >= 1:
                tensor.wait_ge(s_xw, 4 * b)
            mm = None
            for cc in range(36):
                sidx = b * 36 + cc
                dwd.wait(tensor, sidx)
                dhl.wait(tensor, sidx)
                for sub in range(4):
                    for half in range(2):
                        mm = tensor.matmul(
                            banks[sub * 2 + half][:],
                            sht[:, (sidx % 6) * 512 + sub * 128:
                                (sidx % 6) * 512 + sub * 128 + 128],
                            swd[:, (sidx % 6) * E + half * 512:
                                (sidx % 6) * E + half * 512 + 512],
                            start=(cc == 0), stop=(cc == 35))
                mm.then_inc(s_dc, 1)

    emit_ph_c_debug(p, block)
    ces.close()


def emit_ph_c_debug(p, block):
    d, dbg = p.dram, p.dbg
    if not dbg:
        return
    s_dbg = p.sem("s_dbgc")
    n = [0]

    @block.sync
    def _(sync):
        def cp(dst, src):
            sync.dma_start(out=dst, in_=src).then_inc(s_dbg, 16)
            n[0] += 1
        if "h2" in dbg:
            for s in range(4):
                sync.wait_ge(p.sem(f"c_dh2{s}"), 16 * (NT // 4))
            cp(dbg["h2"][:], d["h2buf"][:])
        if "sizes" in dbg:
            sync.wait_ge(p.sem("c_msk"), 1)
            cp(dbg["sizes"][0:1, 0:1], d["sclr"][0:1, 0:1])
        if "y_ad" in dbg:
            for s in range(2):
                sync.wait_ge(p.sem(f"c_dxo{s}"), 16 * (NT // 2))
            cp(dbg["y_ad"][:], d["xcur"][:])
        if n[0]:
            sync.wait_ge(s_dbg, 16 * n[0])



def ph_ffn(p, block, call):
    nc = p.nc
    d = p.dram
    c = p.consts
    NSL = 4 * S
    NMT = NSL // 128
    OPW = E + 2
    RG = [[0, 1, 2, 3, 4, 5, 6, 7]]

    if call == 0:
        for nm, shp, dt_ in [
                ("snraw", [NSL, E], BF16),
                ("cc_xn_in", [NLOC, E], BF16), ("cc_xn_out", [8 * NLOC, E], BF16),
                ("xnb", [L, E], BF16), ("xnbT", [E, L], BF16),
                ("edisp", [NSL, L], BF16),
                ("slotsn", [NSL, E], BF16),
                ("growf", [2, E], F32),
                ("mspadd", [128, 128], BF16), ("msf2", [16, 128], BF16),
                ("ccpadd", [128, 128], BF16), ("ccpair", [128, 128], BF16),
                ("cc_cn2", [128, 128], BF16),
                ("cc_c_in", [64, 128], BF16), ("cc_c_out", [512, 128], BF16),
                ("cc_op_in", [L, OPW], BF16), ("cc_ops", [NLOC, OPW], BF16),
                ("cc_op_out", [8 * NLOC, OPW], BF16),
                ("mbuf", [NLOC, OPW], BF16), ("pbuf", [NLOC, OPW], BF16)]:
            d[nm] = nc.dram_tensor(nm, shp, dt_)
    Pf = f"f{call}_"
    fes = contextlib.ExitStack()
    p.gr = fes.enter_context(nc.gpsimd.register(Pf + "gr"))

    def sb(name, shape, dtype, es=None):
        return (es or fes).enter_context(nc.sbuf_tensor(Pf + name, shape, dtype))

    def sem(name):
        return p.sem(Pf + name)

    def DS(name, n, per=1):
        return Dstr(p, Pf + name, n, per=per)

    smsc = sb("msc", [128, 128], F32)
    sccl = sb("ccl", [128, 128], F32)
    mspadb = sb("mspadb", [128, 128], BF16)
    sfrow = sb("sfrow", [128, NSL], BF16)
    smsrowb = sb("smsrowb", [128, NSL], BF16)
    scmT = sb("scmT", [128, 64], F32)
    scol = sb("scol", [128, 64], F32)
    sgg = sb("sgg", [128, 8], F32)
    sgn32 = sb("sgn32", [128, 8], F32)
    sb1 = sb("sb1", [128, 128], F32)
    strt = sb("strt", [128, 128], BF16)

    s_gio = sem("gio")
    s_tr = sem("tr"); s_te = sem("te"); s_xtr = sem("xtr")
    s_sq = sem("sq"); s_al = sem("al"); s_xn = sem("xn"); s_cc1 = sem("cc1")
    s_xb = sem("xb"); s_gg = sem("gg"); s_init = sem("init")
    s_sn0 = sem("sn0"); s_sn1 = sem("sn1"); s_sn2 = sem("sn2"); s_snr = sem("snr")
    s_sjf = sem("sjf"); s_lg = sem("lg"); s_le = sem("le"); s_mx = sem("mx")
    s_ex = sem("ex")
    s_mt = sem("mt"); s_mk = sem("mk"); s_sf = sem("sf"); s_fc = sem("fc")
    s_rd = sem("rd")
    s_sl = sem("sl"); s_se = sem("se"); s_cs = sem("cs"); s_cl = sem("cl")
    s_ct = sem("ct"); s_cg = sem("cg"); s_cc2 = sem("cc2"); s_sc2 = sem("sc2")
    s_sn3 = sem("sn3"); s_sld = sem("sld")
    s_sf2 = sem("sf2"); s_hm = sem("hm"); s_he = sem("he"); s_em = sem("em")
    s_ee = sem("ee")
    s_cb = sem("cb"); s_oe = sem("oe"); s_cc3 = sem("cc3"); s_f5 = sem("f5")

    xes = contextlib.ExitStack()
    sstat = sb("stat", [128, 8 * 128], F32, xes)
    sjunk = sb("junk", [128, E], F32, xes)
    sx = sb("sx", [128, 2 * E], F32, xes)
    sxn = sb("sxn", [128, 2 * E], BF16, xes)
    sgr = sb("sgr", [1, 2 * E], F32, xes)
    srt = sb("srt", [1, 2 * E], F32, xes)
    dx = DS("dx", 2)
    dxn = DS("dxn", 2)

    @block.vector
    def _(vector):
        vector.memset(sstat[:], 0.0)
        vector.memset(smsc[:], 0.0)
        vector.memset(mspadb[:], 0.0).then_inc(s_init, 1)
        vector.wait_ge(s_gio, 32)
        stg = selfsync(p, vector, Pf + "ssv")
        stg(vector.tensor_tensor(srt[0:1, 0:E], sgr[0:1, 0:E], sgr[0:1, E:2 * E],
                                 ALU.mult))
        stg(vector.tensor_scalar_mul(srt[0:1, 0:E], srt[0:1, 0:E], 1024.0))
        stg(vector.tensor_scalar_mul(srt[0:1, E:2 * E], sgr[0:1, 0:E], 32.0))
        vector.nop().then_inc(s_gg, 1)

    @block.gpsimd
    def _(gpsimd):
        gpsimd.dma_start(out=sgr[0:1, 0:E], in_=d["gn_row"][:]).then_inc(s_gio, 16)
        gpsimd.dma_start(out=sgr[0:1, E:2 * E],
                         in_=d["gs_row"][:]).then_inc(s_gio, 16)
        gpsimd.wait_ge(s_gg, 1)
        gpsimd.dma_start(out=d["growf"][0:1, :],
                         in_=srt[0:1, 0:E]).then_inc(s_gio, 16)
        gpsimd.dma_start(out=d["growf"][1:2, :],
                         in_=srt[0:1, E:2 * E]).then_inc(s_gio, 16)
        gpsimd.wait_ge(s_gio, 64)
        gpsimd.dma_start(out=sgg[:],
                         in_=bass.AP(d["growf"], 0, [[1, 128], [128, 8]]),
                         allow_slow_non_contiguous=True).then_inc(s_gio, 16)
        gpsimd.dma_start(out=sgn32[:],
                         in_=bass.AP(d["growf"], E, [[1, 128], [128, 8]]),
                         allow_slow_non_contiguous=True).then_inc(s_gio, 16)
        gpsimd.wait_ge(s_gio, 96)
        gpsimd.nop().then_inc(s_gg, 1)

    @block.sync
    def _(sync):
        for i in range(NT):
            if i >= 2:
                sync.wait_ge(s_sq, 2 * (i - 1))
            dx.issue(sync, i, [(sx[:, (i % 2) * E:(i % 2) * E + E],
                                d["xcur"][i * 128:(i + 1) * 128, :], {})])
        for i in range(NT):
            if i >= 2:
                sync.wait_ge(s_xn, i - 1)
            dx.issue(sync, NT + i, [(sx[:, (i % 2) * E:(i % 2) * E + E],
                                     d["xcur"][i * 128:(i + 1) * 128, :], {})])

    @block.scalar
    def _(scalar):
        scalar.wait_ge(s_init, 1)
        for i in range(NT):
            dx.wait(scalar, i)
            xt = sx[:, (i % 2) * E:(i % 2) * E + E]
            scalar.activation(sjunk[:], xt, AF.Copy,
                              accum_out=sstat[:, i:i + 1]).then_inc(s_sq, 1)
            scalar.activation(sjunk[:], xt, AF.Square,
                              accum_out=sstat[:, 128 + i:128 + i + 1]).then_inc(s_sq, 1)
        scalar.wait_ge(s_al, 1)
        scalar.activation(sstat[:, 5 * 128:6 * 128], sstat[:, 4 * 128:5 * 128],
                          AF.Sqrt)
        scalar.activation(sstat[:, 6 * 128:7 * 128], sstat[:, 3 * 128:4 * 128],
                          AF.Sqrt).then_inc(s_al, 1)

    @block.vector
    def _(vector):
        R = lambda j: sstat[:, j * 128:(j + 1) * 128]
        stv = selfsync(p, vector, Pf + "ssv")
        vector.wait_ge(s_sq, 2 * NT)
        stv(vector.tensor_scalar_mul(R(2), R(0), 1.0 / E))
        stv(vector.tensor_scalar_mul(R(3), R(1), 1.0 / E))
        stv(vector.tensor_tensor(R(4), R(2), R(2), ALU.mult))
        stv(vector.tensor_tensor(R(3), R(3), R(4), ALU.subtract))
        stv(vector.tensor_scalar_add(R(4), R(3), 1e-5))
        stv(vector.reciprocal(R(7), R(4)))
        stv(vector.tensor_scalar_mul(R(3), R(3), float(E)))
        stv(vector.tensor_tensor(R(3), R(3), R(7), ALU.mult))
        vector.nop().then_inc(s_al, 1)
        vector.wait_ge(s_al, 2)
        stv(vector.reciprocal(R(5), R(5)))
        stv(vector.tensor_scalar_max(R(6), R(6), 1e-12))
        stv(vector.reciprocal(R(6), R(6)))
        stv(vector.tensor_tensor(R(6), R(6), R(5), ALU.mult))
        stv(vector.scalar_tensor_tensor(R(7), R(2), -1.0, R(6), ALU.mult,
                                        ALU.mult))
        vector.nop().then_inc(s_al, 1)
        for i in range(NT):
            vector.wait_ge(s_al, 3)
            dx.wait(vector, NT + i)
            if i >= 2:
                dxn.wait(vector, i - 2)
            vector.tensor_scalar(sxn[:, (i % 2) * E:(i % 2) * E + E],
                                 sx[:, (i % 2) * E:(i % 2) * E + E],
                                 sstat[:, 6 * 128 + i:6 * 128 + i + 1],
                                 sstat[:, 7 * 128 + i:7 * 128 + i + 1],
                                 ALU.mult, ALU.add).then_inc(s_xn, 1)

    @block.gpsimd
    def _(gpsimd):
        for i in range(NT):
            gpsimd.wait_ge(s_xn, i + 1)
            dxn.issue(gpsimd, i, [(d["cc_xn_in"][i * 128:(i + 1) * 128, :],
                                   sxn[:, (i % 2) * E:(i % 2) * E + E], {})])
        p.drains(gpsimd, [dxn])
        if not NOCC:
            gpsimd.collective_compute(
                "AllGather", ALU.bypass, replica_groups=RG,
                ins=[d["cc_xn_in"].ap().opt()], outs=[d["cc_xn_out"].ap().opt()],
            ).then_inc(s_cc1, 1)
        else:
            gpsimd.nop().then_inc(s_cc1, 1)
        gpsimd.wait_ge(s_cc1, 1)
        if NOCC:
            gpsimd.dma_start(out=d["xnb"][0:NLOC, :],
                             in_=d["cc_xn_in"][:]).then_inc(s_xb, 16)
            gpsimd.dma_start(out=d["xnb"][NLOC:L, :],
                             in_=d["cc_xn_in"][:]).then_inc(s_xb, 16)
        else:
            gpsimd.reg_load(p.gr, c["offs"][0:1, 0:1])
            gpsimd.dma_start(out=d["xnb"][:],
                             in_=bass.AP(d["cc_xn_out"], p.gr, [[E, L], [1, E]])
                             ).then_inc(s_xb, 16)
        gpsimd.wait_ge(s_xb, 32 if NOCC else 16)
        gpsimd.nop().then_inc(sem("xbr"), 1)

    if call == 0:
        sse = sb("se_", [128, 2 * E], F32, xes)
        ssn = sb("ssn", [128, 2 * E], BF16, xes)
        sst2 = sb("sst2", [128, 32], F32, xes)
        dse = DS("dse", 2)
        dsn = DS("dsn", 2)

        @block.sync
        def _(sync):
            for i in range(NMT):
                if i >= 2:
                    sync.wait_ge(s_sn0, i - 1)
                dse.issue(sync, i, [(sse[:, (i % 2) * E:(i % 2) * E + E],
                                     d["slot_emb"][i * 128:(i + 1) * 128, :], {})])
            for i in range(NMT):
                if i >= 2:
                    sync.wait_ge(s_sn2, i - 1)
                dse.issue(sync, NMT + i,
                          [(sse[:, (i % 2) * E:(i % 2) * E + E],
                            d["slot_emb"][i * 128:(i + 1) * 128, :], {})])

        @block.scalar
        def _(scalar):
            for i in range(NMT):
                dse.wait(scalar, i)
                scalar.activation(sjunk[:], sse[:, (i % 2) * E:(i % 2) * E + E],
                                  AF.Square,
                                  accum_out=sst2[:, i:i + 1]).then_inc(s_sn0, 1)
            scalar.wait_ge(s_sn1, 1)
            scalar.activation(sst2[:, 16:32], sst2[:, 0:16],
                              AF.Sqrt).then_inc(s_sn1, 1)

        @block.vector
        def _(vector):
            stv = selfsync(p, vector, Pf + "ssv")
            vector.wait_ge(s_sn0, NMT)
            vector.nop().then_inc(s_sn1, 1)
            vector.wait_ge(s_sn1, 2)
            stv(vector.tensor_scalar_max(sst2[:, 16:32], sst2[:, 16:32], 1e-12))
            stv(vector.reciprocal(sst2[:, 16:32], sst2[:, 16:32]))
            vector.nop().then_inc(s_sn1, 1)
            for i in range(NMT):
                vector.wait_ge(s_sn1, 3)
                dse.wait(vector, NMT + i)
                if i >= 2:
                    dsn.wait(vector, i - 2)
                vector.tensor_scalar_mul(ssn[:, (i % 2) * E:(i % 2) * E + E],
                                         sse[:, (i % 2) * E:(i % 2) * E + E],
                                         sst2[:, 16 + i:17 + i]).then_inc(s_sn2, 1)

        @block.gpsimd
        def _(gpsimd):
            for i in range(NMT):
                gpsimd.wait_ge(s_sn2, i + 1)
                dsn.issue(gpsimd, i, [(d["snraw"][i * 128:(i + 1) * 128, :],
                                       ssn[:, (i % 2) * E:(i % 2) * E + E], {})])
            p.drains(gpsimd, [dsn])
            gpsimd.nop().then_inc(s_snr, 1)
    else:
        @block.gpsimd
        def _(gpsimd):
            gpsimd.nop().then_inc(s_snr, 1)

    sxr = sb("sxr", [128, 2 * E], BF16, xes)
    sxtT = sb("sxtT", [128, 8 * 512], BF16, xes)
    ptr = [xes.enter_context(nc.psum_tensor(Pf + f"pt{i}", [128, 512], F32))
           for i in range(2)]
    dxr = DS("dxr", 2)
    dxw = DS("dxw", 2, per=8)

    @block.sync
    def _(sync):
        sync.wait_ge(sem("xbr"), 1)
        for i in range(64):
            if i >= 2:
                sync.wait_ge(s_tr, 8 * (i - 1))
            dxr.issue(sync, i, [(sxr[:, (i % 2) * E:(i % 2) * E + E],
                                 d["xnb"][i * 128:(i + 1) * 128, :], {})])

    @block.tensor
    def _(tensor):
        for i in range(64):
            dxr.wait(tensor, i)
            for dc in range(8):
                g = i * 8 + dc
                if g >= 2:
                    tensor.wait_ge(s_te, g - 1)
                tensor.matmul(ptr[g % 2][:, 0:128],
                              sxr[:, (i % 2) * E + dc * 128:
                                  (i % 2) * E + (dc + 1) * 128],
                              c["identb"][:],
                              start=True, stop=True).then_inc(s_tr, 1)

    @block.scalar
    def _(scalar):
        for i in range(64):
            r = i // 4
            if i % 4 == 0 and r >= 1:
                dxw.wait(scalar, r - 1)
            for dc in range(8):
                g = i * 8 + dc
                scalar.wait_ge(s_tr, g + 1)
                scalar.activation(
                    sxtT[:, dc * 512 + (i % 4) * 128:
                         dc * 512 + (i % 4) * 128 + 128],
                    ptr[g % 2][:, 0:128], AF.Copy).then_inc(s_te, 1)

    @block.gpsimd
    def _(gpsimd):
        for r in range(16):
            gpsimd.wait_ge(s_te, 8 * 4 * (r + 1))
            dxw.issue(gpsimd, r, [
                (d["xnbT"][dc * 128:(dc + 1) * 128, r * 512:(r + 1) * 512],
                 sxtT[:, dc * 512:(dc + 1) * 512], {}) for dc in range(8)])
        p.drains(gpsimd, [dxw])
        gpsimd.nop().then_inc(s_xtr, 1)
    xes.close()

    aes = contextlib.ExitStack()
    ssj = sb("ssj", [128, 2 * 16 * 128], BF16, aes)
    sxt = sb("sxt", [128, 2 * 8 * 1024], BF16, aes)
    slb = sb("slb", [128, 3 * L], F32, aes)
    seb = sb("seb", [128, 2 * L], BF16, aes)
    pl = [aes.enter_context(nc.psum_tensor(Pf + f"pl{i}", [128, 512], F32))
          for i in range(4)]
    dsj = DS("dsj", 2, per=16)
    dxt = DS("dxt", 2, per=8)
    des = DS("des", 2)

    def ssj_t(pp, jj, t):
        return ssj[:, ((pp % 2) * 16 + jj * 8 + t) * 128:
                   ((pp % 2) * 16 + jj * 8 + t) * 128 + 128]

    def sxt_t(g2, t):
        return sxt[:, ((g2 % 2) * 8 + t) * 1024:((g2 % 2) * 8 + t) * 1024 + 1024]

    def slb_r(j):
        return slb[:, (j % 3) * L:(j % 3) * L + L]

    @block.sync
    def _(sync):
        sync.wait_ge(s_xtr, 1)
        sync.wait_ge(s_snr, 1)
        sync.dma_start(out=sb1[:], in_=d["b1c"][:]).then_inc(s_gio, 16)
        for pp in range(8):
            if pp >= 2:
                sync.wait_ge(s_lg, 32 * (pp - 1))
            dsj.issue(sync, pp, [
                (ssj_t(pp, jj, t),
                 d["snraw"][(2 * pp + jj) * 128:(2 * pp + jj) * 128 + 128,
                            t * 128:(t + 1) * 128],
                 {"transpose": True}) for jj in range(2) for t in range(8)])
            for np2 in range(8):
                g2 = pp * 8 + np2
                if g2 >= 2:
                    sync.wait_ge(s_lg, 4 * (g2 - 1))
                dxt.issue(sync, g2, [
                    (sxt_t(g2, t),
                     d["xnbT"][t * 128:(t + 1) * 128,
                               np2 * 1024:(np2 + 1) * 1024], {})
                    for t in range(8)])

    @block.vector
    def _(vector):
        def fold(pp):
            dsj.wait(vector, pp)
            vector.wait_ge(s_gg, 2)
            for jj in range(2):
                for t in range(8):
                    vector.tensor_scalar_mul(ssj_t(pp, jj, t), ssj_t(pp, jj, t),
                                             sgg[:, t:t + 1]).then_inc(s_sjf, 1)
        fold(0)
        fold(1)
        for pp in range(8):
            for jj in range(2):
                j = 2 * pp + jj
                vector.wait_ge(s_le, 32 * (pp + 1))
                stv = selfsync(p, vector, Pf + "ssv")
                stv(vector.reduce_max(smsc[:, 48 + j:49 + j], slb_r(j), axis=AX.X))
                stv(vector.tensor_copy(mspadb[:, j:j + 1], smsc[:, 48 + j:49 + j]))
                stv(vector.tensor_copy(smsc[:, 48 + j:49 + j], mspadb[:, j:j + 1]))
                stv(vector.tensor_scalar_mul(smsc[:, j:j + 1],
                                             smsc[:, 48 + j:49 + j], -1.0))
                vector.nop().then_inc(s_mx, 1)
            if pp + 2 < 8:
                fold(pp + 2)

    @block.tensor
    def _(tensor):
        for pp in range(8):
            dsj.wait(tensor, pp)
            tensor.wait_ge(s_sjf, 16 * (pp + 1))
            for nb in range(16):
                g2 = pp * 8 + nb // 2
                if nb % 2 == 0:
                    dxt.wait(tensor, g2)
                for jj in range(2):
                    gg = (pp * 16 + nb) * 2 + jj
                    if gg >= 4:
                        tensor.wait_ge(s_le, gg - 3)
                    bank = pl[(nb % 2) * 2 + jj]
                    mm = None
                    for t in range(8):
                        mm = tensor.matmul(
                            bank[:], ssj_t(pp, jj, t),
                            sxt_t(g2, t)[:, (nb % 2) * 512:(nb % 2) * 512 + 512],
                            start=(t == 0), stop=(t == 7))
                    mm.then_inc(s_lg, 1)

    @block.scalar
    def _(scalar):
        for pp in range(8):
            for nb in range(16):
                g = pp * 16 + nb
                for jj in range(2):
                    gg = g * 2 + jj
                    j = 2 * pp + jj
                    scalar.wait_ge(s_lg, gg + 1)
                    if j >= 3:
                        scalar.wait_ge(s_ex, j - 2)
                    scalar.activation(slb_r(j)[:, nb * 512:(nb + 1) * 512],
                                      pl[(nb % 2) * 2 + jj][:],
                                      AF.Copy).then_inc(s_le, 1)
            for jj in range(2):
                j = 2 * pp + jj
                scalar.wait_ge(s_mx, j + 1)
                if j >= 2:
                    for q in range(4):
                        des.wait(scalar, 4 * (j - 2) + q)
                scalar.activation(seb[:, (j % 2) * L:(j % 2) * L + L],
                                  slb_r(j), AF.Exp,
                                  bias=smsc[:, j:j + 1],
                                  accum_out=smsc[:, 16 + j:17 + j]).then_inc(s_ex, 1)

    @block.gpsimd
    def _(gpsimd):
        for j in range(NMT):
            gpsimd.wait_ge(s_ex, j + 1)
            for q in range(4):
                des.issue(gpsimd, 4 * j + q,
                          [(d["edisp"][j * 128:(j + 1) * 128,
                                       q * 2048:(q + 1) * 2048],
                            seb[:, (j % 2) * L + q * 2048:
                                (j % 2) * L + (q + 1) * 2048], {})])
        p.drains(gpsimd, [des])
        gpsimd.nop().then_inc(s_ex, 1)
        gpsimd.wait_ge(s_mx, NMT)
        gpsimd.dma_start(out=d["mspadd"][:], in_=mspadb[:]).then_inc(s_mt, 16)
    aes.close()

    bes = contextlib.ExitStack()
    sxk = sb("sxk", [128, 2 * 4 * E], BF16, bes)
    set_ = sb("set", [128, 2 * 4 * 2048], BF16, bes)
    acc = sb("acc", [128, NMT * E], F32, bes)
    tmpE = sb("tmpE", [128, 2048], BF16, bes)
    sslt = sb("sslt", [128, 2 * E], BF16, bes)
    ps4 = [bes.enter_context(nc.psum_tensor(Pf + f"ps{i}", [128, 512], F32))
           for i in range(4)]
    dke = DS("dke", 2, per=8)
    dso = DS("dso", 2)

    def set_t(hs, t):
        return set_[:, ((hs % 2) * 4 + t) * 2048:((hs % 2) * 4 + t) * 2048 + 2048]

    def sxk_t(hs, t):
        return sxk[:, ((hs % 2) * 4 + t) * E:((hs % 2) * 4 + t) * E + E]

    @block.sync
    def _(sync):
        sync.wait_ge(s_mt, 16)
        sync.dma_start(out=strt[:], in_=d["mspadd"][:],
                       transpose=True).then_inc(sem("mts"), 16)
        sync.wait_ge(s_ex, 17)
        for hs in range(16):
            if hs >= 2:
                sync.wait_ge(s_sl, 16 * (hs // 2))
                sync.wait_ge(s_cs, 4 * (hs - 1))
            dke.issue(sync, hs, [(sxk_t(hs, t),
                                  d["xnb"][(hs * 4 + t) * 128:(hs * 4 + t + 1) * 128, :],
                                  {}) for t in range(4)] +
                      [(set_t(hs, t),
                        d["edisp"][:, (hs * 4 + t) * 128:(hs * 4 + t + 1) * 128],
                        {"transpose": True}) for t in range(4)])

    @block.gpsimd
    def _(gpsimd):
        gpsimd.wait_ge(sem("mts"), 16)
        gpsimd.dma_start(out=d["msf2"][:], in_=strt[0:16, :]).then_inc(s_mt, 16)
        gpsimd.wait_ge(s_mt, 32)
        gpsimd.dma_start(out=smsrowb[:],
                         in_=bass.AP(d["msf2"], 0, [[0, 128], [1, NSL]])
                         ).then_inc(s_mt, 16)

    @block.vector
    def _(vector):
        vector.wait_ge(s_mt, 48)
        stv = selfsync(p, vector, Pf + "ssv")
        stv(vector.reduce_max(smsc[:, 80:81], smsrowb[:], axis=AX.X))
        stv(vector.tensor_scalar(smsc[:, 81:82], smsc[:, 80:81], -1.0, 40.0,
                                 ALU.mult, ALU.add))
        stv(vector.tensor_scalar_add(smsc[:, 82:83], smsc[:, 80:81], -40.0))
        vector.nop().then_inc(s_mk, 1)
        vector.wait_ge(s_ex, NMT)
        stv(vector.reciprocal(smsc[:, 32:48], smsc[:, 16:32]))
        vector.nop().then_inc(s_rd, 1)

    @block.scalar
    def _(scalar):
        scalar.wait_ge(s_mk, 1)
        scalar.activation(sfrow[:], smsrowb[:], AF.Exp,
                          bias=smsc[:, 81:82]).then_inc(s_sf, 1)
        scalar.activation(smsc[:, 84:100], smsc[:, 48:64], AF.Exp,
                          bias=smsc[:, 81:82]).then_inc(s_fc, 1)

    @block.tensor
    def _(tensor):
        for k in range(8):
            for hs in (2 * k, 2 * k + 1):
                dke.wait(tensor, hs)
            for j in range(NMT):
                gid = k * 16 + j
                if gid >= 2:
                    tensor.wait_ge(s_se, gid - 1)
                for half in range(2):
                    bank = ps4[(gid % 2) * 2 + half]
                    mm = None
                    for hh2 in range(2):
                        hs = 2 * k + hh2
                        for t in range(4):
                            mm = tensor.matmul(
                                bank[:],
                                set_t(hs, t)[:, j * 128:(j + 1) * 128],
                                sxk_t(hs, t)[:, half * 512:half * 512 + 512],
                                start=(hh2 == 0 and t == 0),
                                stop=(hh2 == 1 and t == 3))
                    if half == 1:
                        mm.then_inc(s_sl, 1)

    @block.vector
    def _(vector):
        for k in range(8):
            for j in range(NMT):
                gid = k * 16 + j
                vector.wait_ge(s_sl, gid + 1)
                for half in range(2):
                    dst = acc[:, j * E + half * 512:j * E + half * 512 + 512]
                    bank = ps4[(gid % 2) * 2 + half]
                    if k == 0:
                        ins = vector.tensor_copy(dst, bank[:])
                    else:
                        ins = vector.tensor_tensor(dst, dst, bank[:], ALU.add)
                    if half == 1:
                        ins.then_inc(s_se, 1)
            vector.wait_ge(s_sf, 1)
            for hh2 in range(2):
                hs = 2 * k + hh2
                for t in range(4):
                    ci = hs * 4 + t
                    vector.tensor_tensor(tmpE[:], set_t(hs, t), sfrow[:],
                                         ALU.mult)
                    vector.reduce_max(sccl[:, ci:ci + 1], tmpE[:],
                                      axis=AX.X).then_inc(s_cs, 1)
        vector.wait_ge(s_rd, 1)
        for j in range(NMT):
            if j >= 2:
                dso.wait(vector, j - 2)
            vector.tensor_scalar_mul(sslt[:, (j % 2) * E:(j % 2) * E + E],
                                     acc[:, j * E:(j + 1) * E],
                                     smsc[:, 32 + j:33 + j]).then_inc(s_sn3, 1)
        vector.wait_ge(s_cl, 1)
        stc = selfsync(p, vector, Pf + "ssv")
        stc(vector.tensor_scalar(sccl[:, 64:128], sccl[:, 64:128],
                                 smsc[:, 82:83], None, ALU.add))
        stc(vector.tensor_copy(mspadb[:, 0:64], sccl[:, 64:128]))
        vector.nop().then_inc(sem("ctv"), 1)

    @block.scalar
    def _(scalar):
        scalar.wait_ge(s_cs, 64)
        scalar.activation(sccl[:, 64:128], sccl[:, 0:64],
                          AF.Ln).then_inc(s_cl, 1)

    @block.gpsimd
    def _(gpsimd):
        for j in range(NMT):
            gpsimd.wait_ge(s_sn3, j + 1)
            dso.issue(gpsimd, j, [(d["slotsn"][j * 128:(j + 1) * 128, :],
                                   sslt[:, (j % 2) * E:(j % 2) * E + E], {})])
        p.drains(gpsimd, [dso])
        gpsimd.nop().then_inc(s_sld, 1)
        gpsimd.wait_ge(sem("ctv"), 1)
        gpsimd.dma_start(out=d["ccpadd"][:], in_=mspadb[:]).then_inc(s_ct, 16)

    @block.sync
    def _(sync):
        sync.wait_ge(s_ct, 16)
        sync.dma_start(out=strt[:], in_=d["ccpadd"][:],
                       transpose=True).then_inc(sem("mts"), 16)

    @block.gpsimd
    def _(gpsimd):
        gpsimd.wait_ge(sem("mts"), 32)
        gpsimd.dma_start(out=d["cc_c_in"][:], in_=strt[0:64, :]).then_inc(s_ct, 16)
        gpsimd.dma_start(out=d["cc_cn2"][0:64, :],
                         in_=strt[0:64, :]).then_inc(s_ct, 16)
        gpsimd.dma_start(out=d["cc_cn2"][64:128, :],
                         in_=strt[0:64, :]).then_inc(s_ct, 16)
        gpsimd.wait_ge(s_ct, 64)
        if not NOCC:
            gpsimd.collective_compute(
                "AllGather", ALU.bypass, replica_groups=RG,
                ins=[d["cc_c_in"].ap().opt()], outs=[d["cc_c_out"].ap().opt()],
            ).then_inc(s_cc2, 1)
        else:
            gpsimd.nop().then_inc(s_cc2, 1)
    bes.close()

    les = contextlib.ExitStack()
    seo = sb("seo", [128, NMT * OPW], BF16, les)
    ces = contextlib.ExitStack()
    ssnT = sb("ssnT", [128, 2 * 8 * 512], BF16, ces)
    sw1 = sb("sw1", [128, 2 * 8 * 128], BF16, ces)
    sht = sb("sht", [128, 32 * 512], BF16, ces)
    sw2 = sb("sw2", [128, 2 * 512], BF16, ces)
    sb2 = sb("sb2", [128, E], F32, ces)
    steo = sb("steo", [128, 2 * 512], F32, ces)
    ph_ = [ces.enter_context(nc.psum_tensor(Pf + f"ph{i}", [128, 512], F32))
           for i in range(2)]
    peo = [ces.enter_context(nc.psum_tensor(Pf + f"pe{i}", [128, 512], F32))
           for i in range(4)]
    dsl = DS("dsl", 2, per=8)
    dw1 = DS("dw1", 2, per=8)
    dw2 = DS("dw2", 2)
    db2 = DS("db2", 1)

    def ssnT_t(q, dc):
        return ssnT[:, ((q % 2) * 8 + dc) * 512:((q % 2) * 8 + dc) * 512 + 512]

    def sw1_t(idx, dc):
        return sw1[:, ((idx % 2) * 8 + dc) * 128:((idx % 2) * 8 + dc) * 128 + 128]

    @block.sync
    def _(sync):
        sync.wait_ge(s_sld, 1)
        sync.wait_ge(p.sem("s_wcv", protect=True), 260)
        for q in range(4):
            if q >= 2:
                sync.wait_ge(s_hm, 32 * (q - 1))
            dsl.issue(sync, q, [
                (ssnT_t(q, dc),
                 d["slotsn"][q * 512:(q + 1) * 512, dc * 128:(dc + 1) * 128],
                 {"transpose": True}) for dc in range(8)])
            for ht in range(32):
                idx = q * 32 + ht
                if idx >= 2:
                    sync.wait_ge(s_hm, idx - 1)
                dw1.issue(sync, idx, [
                    (sw1_t(idx, dc),
                     d["w1b"][q * E + dc * 128:q * E + dc * 128 + 128,
                              ht * 128:(ht + 1) * 128], {})
                    for dc in range(8)])
            for half in range(2):
                for hc in range(32):
                    w2i = q * 64 + half * 32 + hc
                    if w2i >= 2:
                        sync.wait_ge(s_em, w2i - 1)
                    dw2.issue(sync, w2i, [
                        (sw2[:, (w2i % 2) * 512:(w2i % 2) * 512 + 512],
                         d["w2b"][q * HID + hc * 128:q * HID + hc * 128 + 128,
                                  half * 512:half * 512 + 512], {})])

    @block.gpsimd
    def _(gpsimd):
        for q in range(4):
            gpsimd.wait_ge(s_ee, 8 * q)
            db2.issue(gpsimd, q, [(sb2[:],
                                   bass.AP(d["b2"], q * E, [[0, 128], [1, E]]), {})])

    @block.vector
    def _(vector):
        vector.memset(seo[:], 0.0)
        vector.wait_ge(s_fc, 1)
        fview = seo.rearrange("p (s w) -> p s w", w=OPW)[:, :, 1024:1025] \
            .rearrange("p s w -> p (s w)")
        vector.tensor_copy(fview, smsc[:, 84:100]).then_inc(sem("eoc"), 1)
        for q in range(4):
            dsl.wait(vector, q)
            vector.wait_ge(s_gg, 2)
            for dc in range(8):
                vector.tensor_scalar_mul(ssnT_t(q, dc), ssnT_t(q, dc),
                                         sgn32[:, dc:dc + 1]).then_inc(s_sf2, 1)
            for half in range(2):
                for m in range(4):
                    sc = q * 4 + m
                    vector.wait_ge(s_em, q * 64 + (half + 1) * 32)
                    db2.wait(vector, q)
                    dst = steo[:, (m % 2) * 512:(m % 2) * 512 + 512]
                    ste = selfsync(p, vector, Pf + "ssv")
                    ste(vector.tensor_tensor(dst, peo[m][:],
                                             sb2[:, half * 512:half * 512 + 512],
                                             ALU.add))
                    vector.tensor_scalar_mul(
                        seo[:, sc * OPW + half * 512:sc * OPW + half * 512 + 512],
                        dst, smsc[:, 84 + sc:85 + sc]).then_inc(s_ee, 1)

    @block.tensor
    def _(tensor):
        for q in range(4):
            dsl.wait(tensor, q)
            tensor.wait_ge(s_sf2, 8 * (q + 1))
            for ht in range(32):
                idx = q * 32 + ht
                dw1.wait(tensor, idx)
                if idx >= 2:
                    tensor.wait_ge(s_he, idx - 1)
                mm = None
                for dc in range(8):
                    mm = tensor.matmul(ph_[idx % 2][:], sw1_t(idx, dc),
                                       ssnT_t(q, dc),
                                       start=(dc == 0), stop=(dc == 7))
                mm.then_inc(s_hm, 1)
            for half in range(2):
                qh = q * 2 + half
                if qh >= 1:
                    tensor.wait_ge(s_ee, 4 * qh)
                for hc in range(32):
                    w2i = q * 64 + half * 32 + hc
                    dw2.wait(tensor, w2i)
                    if hc == 0:
                        tensor.wait_ge(s_he, q * 32 + 32)
                    mm = None
                    for m in range(4):
                        mm = tensor.matmul(
                            peo[m][:],
                            sht[:, hc * 512 + m * 128:hc * 512 + m * 128 + 128],
                            sw2[:, (w2i % 2) * 512:(w2i % 2) * 512 + 512],
                            start=(hc == 0), stop=(hc == 31))
                    mm.then_inc(s_em, 1)

    @block.scalar
    def _(scalar):
        scalar.wait_ge(s_gio, 112)
        for q in range(4):
            for ht in range(32):
                idx = q * 32 + ht
                scalar.wait_ge(s_hm, idx + 1)
                if q >= 1:
                    scalar.wait_ge(s_em, 64 * q)
                scalar.activation(sht[:, ht * 512:(ht + 1) * 512],
                                  ph_[idx % 2][:], AF.Gelu,
                                  bias=sb1[:, idx:idx + 1]).then_inc(s_he, 1)
    ces.close()

    des2 = contextlib.ExitStack()
    sed = sb("sed", [128, 2 * 16 * 512], BF16, des2)
    sop = sb("sop", [128, 2 * OPW], BF16, des2)
    poA = [des2.enter_context(nc.psum_tensor(Pf + f"pa{i}", [128, 512], F32))
           for i in range(2)]
    poB = [des2.enter_context(nc.psum_tensor(Pf + f"pb{i}", [128, 512], F32))
           for i in range(2)]
    poD = [des2.enter_context(nc.psum_tensor(Pf + f"pd{i}", [128, 16], F32))
           for i in range(2)]
    dlt = DS("dlt", 4)
    dop = DS("dop", 2)

    @block.gpsimd
    def _(gpsimd):
        gpsimd.wait_ge(s_cc2, 1)
        if not NOCC:
            gpsimd.reg_load(p.gr, c["offs"][0:1, 2:3])
            gpsimd.dma_start(out=d["ccpair"][:],
                             in_=bass.AP(d["cc_c_out"], p.gr, [[128, 128], [1, 128]])
                             ).then_inc(sem("ccp"), 16)
        else:
            gpsimd.dma_start(out=d["ccpair"][:],
                             in_=d["cc_cn2"][:]).then_inc(sem("ccp"), 16)
        gpsimd.wait_ge(sem("ccp"), 16)
        gpsimd.nop().then_inc(s_cc2, 100)

    @block.sync
    def _(sync):
        sync.wait_ge(s_cc2, 100)
        sync.dma_start(out=strt[:], in_=d["ccpair"][:],
                       transpose=True).then_inc(s_cg, 16)

    @block.vector
    def _(vector):
        vector.wait_ge(s_cg, 16)
        vector.tensor_tensor(scmT[:], strt[:, 0:64], strt[:, 64:128],
                             ALU.max).then_inc(sem("ctv"), 1)

    @block.scalar
    def _(scalar):
        scalar.wait_ge(sem("ctv"), 2)
        scalar.activation(scol[:], scmT[:], AF.Exp, scale=-1.0,
                          bias=smsc[:, 82:83]).then_inc(s_sc2, 1)

    @block.sync
    def _(sync):
        sync.wait_ge(s_em, 256)
        sync.wait_ge(s_ee, 32)
        for mg in range(16):
            for sc in range(16):
                g = mg * 16 + sc
                if g >= 32:
                    sync.wait_ge(s_cb, 16 * 4 * (mg - 1))
                dlt.issue(sync, g, [
                    (sed[:, ((mg % 2) * 16 + sc) * 512:
                         ((mg % 2) * 16 + sc) * 512 + 512],
                     d["edisp"][sc * 128:(sc + 1) * 128,
                                mg * 512:(mg + 1) * 512], {})])

    @block.tensor
    def _(tensor):
        tensor.wait_ge(s_ee, 32)
        tensor.wait_ge(sem("eoc"), 1)
        for m in range(64):
            mg = m // 4
            if m >= 2:
                tensor.wait_ge(s_oe, m - 1)
            for sc in range(16):
                if m % 4 == 0:
                    dlt.wait(tensor, mg * 16 + sc)
                lt = sed[:, ((mg % 2) * 16 + sc) * 512 + (m % 4) * 128:
                         ((mg % 2) * 16 + sc) * 512 + (m % 4) * 128 + 128]
                tensor.matmul(poA[m % 2][:], lt, seo[:, sc * OPW:sc * OPW + 512],
                              start=(sc == 0), stop=(sc == 15))
                tensor.matmul(poB[m % 2][:], lt,
                              seo[:, sc * OPW + 512:sc * OPW + 1024],
                              start=(sc == 0), stop=(sc == 15))
                tensor.matmul(poD[m % 2][:, 0:2], lt,
                              seo[:, sc * OPW + 1024:sc * OPW + 1026],
                              start=(sc == 0), stop=(sc == 15)).then_inc(s_cb, 1)

    @block.scalar
    def _(scalar):
        scalar.wait_ge(s_sc2, 1)
        for m in range(64):
            scalar.wait_ge(s_cb, 16 * (m + 1))
            if m >= 2:
                dop.wait(scalar, m - 2)
            o = (m % 2) * OPW
            scalar.activation(sop[:, o:o + 512], poA[m % 2][:], AF.Copy,
                              scale=scol[:, m:m + 1])
            scalar.activation(sop[:, o + 512:o + 1024], poB[m % 2][:], AF.Copy,
                              scale=scol[:, m:m + 1])
            scalar.activation(sop[:, o + 1024:o + 1026], poD[m % 2][:, 0:2],
                              AF.Copy, scale=scol[:, m:m + 1]).then_inc(s_oe, 1)

    @block.gpsimd
    def _(gpsimd):
        for m in range(64):
            gpsimd.wait_ge(s_oe, m + 1)
            dop.issue(gpsimd, m, [(d["cc_op_in"][m * 128:(m + 1) * 128, :],
                                   sop[:, (m % 2) * OPW:(m % 2) * OPW + OPW], {})])
        p.drains(gpsimd, [dop])
        gpsimd.reg_load(p.gr, c["offs"][0:1, 6:7])
        gpsimd.dma_start(out=d["cc_ops"][:],
                         in_=bass.AP(d["cc_op_in"], p.gr, [[OPW, NLOC], [1, OPW]])
                         ).then_inc(sem("ops"), 16)
        gpsimd.wait_ge(sem("ops"), 16)
        if not NOCC:
            gpsimd.collective_compute(
                "AllGather", ALU.bypass, replica_groups=RG,
                ins=[d["cc_ops"].ap().opt()], outs=[d["cc_op_out"].ap().opt()],
            ).then_inc(s_cc3, 1)
        else:
            gpsimd.nop().then_inc(s_cc3, 1)
        gpsimd.wait_ge(s_cc3, 1)
        gpsimd.reg_load(p.gr, c["offs"][0:1, 4:5])
        gpsimd.dma_start(out=d["mbuf"][:],
                         in_=bass.AP(d["cc_op_in"], p.gr, [[OPW, NLOC], [1, OPW]])
                         ).then_inc(sem("ccp"), 16)
        if NOCC:
            gpsimd.reg_load(p.gr, c["offs"][0:1, 4:5])
            gpsimd.dma_start(out=d["pbuf"][:],
                             in_=bass.AP(d["cc_op_in"], p.gr,
                                         [[OPW, NLOC], [1, OPW]])
                             ).then_inc(sem("ccp"), 16)
        else:
            gpsimd.reg_load(p.gr, c["offs"][0:1, 5:6])
            gpsimd.dma_start(out=d["pbuf"][:],
                             in_=bass.AP(d["cc_op_out"], p.gr,
                                         [[OPW, NLOC], [1, OPW]])
                             ).then_inc(sem("ccp"), 16)
        gpsimd.wait_ge(sem("ccp"), 48)
        gpsimd.nop().then_inc(s_cc3, 100)
    des2.close()
    les.close()

    ees = contextlib.ExitStack()
    smi = sb("smi", [128, 2 * OPW], BF16, ees)
    spa = sb("spa", [128, 2 * OPW], BF16, ees)
    sxc = sb("sxc", [128, 2 * E], F32, ees)
    sf5 = sb("sf5", [128, 2 * OPW], F32, ees)
    dft = DS("dft", 2, per=3)
    dxo = DS("dxo", 2)

    @block.sync
    def _(sync):
        sync.wait_ge(s_cc3, 100)
        for i in range(NT):
            if i >= 2:
                sync.wait_ge(s_f5, i - 1)
            dft.issue(sync, i, [
                (smi[:, (i % 2) * OPW:(i % 2) * OPW + OPW],
                 d["mbuf"][i * 128:(i + 1) * 128, :], {}),
                (spa[:, (i % 2) * OPW:(i % 2) * OPW + OPW],
                 d["pbuf"][i * 128:(i + 1) * 128, :], {}),
                (sxc[:, (i % 2) * E:(i % 2) * E + E],
                 d["xcur"][i * 128:(i + 1) * 128, :], {})])

    @block.vector
    def _(vector):
        for i in range(NT):
            dft.wait(vector, i)
            if i >= 2:
                dxo.wait(vector, i - 2)
            o = (i % 2) * OPW
            t = sf5[:, o:o + OPW]
            stf = selfsync(p, vector, Pf + "ssv")
            stf(vector.tensor_tensor(t, smi[:, o:o + OPW], spa[:, o:o + OPW],
                                     ALU.add))
            stf(vector.reciprocal(t[:, 1024:1025], t[:, 1024:1025]))
            stf(vector.tensor_scalar_mul(t[:, 0:1024], t[:, 0:1024],
                                         t[:, 1024:1025]))
            vector.tensor_tensor(t[:, 0:1024], t[:, 0:1024],
                                 sxc[:, (i % 2) * E:(i % 2) * E + E],
                                 ALU.add).then_inc(s_f5, 1)

    @block.gpsimd
    def _(gpsimd):
        for i in range(NT):
            gpsimd.wait_ge(s_f5, i + 1)
            dxo.issue(gpsimd, i, [(d["xcur"][i * 128:(i + 1) * 128, :],
                                   sf5[:, (i % 2) * OPW:(i % 2) * OPW + 1024], {})])
        p.drains(gpsimd, [dxo])
        gpsimd.nop().then_inc(s_f5, 100)

    for name in ("sync", "vector", "scalar", "tensor"):
        def fin(eng):
            eng.wait_ge(s_f5, NT + 100)
        getattr(block, name)(fin)
    ees.close()

    if call == 0 and DEBUG:
        emit_ffn_debug(p, block)
    fes.close()


def emit_ffn_debug(p, block):
    d = p.dram
    sdb = p.sem("s_fdbg")
    n = [0]

    @block.sync
    def _(sync):
        def cp(dst, src):
            sync.dma_start(out=dst, in_=src).then_inc(sdb, 16)
            n[0] += 1
        if "xn" in p.dbg:
            cp(p.dbg["xn"][:], d["xnb"][0:NLOC, :])
        if "edisp" in p.dbg:
            cp(p.dbg["edisp"][:], d["edisp"][0:128, :])
        if "slots" in p.dbg:
            for ii in range(4):
                cp(p.dbg["slots"][ii * 512:(ii + 1) * 512, :],
                   d["slotsn"][ii * 512:(ii + 1) * 512, :])
        if "outp" in p.dbg:
            for ii in range(4):
                cp(p.dbg["outp"][ii * 2048:(ii + 1) * 2048, :],
                   d["cc_op_in"][ii * 2048:(ii + 1) * 2048, :])
        if "cn" in p.dbg:
            cp(p.dbg["cn"][:], d["ccpair"][:])
        if "mbufd" in p.dbg:
            cp(p.dbg["mbufd"][:], d["mbuf"][:])
        if "pbufd" in p.dbg:
            cp(p.dbg["pbufd"][:], d["pbuf"][:])
        if "dm" in p.dbg:
            cp(p.dbg["dm"][:], d["msf2"][:])
        if "ffn1" in p.dbg:
            cp(p.dbg["ffn1"][:], d["xcur"][:])
        sync.wait_ge(sdb, 16 * n[0])


def ph_stub(p, block):
    d = p.dram
    s_st = p.sem("s_stub", protect=True)

    @block.sync
    def _(sync):
        for i in range(4):
            sync.dma_start(out=d["xcur"][i * 1024:(i + 1) * 1024, :],
                           in_=d["x_in"][i * 1024:(i + 1) * 1024, :]).then_inc(s_st, 16)
        sync.wait_ge(s_st, 64)



def ph_attnb(p, block):
    nc = p.nc
    d = p.dram
    c = p.consts
    RG = [[0, 1, 2, 3, 4, 5, 6, 7]]
    VW = 132
    KQR = 9 * 128

    for nm, shp, dt_ in [
            ("ablnb", [NLOC, E], BF16),
            ("cc_ab_in", [KQR, NLOC], BF16), ("cc_ab_out", [8 * KQR, NLOC], BF16),
            ("ccvloc", [NLOC, VW], BF16), ("ccvgath", [8 * NLOC, VW], BF16),
            ("cc_o_in", [4 * L, DB], BF16), ("cc_o_out", [32 * L, DB], BF16),
            ("abo", [NLOC, E], BF16)]:
        d[nm] = nc.dram_tensor(nm, shp, dt_)

    fes = contextlib.ExitStack()
    p.gr = fes.enter_context(nc.gpsimd.register("ab_gr"))

    def sb(name, shape, dtype, es=None):
        return (es or fes).enter_context(nc.sbuf_tensor("ab_" + name, shape, dtype))

    def sem(name):
        return p.sem("ab_" + name)

    def DS(name, n, per=1):
        return Dstr(p, "ab_" + name, n, per=per)

    swk = sb("swk", [128, 8 * 128], BF16)
    swv = sb("swv", [128, 8 * 128], BF16)
    swq = sb("swq", [128, 64 * 128], BF16)
    swp = sb("swp", [128, 8 * E], BF16)
    sbp = sb("sbp", [128, E], F32)

    s_sq = sem("sq"); s_al = sem("al"); s_xn = sem("xn"); s_xs = sem("xs")
    s_wl = sem("wl")
    s_km = sem("km"); s_ke = sem("ke"); s_vm = sem("vm"); s_ve = sem("ve")
    s_qm = sem("qm"); s_qe = sem("qe"); s_kl = sem("kl"); s_kq = sem("kq")
    s_cc1 = sem("cc1")
    s_sc = sem("sc"); s_pe = sem("pe"); s_pr = sem("pr"); s_am = sem("am")
    s_od = sem("od")
    s_cc3 = sem("cc3"); s_ab2 = sem("ab2"); s_ab2r = sem("ab2r")
    s_pm2 = sem("pm2"); s_pe2 = sem("pe2"); s_fin = sem("fin")

    sched = [(b2, Q, kc) for b2 in range(4) for Q in range(16)
             for kc in range(4 * Q + 4)]
    gidx_of = {t: i for i, t in enumerate(sched)}
    pr_of = {}
    npr = 0
    for t in sched:
        b2, Q, kc = t
        if kc >= 4 * Q:
            npr += 1
        pr_of[t] = npr

    xes = contextlib.ExitStack()
    sstat = sb("stat", [128, 8 * 128], F32, xes)
    sjunk = sb("junk", [128, E], F32, xes)
    sx = sb("sx", [128, 2 * E], F32, xes)
    sxn = sb("sxn", [128, 2 * E], BF16, xes)
    dx = DS("dx", 2)
    dxn = DS("dxn", 2)

    @block.sync
    def _(sync):
        for i in range(NT):
            if i >= 2:
                sync.wait_ge(s_sq, 2 * (i - 1))
            dx.issue(sync, i, [(sx[:, (i % 2) * E:(i % 2) * E + E],
                                d["xcur"][i * 128:(i + 1) * 128, :], {})])
        for i in range(NT):
            if i >= 2:
                sync.wait_ge(s_xn, i - 1)
            dx.issue(sync, NT + i, [(sx[:, (i % 2) * E:(i % 2) * E + E],
                                     d["xcur"][i * 128:(i + 1) * 128, :], {})])

    @block.gpsimd
    def _(gpsimd):
        for dc in range(8):
            gpsimd.dma_start(out=swk[:, dc * 128:(dc + 1) * 128],
                             in_=d["wk"][dc * 128:(dc + 1) * 128, :]
                             ).then_inc(s_wl, 16)
            gpsimd.dma_start(out=swv[:, dc * 128:(dc + 1) * 128],
                             in_=d["wv"][dc * 128:(dc + 1) * 128, :]
                             ).then_inc(s_wl, 16)
        for h in range(8):
            for dc in range(8):
                gpsimd.dma_start(
                    out=swq[:, (h * 8 + dc) * 128:(h * 8 + dc) * 128 + 128],
                    in_=d["wqf"][h * E + dc * 128:h * E + dc * 128 + 128, :]
                    ).then_inc(s_wl, 16)
        for dc in range(8):
            gpsimd.dma_start(out=swp[:, dc * E:(dc + 1) * E],
                             in_=d["wp"][dc * 128:(dc + 1) * 128, :]
                             ).then_inc(s_wl, 16)
        gpsimd.dma_start(out=sbp[:],
                         in_=bass.AP(d["bp_row"], 0, [[0, 128], [1, E]])
                         ).then_inc(s_wl, 16)
        for i in range(NT):
            gpsimd.wait_ge(s_xn, i + 1)
            dxn.issue(gpsimd, i, [(d["ablnb"][i * 128:(i + 1) * 128, :],
                                   sxn[:, (i % 2) * E:(i % 2) * E + E], {})])
        p.drains(gpsimd, [dxn])
        gpsimd.nop().then_inc(s_xs, 1)

    @block.scalar
    def _(scalar):
        for i in range(NT):
            dx.wait(scalar, i)
            xt = sx[:, (i % 2) * E:(i % 2) * E + E]
            scalar.activation(sjunk[:], xt, AF.Copy,
                              accum_out=sstat[:, i:i + 1]).then_inc(s_sq, 1)
            scalar.activation(sjunk[:], xt, AF.Square,
                              accum_out=sstat[:, 128 + i:128 + i + 1]).then_inc(s_sq, 1)
        scalar.wait_ge(s_al, 1)
        scalar.activation(sstat[:, 5 * 128:6 * 128], sstat[:, 4 * 128:5 * 128],
                          AF.Sqrt).then_inc(s_al, 1)

    @block.vector
    def _(vector):
        vector.memset(sstat[:], 0.0).then_inc(sem("init"), 1)
        R = lambda j: sstat[:, j * 128:(j + 1) * 128]
        stv = selfsync(p, vector, "ab_ssx")
        vector.wait_ge(s_sq, 2 * NT)
        stv(vector.tensor_scalar_mul(R(2), R(0), 1.0 / E))
        stv(vector.tensor_scalar_mul(R(3), R(1), 1.0 / E))
        stv(vector.tensor_tensor(R(4), R(2), R(2), ALU.mult))
        stv(vector.tensor_tensor(R(3), R(3), R(4), ALU.subtract))
        stv(vector.tensor_scalar_add(R(4), R(3), 1e-5))
        vector.nop().then_inc(s_al, 1)
        vector.wait_ge(s_al, 2)
        stv(vector.reciprocal(R(6), R(5)))
        stv(vector.scalar_tensor_tensor(R(7), R(2), -1.0, R(6), ALU.mult,
                                        ALU.mult))
        vector.nop().then_inc(s_al, 1)
        for i in range(NT):
            vector.wait_ge(s_al, 3)
            dx.wait(vector, NT + i)
            if i >= 2:
                dxn.wait(vector, i - 2)
            vector.tensor_scalar(sxn[:, (i % 2) * E:(i % 2) * E + E],
                                 sx[:, (i % 2) * E:(i % 2) * E + E],
                                 sstat[:, 6 * 128 + i:6 * 128 + i + 1],
                                 sstat[:, 7 * 128 + i:7 * 128 + i + 1],
                                 ALU.mult, ALU.add).then_inc(s_xn, 1)
    xes.close()

    pes = contextlib.ExitStack()
    sxt = sb("sxt", [128, 2 * 8 * 512], BF16, pes)
    skq = sb("skq", [128, 2 * 512], BF16, pes)
    sVb = sb("sVb", [128, 4 * VW], BF16, pes)
    pk = [pes.enter_context(nc.psum_tensor(f"ab_pk{i}", [128, 512], F32))
          for i in range(2)]
    pv = [pes.enter_context(nc.psum_tensor(f"ab_pv{i}", [128, 512], F32))
          for i in range(2)]
    pq = [pes.enter_context(nc.psum_tensor(f"ab_pq{i}", [128, 512], F32))
          for i in range(2)]
    dxt = DS("dxt", 2, per=8)
    dkv = DS("dkv", 4)
    dvv = DS("dvv", 4)

    def sxt_t(tb, dc):
        return sxt[:, ((tb % 2) * 8 + dc) * 512:((tb % 2) * 8 + dc) * 512 + 512]

    @block.sync
    def _(sync):
        sync.wait_ge(s_xs, 1)
        for tb in range(8):
            if tb >= 2:
                sync.wait_ge(s_qm, 8 * (tb - 1))
            dxt.issue(sync, tb, [
                (sxt_t(tb, dc),
                 d["ablnb"][tb * 512:(tb + 1) * 512, dc * 128:(dc + 1) * 128],
                 {"transpose": True}) for dc in range(8)])

    @block.tensor
    def _(tensor):
        tensor.wait_ge(s_wl, 16 * (16 + 64 + 8 + 1))
        for tb in range(8):
            dxt.wait(tensor, tb)
            if tb >= 2:
                tensor.wait_ge(s_ke, tb - 1)
            mm = None
            for dc in range(8):
                mm = tensor.matmul(pk[tb % 2][:], swk[:, dc * 128:(dc + 1) * 128],
                                   sxt_t(tb, dc), start=(dc == 0), stop=(dc == 7))
            mm.then_inc(s_km, 1)
            for sub in range(4):
                vg = tb * 4 + sub
                if vg >= 2:
                    tensor.wait_ge(s_ve, vg - 1)
                mm = None
                for dc in range(8):
                    mm = tensor.matmul(
                        pv[vg % 2][:, 0:128],
                        sxt_t(tb, dc)[:, sub * 128:sub * 128 + 128],
                        swv[:, dc * 128:(dc + 1) * 128],
                        start=(dc == 0), stop=(dc == 7))
                mm.then_inc(s_vm, 1)
            for h in range(8):
                qg = tb * 8 + h
                if qg >= 2:
                    tensor.wait_ge(s_qe, qg - 1)
                mm = None
                for dc in range(8):
                    mm = tensor.matmul(
                        pq[qg % 2][:],
                        swq[:, (h * 8 + dc) * 128:(h * 8 + dc) * 128 + 128],
                        sxt_t(tb, dc), start=(dc == 0), stop=(dc == 7))
                mm.then_inc(s_qm, 1)

    @block.scalar
    def _(scalar):
        for tb in range(8):
            scalar.wait_ge(s_km, tb + 1)
            if tb >= 1:
                dkv.wait(scalar, 9 * (tb - 1))
            scalar.activation(skq[:, 0:512], pk[tb % 2][:],
                              AF.Copy).then_inc(s_ke, 1)
            for sub in range(4):
                vg = tb * 4 + sub
                scalar.wait_ge(s_vm, vg + 1)
                scalar.wait_ge(sem("vz"), 1)
                if vg >= 4:
                    dvv.wait(scalar, vg - 4)
                scalar.activation(sVb[:, (vg % 4) * VW:(vg % 4) * VW + 128],
                                  pv[vg % 2][:, 0:128], AF.Copy).then_inc(s_ve, 1)
            for h in range(8):
                qg = tb * 8 + h
                scalar.wait_ge(s_qm, qg + 1)
                prev = 9 * tb + h if h >= 1 else 9 * tb - 1
                if qg >= 1:
                    dkv.wait(scalar, prev)
                scalar.activation(skq[:, 512:1024], pq[qg % 2][:],
                                  AF.Copy).then_inc(s_qe, 1)

    @block.vector
    def _(vector):
        vector.wait_ge(s_xs, 1)
        for sl in range(4):
            vector.memset(sVb[:, sl * VW + 128:sl * VW + 129], 1.0)
            vector.memset(sVb[:, sl * VW + 129:sl * VW + VW], 0.0)
        vector.nop().then_inc(sem("vz"), 1)

    @block.gpsimd
    def _(gpsimd):
        for tb in range(8):
            gpsimd.wait_ge(s_ke, tb + 1)
            dkv.issue(gpsimd, 9 * tb,
                      [(d["cc_ab_in"][0:128, tb * 512:(tb + 1) * 512],
                        skq[:, 0:512], {})])
            for sub in range(4):
                vg = tb * 4 + sub
                gpsimd.wait_ge(s_ve, vg + 1)
                dvv.issue(gpsimd, vg,
                          [(d["ccvloc"][vg * 128:(vg + 1) * 128, :],
                            sVb[:, (vg % 4) * VW:(vg % 4) * VW + VW], {})])
            for h in range(8):
                qg = tb * 8 + h
                gpsimd.wait_ge(s_qe, qg + 1)
                dkv.issue(gpsimd, 9 * tb + h + 1,
                          [(d["cc_ab_in"][128 + h * 128:256 + h * 128,
                                          tb * 512:(tb + 1) * 512],
                            skq[:, 512:1024], {})])
        p.drains(gpsimd, [dkv, dvv])
        if not NOCC and (CCMASK & 1):
            gpsimd.collective_compute(
                "AllGather", ALU.bypass, replica_groups=RG,
                ins=[d["cc_ab_in"].ap().opt()], outs=[d["cc_ab_out"].ap().opt()],
            ).then_inc(s_cc1, 1)
        else:
            gpsimd.nop().then_inc(s_cc1, 1)
        gpsimd.wait_ge(s_cc1, 1)
        if not NOCC and (CCMASK & 2):
            gpsimd.collective_compute(
                "AllGather", ALU.bypass, replica_groups=RG,
                ins=[d["ccvloc"].ap().opt()], outs=[d["ccvgath"].ap().opt()],
            ).then_inc(s_cc1, 1)
        else:
            gpsimd.nop().then_inc(s_cc1, 1)
    pes.close()

    aes = contextlib.ExitStack()
    skt = sb("skt", [128, L], BF16, aes)
    sqt = sb("sqt", [128, L], BF16, aes)
    sv = sb("sv", [128, 6 * VW], BF16, aes)
    spb = sb("spb", [128, 2 * 512], BF16, aes)
    soA = sb("soA", [128, 4 * 128], BF16, aes)
    srcp = sb("srcp", [128, 8], F32, aes)
    psc = [aes.enter_context(nc.psum_tensor(f"ab_ps{i}", [128, 512], F32))
           for i in range(2)]
    pav = [aes.enter_context(nc.psum_tensor(f"ab_pa{i}", [128, VW], F32))
           for i in range(4)]
    dsv = DS("dsv", 6)
    dso2 = DS("dso2", 4, per=4)

    @block.sync
    def _(sync):
        sync.wait_ge(s_cc1, 2)
        for b2 in range(4):
            if b2 >= 1:
                sync.wait_ge(s_sc, gidx_of[(b2 - 1, 15, 63)] + 1)
            if NOCC or (ABN & 1):
                for pair in [(skt[:, 0:NLOC], d["cc_ab_in"][0:128, :]),
                             (skt[:, NLOC:L], d["cc_ab_in"][0:128, :]),
                             (sqt[:, 0:NLOC], d["cc_ab_in"][128:256, :]),
                             (sqt[:, NLOC:L], d["cc_ab_in"][128:256, :])]:
                    sync.dma_start(out=pair[0], in_=pair[1]).then_inc(s_kl, 16)
            else:
                for hf in range(2):
                    r = 2 * b2 + hf
                    sync.dma_start(
                        out=skt[:, hf * NLOC:(hf + 1) * NLOC],
                        in_=d["cc_ab_out"][r * KQR:r * KQR + 128, :]
                        ).then_inc(s_kl, 16)
            for Q in range(16):
                for kc in range(4 * Q + 4):
                    gi = gidx_of[(b2, Q, kc)]
                    if gi >= 6:
                        sync.wait_ge(s_am, gi - 5)
                    kb = kc % 32
                    kh = kc // 32
                    if NOCC or (ABN & 2):
                        src = d["ccvloc"][kb * 128:(kb + 1) * 128, :]
                    else:
                        src = d["ccvgath"][(2 * b2 + kh) * NLOC + kb * 128:
                                            (2 * b2 + kh) * NLOC + (kb + 1) * 128, :]
                    dsv.issue(sync, gi, [(sv[:, (gi % 6) * VW:(gi % 6) * VW + VW],
                                          src, {})])

    @block.gpsimd
    def _(gpsimd):
        gpsimd.wait_ge(s_cc1, 2)
        for b2 in range(4):
            if not (NOCC or (ABN & 1)):
                if b2 >= 1:
                    gpsimd.wait_ge(s_sc, gidx_of[(b2 - 1, 15, 63)] + 1)
                gpsimd.reg_load(p.gr, c["offs"][0:1, 8 + b2:9 + b2])
                gpsimd.dma_start(
                    out=sqt[:],
                    in_=bass.AP(d["cc_ab_out"], p.gr,
                                [[NLOC, 128], [KQR * NLOC, 2], [1, NLOC]])
                    ).then_inc(s_kq, 16)
            for Q in range(16):
                qg = b2 * 16 + Q
                gpsimd.wait_ge(s_od, 4 * (qg + 1))
                dso2.issue(gpsimd, qg, [
                    (d["cc_o_in"][b2 * L + Q * 512 + sub * 128:
                                  b2 * L + Q * 512 + (sub + 1) * 128, :],
                     soA[:, sub * 128:(sub + 1) * 128], {}) for sub in range(4)])
        p.drains(gpsimd, [dso2])
        if not NOCC and (CCMASK & 4):
            gpsimd.collective_compute(
                "AllGather", ALU.bypass, replica_groups=RG,
                ins=[d["cc_o_in"].ap().opt()], outs=[d["cc_o_out"].ap().opt()],
            ).then_inc(s_cc3, 1)
        else:
            gpsimd.nop().then_inc(s_cc3, 1)
        gpsimd.wait_ge(s_cc3, 1)
        if NOCC or (ABN & 4):
            for h in range(8):
                gpsimd.dma_start(
                    out=bass.AP(d["abo"], h * DB, [[E, NLOC], [1, DB]]),
                    in_=d["cc_o_in"][0:NLOC, :]).then_inc(s_ab2, 16)
            gpsimd.wait_ge(s_ab2, 16 * 8)
        else:
            for h in range(8):
                gpsimd.reg_load(p.gr, c["offs"][0:1, 16 + h:17 + h])
                gpsimd.dma_start(
                    out=bass.AP(d["abo"], h * DB, [[E, NLOC], [1, DB]]),
                    in_=bass.AP(d["cc_o_out"], p.gr, [[DB, NLOC], [1, DB]])
                    ).then_inc(s_ab2, 16)
            gpsimd.wait_ge(s_ab2, 128)
        gpsimd.nop().then_inc(s_ab2r, 1)

    @block.tensor
    def _(tensor):
        for b2 in range(4):
            tensor.wait_ge(s_kl, (64 if (NOCC or (ABN & 1)) else 32) * (b2 + 1))
            if not (NOCC or (ABN & 1)):
                tensor.wait_ge(s_kq, 16 * (b2 + 1))
            for Q in range(16):
                qg = b2 * 16 + Q
                nk = 4 * Q + 4
                if qg >= 1:
                    tensor.wait_ge(s_od, 4 * qg)
                for step in range(nk + 1):
                    if step < nk:
                        kc = step
                        gi = gidx_of[(b2, Q, kc)]
                        if gi >= 2:
                            tensor.wait_ge(s_pe, gi - 1)
                        tensor.matmul(psc[kc % 2][:],
                                      skt[:, kc * 128:(kc + 1) * 128],
                                      sqt[:, Q * 512:(Q + 1) * 512],
                                      start=True, stop=True).then_inc(s_sc, 1)
                    if step >= 1:
                        kc = step - 1
                        gi = gidx_of[(b2, Q, kc)]
                        dsv.wait(tensor, gi)
                        if kc >= 4 * Q:
                            tensor.wait_ge(s_pr, pr_of[(b2, Q, kc)])
                        else:
                            tensor.wait_ge(s_pe, gi + 1)
                        mm = None
                        for sub in range(4):
                            mm = tensor.matmul(
                                pav[sub][:],
                                spb[:, (kc % 2) * 512 + sub * 128:
                                    (kc % 2) * 512 + (sub + 1) * 128],
                                sv[:, (gi % 6) * VW:(gi % 6) * VW + VW],
                                start=(kc == 0), stop=(kc == nk - 1))
                        mm.then_inc(s_am, 1)

    @block.scalar
    def _(scalar):
        for b2 in range(4):
            for Q in range(16):
                for kc in range(4 * Q + 4):
                    gi = gidx_of[(b2, Q, kc)]
                    scalar.wait_ge(s_sc, gi + 1)
                    if gi >= 2:
                        scalar.wait_ge(s_am, gi - 1)
                    scalar.activation(spb[:, (kc % 2) * 512:(kc % 2) * 512 + 512],
                                      psc[kc % 2][:], AF.Exp,
                                      scale=1.0 / 32.0).then_inc(s_pe, 1)

    @block.vector
    def _(vector):
        for b2 in range(4):
            for Q in range(16):
                qg = b2 * 16 + Q
                nk = 4 * Q + 4
                for kc in range(4 * Q, nk):
                    o = kc - 4 * Q
                    gi = gidx_of[(b2, Q, kc)]
                    vector.wait_ge(s_pe, gi + 1)
                    vector.tensor_tensor(
                        spb[:, (kc % 2) * 512:(kc % 2) * 512 + 512],
                        spb[:, (kc % 2) * 512:(kc % 2) * 512 + 512],
                        c["diagmask"][:, o * 512:(o + 1) * 512],
                        ALU.mult).then_inc(s_pr, 1)
                vector.wait_ge(s_am, gidx_of[(b2, Q, nk - 1)] + 1)
                if qg >= 1:
                    dso2.wait(vector, qg - 1)
                stq = selfsync(p, vector, "ab_ssq")
                for sub in range(4):
                    stq(vector.reciprocal(srcp[:, sub:sub + 1],
                                          pav[sub][:, 128:129]))
                    vector.tensor_scalar_mul(soA[:, sub * 128:(sub + 1) * 128],
                                             pav[sub][:, 0:128],
                                             srcp[:, sub:sub + 1]).then_inc(s_od, 1)
    aes.close()

    oes = contextlib.ExitStack()
    sot = sb("sot", [128, 2 * 8 * 512], BF16, oes)
    sxc = sb("sxc", [128, 2 * E], F32, oes)
    sfo = sb("sfo", [128, 2 * E], F32, oes)
    pop = [oes.enter_context(nc.psum_tensor(f"ab_po{i}", [128, 512], F32))
           for i in range(4)]
    dot = DS("dot", 2, per=8)
    dxc3 = DS("dxc3", 2)
    dxo2 = DS("dxo2", 2)

    def sot_t(tb, dc):
        return sot[:, ((tb % 2) * 8 + dc) * 512:((tb % 2) * 8 + dc) * 512 + 512]

    @block.sync
    def _(sync):
        sync.wait_ge(s_ab2r, 1)
        for i in range(NT):
            if i % 4 == 0:
                tb = i // 4
                if tb >= 2:
                    sync.wait_ge(s_pm2, 8 * (tb - 1))
                dot.issue(sync, tb, [
                    (sot_t(tb, dc),
                     d["abo"][tb * 512:(tb + 1) * 512, dc * 128:(dc + 1) * 128],
                     {"transpose": True}) for dc in range(8)])
            if i >= 2:
                sync.wait_ge(s_fin, i - 1)
            dxc3.issue(sync, i, [(sxc[:, (i % 2) * E:(i % 2) * E + E],
                                  d["xcur"][i * 128:(i + 1) * 128, :], {})])

    @block.tensor
    def _(tensor):
        for tb in range(8):
            dot.wait(tensor, tb)
            for sub in range(4):
                i = tb * 4 + sub
                for half in range(2):
                    og = i * 2 + half
                    if og >= 4:
                        tensor.wait_ge(s_pe2, og - 3)
                    mm = None
                    for dc in range(8):
                        mm = tensor.matmul(
                            pop[og % 4][:],
                            sot_t(tb, dc)[:, sub * 128:sub * 128 + 128],
                            swp[:, dc * E + half * 512:dc * E + half * 512 + 512],
                            start=(dc == 0), stop=(dc == 7))
                    mm.then_inc(s_pm2, 1)

    @block.vector
    def _(vector):
        for i in range(NT):
            dxc3.wait(vector, i)
            if i >= 2:
                dxo2.wait(vector, i - 2)
            sto = selfsync(p, vector, "ab_sso")
            for half in range(2):
                og = i * 2 + half
                vector.wait_ge(s_pm2, og + 1)
                t = sfo[:, (i % 2) * E + half * 512:(i % 2) * E + half * 512 + 512]
                sto(vector.tensor_tensor(t, pop[og % 4][:],
                                         sbp[:, half * 512:half * 512 + 512],
                                         ALU.add))
                sto(vector.tensor_tensor(
                    t, t, sxc[:, (i % 2) * E + half * 512:
                              (i % 2) * E + half * 512 + 512],
                    ALU.add))
                vector.nop().then_inc(s_pe2, 1)
            vector.nop().then_inc(s_fin, 1)

    @block.gpsimd
    def _(gpsimd):
        for i in range(NT):
            gpsimd.wait_ge(s_fin, i + 1)
            dxo2.issue(gpsimd, i, [(d["xcur"][i * 128:(i + 1) * 128, :],
                                    sfo[:, (i % 2) * E:(i % 2) * E + E], {})])
        p.drains(gpsimd, [dxo2])
        gpsimd.nop().then_inc(s_fin, 100)

    for name in ("sync", "vector", "scalar", "tensor"):
        def fin2(eng):
            eng.wait_ge(s_fin, NT + 100)
        getattr(block, name)(fin2)
    oes.close()

    if DEBUG:
        sdb = p.sem("ab_dbg")
        n = [0]

        @block.sync
        def _(sync):
            def cp(dst, src):
                sync.dma_start(out=dst, in_=src).then_inc(sdb, 16)
                n[0] += 1
            if "xlnb" in p.dbg:
                cp(p.dbg["xlnb"][:], d["ablnb"][:])
            if "ob" in p.dbg:
                for ii in range(8):
                    cp(p.dbg["ob"][ii * NLOC:(ii + 1) * NLOC, :],
                       d["cc_o_in"][ii * NLOC:(ii + 1) * NLOC, :])
            if "attnb" in p.dbg:
                cp(p.dbg["attnb"][:], d["xcur"][:])
            sync.wait_ge(sdb, 16 * n[0])
    fes.close()



def ph_finish(p, block):
    nc = p.nc
    d = p.dram
    s_out = p.sem("s_out")
    srct = d["xcur"] if (N_PHASES >= 3 or MK_ONLY) else d["ybuf"]

    @block.sync
    def _(sync):
        for i in range(4):
            sync.dma_start(out=d["x_out"][i * 1024:(i + 1) * 1024, :],
                           in_=srct[i * 1024:(i + 1) * 1024, :]).then_inc(s_out, 16)
        sync.wait_ge(s_out, 16 * 4)



def _shard_inputs(inputs):
    x = np.asarray(inputs["L0_attn_a_x"])
    g1 = np.asarray(inputs["L0_attn_a_g1"])
    g2 = np.asarray(inputs["L0_attn_a_g2"])
    gn = np.asarray(inputs["L0_ffn_norm_gamma"])
    gs = np.asarray(inputs["L0_ffn_slot_norm_gamma"])
    ghid = np.asarray(inputs["L0_attn_a_g_hidden"])

    def cols(v, nch):
        return np.ascontiguousarray(v.reshape(nch, 128).T.astype(np.float32))

    pos_all = np.arange(L, dtype=np.float64)
    inv_freq = 1.0 / (10000.0 ** (np.arange(0, DH, 2, dtype=np.float64) / DH))
    emb = np.concatenate([pos_all[:, None] * inv_freq[None, :]] * 2, axis=1)
    cos_t = np.cos(emb).astype(np.float32)
    sin_t = np.sin(emb).astype(np.float32)
    sin_sign = sin_t.copy()
    sin_sign[:, 0::2] *= -1.0
    cos_rep_all = np.tile(cos_t, (1, H)).astype(np.float32)
    sin_rep_all = np.tile(sin_sign, (1, H)).astype(np.float32)

    tril = np.triu(np.ones((128, 128), np.float32)).astype(bf16)
    dm = np.zeros((4, 128, 512), np.float32)
    kk = np.arange(128)[:, None]
    qq = np.arange(512)[None, :]
    for o in range(4):
        dm[o] = (o * 128 + kk <= qq)
    diagmask = dm.reshape(512, 512).astype(bf16)
    ones_b = np.ones((128, 128), np.float32).astype(bf16)
    onesf = np.ones((128, 8), np.float32)
    iota_row = np.arange(F, dtype=np.float32).reshape(1, F)
    ident = np.eye(128, dtype=np.float32)

    base = {
        "w_qkv": inputs["L0_attn_a_w_qkv"], "w_out": inputs["L0_attn_a_w_out"],
        "g1c": cols(g1, 8), "g2c": cols(g2, 8),
        "w_dp1": inputs["L0_attn_a_w_dp1"], "w_dp2": inputs["L0_attn_a_w_dp2"],
        "w_gate": inputs["L0_attn_a_w_gate"], "w_up": inputs["L0_attn_a_w_up"],
        "w_down": inputs["L0_attn_a_w_down"], "ghidc": cols(ghid, 36),
        "wk": inputs["L0_attn_b_Wk"], "wv": inputs["L0_attn_b_Wv"],
        "wp": inputs["L0_attn_b_Wp"],
        "bp_row": np.asarray(inputs["L0_attn_b_bp"]).reshape(1, E),
        "gn_row": gn.reshape(1, E).astype(np.float32),
        "gs_row": gs.reshape(1, E).astype(np.float32),
        "tril": tril, "diagmask": diagmask, "ones": ones_b, "onesf": onesf,
        "iota_row": iota_row, "identb": ident.astype(bf16), "identf": ident,
        "onesrowf": np.ones((1, 128), np.float32),
        "g2_row": g2.reshape(1, E).astype(np.float32),
        "ghid_row": ghid.reshape(1, F).astype(bf16),
    }
    base["wqf"] = np.asarray(inputs["L0_attn_b_Wq"]).reshape(8 * E, DB)
    base = {k: np.ascontiguousarray(np.asarray(v)) for k, v in base.items()}

    slot_emb = np.asarray(inputs["L0_ffn_slot_embeds"])
    w1 = np.asarray(inputs["L0_ffn_w1"])
    b1 = np.asarray(inputs["L0_ffn_b1"])
    w2 = np.asarray(inputs["L0_ffn_w2"])
    b2 = np.asarray(inputs["L0_ffn_b2"])

    in_maps = []
    for cid in range(8):
        bb, hh = cid >> 1, cid & 1
        sl = slice(hh * NLOC, (hh + 1) * NLOC)
        wstate = np.zeros((128, 8), np.float32)
        if hh == 1:
            wstate[:, cid - 1] = 1.0
        offs = np.zeros((1, 32), np.int32)
        offs[0, 0] = bb * (2 * NLOC * E)
        offs[0, 2] = 2 * bb * 64 * 128
        offs[0, 3] = (bb * L + hh * NLOC) * DB
        offs[0, 4] = hh * NLOC * 1026
        offs[0, 5] = (cid ^ 1) * (NLOC * 1026)
        offs[0, 6] = (1 - hh) * NLOC * 1026
        for qb in range(4):
            offs[0, 8 + qb] = (2 * qb * 9 * 128 + 128 + cid * 128) * NLOC
        for hq in range(8):
            offs[0, 16 + hq] = (hq * 4 * L + bb * L + hh * NLOC) * DB
        m = dict(base)
        m["x_in"] = np.ascontiguousarray(x[bb, sl])
        m["cos_rep"] = np.ascontiguousarray(cos_rep_all[sl])
        m["sin_rep"] = np.ascontiguousarray(sin_rep_all[sl])
        m["wstate"] = wstate
        m["offs"] = offs
        ex = slice(4 * hh, 4 * hh + 4)
        m["slot_emb"] = np.ascontiguousarray(slot_emb[ex].reshape(4 * S, E))
        m["w1f"] = np.ascontiguousarray(w1[ex].reshape(4 * E, HID))
        m["b1c"] = np.ascontiguousarray(
            b1[ex].reshape(4, 32, 128).transpose(2, 0, 1).reshape(128, 128))
        m["w2f"] = np.ascontiguousarray(w2[ex].reshape(4 * HID, E))
        m["b2"] = np.ascontiguousarray(b2[ex])
        in_maps.append(m)
    return in_maps


_CACHE = {}


def kernel(**inputs):
    if "prog" not in _CACHE:
        _CACHE["prog"] = build_program()
    p = _CACHE["prog"]
    in_maps = _shard_inputs(inputs)
    res = run_bass_kernel_spmd(p.nc, in_maps, core_ids=list(range(8)))
    out = np.empty((B, L, E), np.float32)
    for cid in range(8):
        bb, hh = cid >> 1, cid & 1
        out[bb, hh * NLOC:(hh + 1) * NLOC] = res.results[cid]["x_out"]
    if DEBUG:
        return out, [res.results[cid] for cid in range(8)]
    return out
